# version 09
# baseline (speedup 1.0000x reference)
import numpy as np
import ml_dtypes
from contextlib import ExitStack

import concourse.bass as bass
import concourse.mybir as mybir
from concourse.bass_utils import run_bass_kernel_spmd

F32 = mybir.dt.float32
BF16 = mybir.dt.bfloat16
AF = mybir.ActivationFunctionType
ALU = mybir.AluOpType
BF = ml_dtypes.bfloat16

B, S, D = 4, 2048, 4096
H, PD, G, ND = 128, 64, 8, 128
KCONV, CHUNK = 4, 128
DIN = H * PD
CONVD = DIN + 2 * G * ND
PROJ = DIN + CONVD + H
EPS = 1e-5
TOK = 1024
NCH = TOK // CHUNK
KD = D // 128
NPAIR = DIN // 128
NHBC = CONVD // 128
NGATE = DIN // 128
HBC0 = DIN
DT0 = DIN + CONVD
RG = [[0, 1], [2, 3], [4, 5], [6, 7]]

_CACHE = {}
CHAIN = "deps"


def _box(o):
    th = o.tensor
    name = th.name
    try:
        shp = list(th.shape)
        rowlen = 1
        for s_ in shp[1:]:
            rowlen *= int(s_)
        pat = [(int(p[0]), int(p[1])) for p in o.ap]
        if len(pat) < 2:
            raise ValueError
        off = int(o.offset)
        p0, f0 = divmod(off, rowlen)
        s0, n0 = pat[0]
        if n0 > 1 and s0 != 0:
            if s0 % rowlen:
                raise ValueError
            pext = 1 + (s0 // rowlen) * (n0 - 1)
        else:
            pext = 1
        fext = 1
        for s_, n_ in pat[1:]:
            if s_ < 0:
                raise ValueError
            fext += s_ * (n_ - 1)
        if f0 + fext > rowlen:
            raise ValueError
        return (name, p0, p0 + pext, f0, f0 + fext)
    except Exception:
        return (name, 0, 1 << 40, 0, 1 << 40)


def _op_boxes(args, kw):
    outs, ins = [], []
    first = True
    for o in args:
        if hasattr(o, "ap") and hasattr(o, "tensor"):
            (outs if first else ins).append(_box(o))
            first = False
    for knm, o in kw.items():
        if hasattr(o, "ap") and hasattr(o, "tensor"):
            b = _box(o)
            ins.append(b)
            if knm == "accum_out":
                outs.append(b)
    return outs, ins


def _inter(a, b):
    return (a[0] == b[0] and a[1] < b[2] and b[1] < a[2]
            and a[3] < b[4] and b[3] < a[4])


def _clash(bo, bi, ro, ri):
    for x in bo:
        for y in ro:
            if _inter(x, y):
                return True
        for y in ri:
            if _inter(x, y):
                return True
    for x in bi:
        for y in ro:
            if _inter(x, y):
                return True
    return False


class _IP:

    def __init__(self, trk):
        self._t = trk

    def then_inc(self, sem, k=1):
        t = self._t
        t._wait_floor(t._chain[1])
        return t._e.nop().then_inc(sem, k)


class Trk:
    _SKIP = {"wait_ge", "nop", "engine_nop", "then_inc", "dma_start",
             "collective_compute", "register", "register64", "reg_load",
             "reg_save", "reg_alu", "reg_add", "reg_mov", "partition_id",
             "tpb_base_ld"}

    def __init__(self, eng, chain=None):
        object.__setattr__(self, "_e", eng)
        object.__setattr__(self, "last", None)
        object.__setattr__(self, "_chain", chain)
        object.__setattr__(self, "_mode", CHAIN if chain is not None else None)
        object.__setattr__(self, "_floor", 0)
        object.__setattr__(self, "_recs", [])

    def _wait_floor(self, v):
        if v > self._floor:
            self._e.wait_ge(self._chain[0], v)
            object.__setattr__(self, "_floor", v)
            rc = self._recs
            if rc:
                self._recs[:] = [r for r in rc if r[0] > v]

    def __getattr__(self, name):
        a = getattr(self._e, name)
        if callable(a) and name not in Trk._SKIP:
            def w(*args, **kw):
                ch = self._chain
                if ch is None:
                    r = a(*args, **kw)
                    if hasattr(r, "then_inc"):
                        object.__setattr__(self, "last", r)
                    return r
                if self._mode == "deps":
                    bo, bi = _op_boxes(args, kw)
                    need = self._floor
                    for seq, ro, ri in self._recs:
                        if seq > need and _clash(bo, bi, ro, ri):
                            need = seq
                    self._wait_floor(need)
                else:
                    self._wait_floor(ch[1])
                r = a(*args, **kw)
                if not hasattr(r, "then_inc"):
                    return r
                object.__setattr__(self, "last", r)
                r.then_inc(ch[0], 1)
                ch[1] += 1
                if self._mode == "deps":
                    self._recs.append((ch[1], bo, bi))
                return _IP(self)
            return w
        return a

    def wait_ge(self, sem, val):
        return self._e.wait_ge(sem, val)

    def nop(self):
        return self._e.nop()

    def mi(self, sem, k=1):
        ch = self._chain
        if ch is not None:
            self._wait_floor(ch[1])
            self._e.nop().then_inc(sem, k)
        else:
            self.last.then_inc(sem, k)



def build_program(stage=99, debug=()):
    nc = bass.Bass("TRN2", target_bir_lowering=False)

    I = lambda n, s_, d_: nc.dram_tensor(n, s_, d_, kind="ExternalInput").ap()
    x_fm = I("x_fm", [D, TOK], F32)
    w_in_t = I("w_in_t", [D, PROJ], F32)
    w_out_t = I("w_out_t", [DIN, D], F32)
    w_ffn_t = I("w_ffn_t", [D, D], F32)
    gw_t = I("gw_t", [D, 8], F32)
    ics_i = I("ics_i", [8, D], F32)
    ocs_i = I("ocs_i", [8, D], F32)
    norm_w_i = I("norm_w_i", [128, KD], F32)
    conv_w_i = I("conv_w_i", [128, NHBC * KCONV], F32)
    conv_b_i = I("conv_b_i", [128, NHBC], F32)
    dtb_i = I("dtb_i", [128, 1], F32)
    alog_i = I("alog_i", [128, 1], F32)
    gnw_i = I("gnw_i", [128, NPAIR], F32)
    dp_i = I("dp_i", [128, NPAIR], F32)
    ffnb_i = I("ffnb_i", [128, KD], F32)
    e64_i = I("e64_i", [128, DIN], BF16)
    mask01_i = I("mask01_i", [128, 128], BF16)
    ones1_i = I("ones1_i", [1, 128], BF16)
    onescol_i = I("onescol_i", [128, 1], BF16)
    onessq_i = I("onessq_i", [128, 128], BF16)
    iden8_i = I("iden8_i", [8, 8], F32)
    iden128_i = I("iden128_i", [128, 128], F32)
    masks_i = I("masks_i", [128, 1], F32)
    onehot16_i = I("onehot16_i", [16, 2048], BF16)
    idbf_i = I("idbf_i", [128, 128], BF16)
    maskb_i = I("maskb_i", [128, 1], F32)

    out_d = nc.dram_tensor("out_d", [D, TOK], F32, kind="ExternalOutput").ap()
    DR = lambda n, s_, d_: nc.dram_tensor(n, s_, d_).ap()
    hid_d = DR("hid_d", [DIN, TOK], BF16)
    gate_d = DR("gate_d", [DIN, TOK], BF16)
    hd_d = DR("hd_d", [TOK, DIN], BF16)
    prev_d = DR("prev_d", [NCH * 128, DIN], BF16)
    ahi_d = DR("ahi_d", [128, TOK], BF16)
    alo_d = DR("alo_d", [128, TOK], BF16)
    ynorm_d = DR("ynorm_d", [DIN, TOK], BF16)
    out2_d = DR("out2_d", [D, TOK], BF16)
    cc1_in = DR("cc1_in", [128, NHBC * 3], F32)
    cc1_out = DR("cc1_out", [128, NHBC * 3], F32)
    cc2_in = DR("cc2_in", [128, DIN], F32)
    cc2_out = DR("cc2_out", [128, DIN], F32)

    dbg = {}
    for name, shape, dt_ in debug:
        dbg[name] = nc.dram_tensor(name, shape, dt_, kind="ExternalOutput").ap()
    dbgctr = [0]

    def dump(pairs, sdbg):
        todo = [(nm, src) for nm, src in pairs if nm in dbg]
        if not todo:
            return
        with nc.Block("dbg%d" % dbgctr[0]) as blk:
            @blk.sync
            def _(sy):
                for nm, src in todo:
                    sy.dma_start(out=dbg[nm], in_=src).then_inc(sdbg, 16)
                    dbgctr[0] += 16
                sy.wait_ge(sdbg, dbgctr[0])

    st = ExitStack()
    with st:
        SB = lambda n, s_, d_: st.enter_context(nc.sbuf_tensor(n, s_, d_))
        SEM = lambda n: st.enter_context(nc.semaphore(n))

        p_ones1 = SB("p_ones1", [1, 128], BF16)
        p_onescol = SB("p_onescol", [128, 1], BF16)
        p_onessq = SB("p_onessq", [128, 128], BF16)
        p_mask01 = SB("p_mask01", [128, 128], BF16)
        p_masks = SB("p_masks", [128, 1], F32)
        p_maskb = SB("p_maskb", [128, 1], F32)
        p_iden8 = SB("p_iden8", [8, 8], F32)
        p_iden128 = SB("p_iden128", [128, 128], F32)
        p_ffnb = SB("p_ffnb", [128, KD], F32)
        p_eps = SB("p_eps", [128, 1], F32)
        CHV = [SEM("chain_v"), 0]
        CHS = [SEM("chain_s"), 0]
        s_cc1 = SEM("s_cc1")
        s_cc2 = SEM("s_cc2")
        s_fin = SEM("s_fin")
        s_dbg = SEM("s_dbg")

        env = dict(locals())

        with ExitStack() as ms:
            MSB = lambda n, s_, d_: ms.enter_context(nc.sbuf_tensor(n, s_, d_))
            p_norm_w = MSB("p_norm_w", [128, KD], F32)
            p_conv_w = MSB("p_conv_w", [128, NHBC * KCONV], F32)
            p_conv_b = MSB("p_conv_b", [128, NHBC], F32)
            p_dtbias = MSB("p_dtbias", [128, 1], F32)
            p_alog = MSB("p_alog", [128, 1], F32)
            p_negea = MSB("p_negea", [128, 1], F32)
            p_gnw = MSB("p_gnw", [128, NPAIR], F32)
            p_dp = MSB("p_dp", [128, NPAIR], F32)
            p_e64 = MSB("p_e64", [128, DIN], BF16)
            p_dtbf = MSB("p_dtbf", [128, TOK], BF16)
            p_dwbf = MSB("p_dwbf", [128, TOK], BF16)
            p_ubf = MSB("p_ubf", [128, TOK], BF16)
            p_nahi = MSB("p_nahi", [128, TOK], BF16)
            p_nalo = MSB("p_nalo", [128, TOK], BF16)
            p_dcol = MSB("p_dcol", [128, NCH], F32)
            p_cdcol = MSB("p_cdcol", [128, NCH], F32)
            p_bc = MSB("p_bc", [128, 16 * TOK], BF16)
            p_halo = MSB("p_halo", [128, NHBC * 3], F32)
            p_pre3 = MSB("p_pre3", [128, NHBC * 3], F32)
            p_patch = MSB("p_patch", [128, 192], BF16)
            p_onehot = MSB("p_onehot", [16, 2048], BF16)
            p_idbf = MSB("p_idbf", [128, 128], BF16)
            env.update(locals())

            _p0_loads(nc, env)
            with ExitStack() as hs:
                p_h = hs.enter_context(nc.sbuf_tensor("p_h", [128, KD * TOK], BF16))
                env["p_h"] = p_h
                _p1_rms(nc, env)
                dump([("dbg_h", p_h[:, 0:4 * TOK])], s_dbg)
                if stage <= 1:
                    _finish_dummy(nc, out_d, s_fin)
                    return nc
                _p2_dt(nc, env)
                dump([("dbg_ubf", p_ubf[:, :]), ("dbg_dw", p_dwbf[:, :])], s_dbg)
                if stage <= 2:
                    _finish_dummy(nc, out_d, s_fin)
                    return nc
                _p3_inproj(nc, env, hbc=True)
                dump([("dbg_bc", p_bc[:, 0:2 * TOK]), ("dbg_halo", p_halo[:, :])],
                     s_dbg)
                if stage <= 3:
                    _finish_dummy(nc, out_d, s_fin)
                    return nc
                _p3_inproj(nc, env, hbc=False)
            env.pop("p_h")
            if stage == 35:
                _finish_dummy(nc, out_d, s_fin)
                return nc

            _p4_sweep1(nc, env)
            dump([("dbg_patch", p_patch[:, :]), ("dbg_cd", p_cdcol[:, :]),
                  ("dbg_prev7", prev_d[7 * 128:8 * 128, :]),
                  ("dbg_hd0", hd_d[0:128, :]),
                  ("dbg_ccst", cc2_in)], s_dbg)
            if stage <= 4:
                _finish_dummy(nc, out_d, s_fin)
                return nc

            _p6_sweep2(nc, env)
            dump([("dbg_yn", ynorm_d.rearrange("(c p) t -> p c t", p=128)
                   [:, :, 7 * 128:8 * 128])], s_dbg)
            if stage <= 6:
                _finish_dummy(nc, out_d, s_fin)
                return nc

        p_x1 = SB("p_x1", [128, KD * TOK], BF16)
        env["p_x1"] = p_x1
        _p7_outproj(nc, env)
        dump([("dbg_x1", p_x1[:, 0:4 * TOK])], s_dbg)
        if stage <= 7:
            _finish_dummy(nc, out_d, s_fin)
            return nc

        _p8_ffn(nc, env)
    return nc


def _p0_loads(nc, env):
    g_ = env
    with nc.Block("p0") as blk:
        s0 = g_["SEM"]("p0_s")
        @blk.sync
        def _(sy):
            loads = [
                (g_["p_norm_w"], g_["norm_w_i"]), (g_["p_conv_w"], g_["conv_w_i"]),
                (g_["p_conv_b"], g_["conv_b_i"]), (g_["p_dtbias"], g_["dtb_i"]),
                (g_["p_alog"], g_["alog_i"]), (g_["p_masks"], g_["masks_i"]),
                (g_["p_maskb"], g_["maskb_i"]), (g_["p_mask01"], g_["mask01_i"]),
                (g_["p_ones1"], g_["ones1_i"]), (g_["p_onescol"], g_["onescol_i"]),
                (g_["p_onessq"], g_["onessq_i"]), (g_["p_iden8"], g_["iden8_i"]),
                (g_["p_iden128"], g_["iden128_i"]), (g_["p_gnw"], g_["gnw_i"]),
                (g_["p_dp"], g_["dp_i"]), (g_["p_ffnb"], g_["ffnb_i"]),
                (g_["p_onehot"], g_["onehot16_i"]),
                (g_["p_idbf"], g_["idbf_i"]),
            ]
            for dst, src in loads:
                sy.dma_start(out=dst[:, :], in_=src).then_inc(s0, 16)
            sy.wait_ge(s0, 16 * len(loads))

        @blk.gpsimd
        def _(g):
            s0g = g_["SEM"]("p0_sg")
            g.dma_start(out=g_["p_e64"][:, :], in_=g_["e64_i"]).then_inc(s0g, 16)
            g.wait_ge(s0g, 16)
    with nc.Block("p0b") as blk:
        @blk.scalar
        def _(sc):
            sc = Trk(sc, g_["CHS"] if CHAIN else None)
            sc.activation(g_["p_negea"][:, :], g_["p_alog"][:, :], AF.Exp)
    with nc.Block("p0c") as blk:
        @blk.vector
        def _(v):
            v = Trk(v, g_["CHV"] if CHAIN else None)
            v.tensor_scalar(g_["p_negea"][:, :], g_["p_negea"][:, :], -1.0,
                            None, ALU.mult)
            v.memset(g_["p_eps"][:, :], EPS)


def _p1_rms(nc, env):
    g_ = env
    p_h, p_onescol, p_ones1, p_norm_w = g_["p_h"], g_["p_onescol"], g_["p_ones1"], g_["p_norm_w"]
    x_fm = g_["x_fm"]
    with nc.Block("p1") as blk, ExitStack() as ps:
        E = ps.enter_context
        xa = E(nc.sbuf_tensor("p1_xa", [128, TOK], F32))
        xb = E(nc.sbuf_tensor("p1_xb", [128, TOK], F32))
        x2 = E(nc.sbuf_tensor("p1_x2", [128, TOK], BF16))
        rrow = E(nc.sbuf_tensor("p1_rrow", [1, TOK], F32))
        rrbf = E(nc.sbuf_tensor("p1_rrbf", [1, TOK], BF16))
        ps_ss = E(nc.psum_tensor("p1_psss", [1, TOK], F32))
        ps_rb = E(nc.psum_tensor("p1_psrb", [128, TOK], F32))
        sx2 = [g_["SEM"]("p1_sx0"), g_["SEM"]("p1_sx1")]
        sq = g_["SEM"]("p1_sq")
        sm = g_["SEM"]("p1_sm")
        sv = g_["SEM"]("p1_sv")
        xt = [xa, xb]

        @blk.sync
        def _(sy):
            for k in range(KD):
                if k >= 2:
                    sy.wait_ge(sq, k - 1)
                sy.dma_start(out=xt[k % 2][:, :],
                             in_=x_fm[k * 128:(k + 1) * 128, :]).then_inc(sx2[k % 2], 16)
            for k in range(KD):
                if k < 2:
                    sy.wait_ge(sq, KD)
                else:
                    sy.wait_ge(sv, k - 1)
                sy.dma_start(out=xt[k % 2][:, :],
                             in_=x_fm[k * 128:(k + 1) * 128, :]).then_inc(sx2[k % 2], 16)

        @blk.scalar
        def _(sc):
            sc = Trk(sc, g_["CHS"] if CHAIN else None)
            for k in range(KD):
                sc.wait_ge(sx2[k % 2], 16 * (k // 2 + 1))
                if k > 0:
                    sc.wait_ge(sm, k)
                sc.activation(x2[:, :], xt[k % 2][:, :], AF.Square).then_inc(sq, 1)
            sc.wait_ge(sm, KD)
            sc.activation(rrow[:, :], ps_ss[:, :], AF.Ln, scale=1.0 / D,
                          bias=g_["p_eps"][0:1, :]).then_inc(sq, 1)
            sc.wait_ge(sq, KD + 1)
            sc.activation(rrow[:, :], rrow[:, :], AF.Exp, scale=-0.5).then_inc(sq, 1)

        @blk.tensor
        def _(t):
            t = Trk(t)
            for k in range(KD):
                t.wait_ge(sq, k + 1)
                t.matmul(ps_ss[0:1, 0:512], p_onescol[:, :], x2[:, 0:512],
                         start=(k == 0), stop=(k == KD - 1))
                t.matmul(ps_ss[0:1, 512:1024], p_onescol[:, :], x2[:, 512:1024],
                         start=(k == 0), stop=(k == KD - 1)).then_inc(sm, 1)
            t.wait_ge(sq, KD + 3)
            t.matmul(ps_rb[:, 0:512], p_ones1[:, :], rrbf[:, 0:512])
            t.matmul(ps_rb[:, 512:1024], p_ones1[:, :], rrbf[:, 512:1024]
                     ).then_inc(sm, 1)

        @blk.vector
        def _(v):
            v = Trk(v, g_["CHV"] if CHAIN else None)
            v.wait_ge(sq, KD + 2)
            v.tensor_copy(rrbf[:, :], rrow[:, :]).then_inc(sq, 1)
            v.wait_ge(sm, KD + 1)
            for k in range(KD):
                v.wait_ge(sx2[k % 2], 16 * (KD // 2 + k // 2 + 1))
                v.scalar_tensor_tensor(p_h[:, k * TOK:(k + 1) * TOK],
                                       xt[k % 2][:, :], p_norm_w[:, k:k + 1],
                                       ps_rb[:, :], ALU.mult, ALU.mult
                                       ).then_inc(sv, 1)


def _p2_dt(nc, env):
    g_ = env
    p_h = g_["p_h"]
    with nc.Block("p2") as blk, ExitStack() as ps:
        E = ps.enter_context
        wdt = E(nc.sbuf_tensor("p2_wdt", [128, KD * 128], BF16))
        ps_dt = E(nc.psum_tensor("p2_psdt", [128, TOK], F32))
        t_abs = E(nc.sbuf_tensor("p2_abs", [128, TOK], F32))
        t_e = E(nc.sbuf_tensor("p2_e", [128, TOK], F32))
        t_r = E(nc.sbuf_tensor("p2_r", [128, TOK], F32))
        t_a = E(nc.sbuf_tensor("p2_a", [128, TOK], F32))
        t_w = E(nc.sbuf_tensor("p2_w", [128, TOK], F32))
        t_hi = E(nc.sbuf_tensor("p2_hi", [128, TOK], BF16))
        p_dtv = E(nc.sbuf_tensor("p2_dtv", [128, TOK], F32))
        p_acs = E(nc.sbuf_tensor("p2_acs", [128, TOK], F32))
        t_lo = E(nc.sbuf_tensor("p2_lo", [128, TOK], BF16))
        sw = g_["SEM"]("p2_sw")
        sm = g_["SEM"]("p2_sm")
        sa = g_["SEM"]("p2_sa")
        sv = g_["SEM"]("p2_sv")
        ss = g_["SEM"]("p2_ss")

        @blk.gpsimd
        def _(g):
            g.dma_start(out=wdt.ap().rearrange("p (k c) -> p k c", k=KD),
                        in_=g_["w_in_t"][:, DT0:DT0 + 128].rearrange(
                            "(k p) c -> p k c", p=128)).then_inc(sw, 16)

        @blk.tensor
        def _(t):
            t = Trk(t)
            t.wait_ge(sw, 16)
            for k in range(KD):
                t.matmul(ps_dt[:, 0:512], wdt[:, k * 128:(k + 1) * 128],
                         p_h[:, k * TOK:k * TOK + 512],
                         start=(k == 0), stop=(k == KD - 1))
                t.matmul(ps_dt[:, 512:1024], wdt[:, k * 128:(k + 1) * 128],
                         p_h[:, k * TOK + 512:(k + 1) * TOK],
                         start=(k == 0), stop=(k == KD - 1))
            t.mi(sm, 1)

        @blk.scalar
        def _(sc):
            sc = Trk(sc, g_["CHS"] if CHAIN else None)
            sch = g_["SEM"]("p2_sch")
            sc.wait_ge(sm, 1)
            sc.activation(t_abs[:, :], ps_dt[:, :], AF.Abs,
                          bias=g_["p_dtbias"][:, :]).then_inc(sch, 1)
            sc.wait_ge(sch, 1)
            sc.activation(t_e[:, :], t_abs[:, :], AF.Exp, scale=-1.0).then_inc(sch, 1)
            sc.wait_ge(sch, 2)
            sc.activation(t_e[:, :], t_e[:, :], AF.Ln, bias=1.0)
            sc.activation(t_r[:, :], ps_dt[:, :], AF.Relu,
                          bias=g_["p_dtbias"][:, :]).then_inc(sa, 1)
            for k in range(NCH):
                sc.wait_ge(sv, 3 + k)
                ck = slice(k * 128, (k + 1) * 128)
                last = k * 128 + 127
                sc.activation(t_w[:, ck], p_acs[:, ck], AF.Exp, scale=-1.0,
                              bias=p_acs[:, last:last + 1])
                sc.activation(g_["p_ubf"][:, ck], p_acs[:, ck], AF.Exp)
                sc.activation(g_["p_dcol"][:, k:k + 1],
                              p_acs[:, last:last + 1], AF.Exp).then_inc(sa, 1)

        @blk.vector
        def _(v):
            v = Trk(v, g_["CHV"] if CHAIN else None)
            v.wait_ge(sa, 1)
            v.tensor_add(p_dtv[:, :], t_r[:, :], t_e[:, :]).then_inc(sv, 1)
            v.tensor_scalar(t_a[:, :], p_dtv[:, :], g_["p_negea"][:, :],
                            None, ALU.mult).then_inc(sv, 1)
            for k in range(NCH):
                ck = slice(k * 128, (k + 1) * 128)
                v.tensor_tensor_scan(p_acs[:, ck], t_a[:, ck], t_a[:, ck],
                                     0.0, ALU.add, ALU.bypass).then_inc(sv, 1)
            v.tensor_copy(g_["p_dtbf"][:, :], p_dtv[:, :])
            v.tensor_copy(t_hi[:, :], p_acs[:, :])
            v.tensor_tensor(t_lo[:, :], p_acs[:, :], t_hi[:, :], ALU.subtract)
            v.tensor_scalar(g_["p_nahi"][:, :], t_hi[:, :], -1.0, None, ALU.mult)
            v.tensor_scalar(g_["p_nalo"][:, :], t_lo[:, :], -1.0, None, ALU.mult)
            v.wait_ge(sa, 1 + NCH)
            v.tensor_tensor(g_["p_dwbf"][:, :], p_dtv[:, :], t_w[:, :],
                            ALU.mult)
            v.memset(g_["p_cdcol"][:, 0:1], 1.0)
            for k in range(1, NCH):
                v.tensor_tensor(g_["p_cdcol"][:, k:k + 1],
                                g_["p_cdcol"][:, k - 1:k],
                                g_["p_dcol"][:, k - 1:k], ALU.mult)
            v.mi(sv, 1)

        @blk.sync
        def _(sy):
            sy.wait_ge(sv, 3 + NCH)
            sy.dma_start(out=g_["ahi_d"], in_=t_hi[:, :]).then_inc(ss, 16)
            sy.dma_start(out=g_["alo_d"], in_=t_lo[:, :]).then_inc(ss, 16)
            sy.wait_ge(ss, 32)


def _p3_inproj(nc, env, hbc):
    g_ = env
    p_h = g_["p_h"]
    tag = "p3" if hbc else "p5"
    NB = NHBC if hbc else NGATE
    col0 = HBC0 if hbc else 0
    with nc.Block(tag) as blk, ExitStack() as ps:
        E = ps.enter_context
        wb = [E(nc.sbuf_tensor(f"{tag}_w{i}", [128, KD * 128], BF16)) for i in range(3)]
        pp2 = [E(nc.psum_tensor(f"{tag}_ps{i}", [128, TOK], F32)) for i in range(2)]
        stg = [E(nc.sbuf_tensor(f"{tag}_st{i}", [128, TOK], BF16)) for i in range(2)]
        hbc_f = E(nc.sbuf_tensor(f"{tag}_hf", [128, TOK], F32)) if hbc else None
        cbuf = E(nc.sbuf_tensor(f"{tag}_cb", [128, TOK], F32)) if hbc else None
        sw3 = [g_["SEM"](f"{tag}_sw{i}") for i in range(3)]
        smm = g_["SEM"](f"{tag}_smm")
        sev = g_["SEM"](f"{tag}_sev")
        scv = g_["SEM"](f"{tag}_scv")
        ssi = g_["SEM"](f"{tag}_ssi")
        sst2 = [g_["SEM"](f"{tag}_sst0"), g_["SEM"](f"{tag}_sst1")]
        shm = g_["SEM"](f"{tag}_shm")

        @blk.gpsimd
        def _(g):
            for cc in range(NB):
                if cc >= 3:
                    g.wait_ge(smm, cc - 2)
                c0 = col0 + cc * 128
                g.dma_start(out=wb[cc % 3].ap().rearrange("p (k c) -> p k c", k=KD),
                            in_=g_["w_in_t"][:, c0:c0 + 128].rearrange(
                                "(k p) c -> p k c", p=128)).then_inc(sw3[cc % 3], 16)
            if hbc:
                g.wait_ge(shm, 16)
                g.collective_compute("AllReduce", ALU.add, replica_groups=RG,
                                     ins=[g_["cc1_in"].opt()],
                                     outs=[g_["cc1_out"].opt()]
                                     ).then_inc(g_["s_cc1"], 1)

        @blk.tensor
        def _(t):
            t = Trk(t)
            for cc in range(NB):
                t.wait_ge(sw3[cc % 3], 16 * (cc // 3 + 1))
                if cc >= 2:
                    t.wait_ge(sev if hbc else ssi, cc - 1)
                pp = pp2[cc % 2]
                w = wb[cc % 3]
                for k in range(KD):
                    t.matmul(pp[:, 0:512], w[:, k * 128:(k + 1) * 128],
                             p_h[:, k * TOK:k * TOK + 512],
                             start=(k == 0), stop=(k == KD - 1))
                    t.matmul(pp[:, 512:1024], w[:, k * 128:(k + 1) * 128],
                             p_h[:, k * TOK + 512:(k + 1) * TOK],
                             start=(k == 0), stop=(k == KD - 1))
                t.mi(smm, 1)

        @blk.vector
        def _(v):
            v = Trk(v, g_["CHV"] if CHAIN else None)
            if hbc:
                v.memset(stg[0][:, 0:3], 0.0)
                v.memset(stg[1][:, 0:3], 0.0)
                v.memset(g_["p_bc"][:, :], 0.0)
            for cc in range(NB):
                v.wait_ge(smm, cc + 1)
                pp = pp2[cc % 2]
                if hbc:
                    v.tensor_copy(hbc_f[:, 0:512], pp[:, 0:512])
                    v.tensor_copy(hbc_f[:, 512:1024], pp[:, 512:1024]
                                  ).then_inc(sev, 1)
                    v.tensor_copy(g_["p_halo"][:, cc * 3:cc * 3 + 3],
                                  hbc_f[:, 1021:1024])
                    v.tensor_copy(g_["p_pre3"][:, cc * 3:cc * 3 + 3],
                                  hbc_f[:, 0:3])
                    if cc >= 1:
                        v.wait_ge(ssi, cc)
                    cw = g_["p_conv_w"]
                    v.tensor_scalar(cbuf[:, 3:1024], hbc_f[:, 0:1021],
                                    cw[:, cc * 4:cc * 4 + 1],
                                    g_["p_conv_b"][:, cc:cc + 1],
                                    ALU.mult, ALU.add)
                    for i in (1, 2, 3):
                        v.scalar_tensor_tensor(cbuf[:, 3:1024],
                                               hbc_f[:, i:i + 1021],
                                               cw[:, cc * 4 + i:cc * 4 + i + 1],
                                               cbuf[:, 3:1024], ALU.mult, ALU.add)
                    v.mi(scv, 1)
                else:
                    if cc >= 2:
                        v.wait_ge(sst2[cc % 2], 16 * (cc // 2))
                    v.tensor_copy(stg[cc % 2][:, 0:512], pp[:, 0:512])
                    v.tensor_copy(stg[cc % 2][:, 512:1024], pp[:, 512:1024]
                                  ).then_inc(ssi, 1)
            if hbc:
                v.wait_ge(scv, NB)
                v.tensor_scalar(g_["p_halo"][:, :], g_["p_halo"][:, :],
                                g_["p_masks"][:, :], None, ALU.mult
                                ).then_inc(scv, 1)

        @blk.scalar
        def _(sc):
            sc = Trk(sc, g_["CHS"] if CHAIN else None)
            if not hbc:
                return
            for cc in range(NB):
                sc.wait_ge(scv, cc + 1)
                if cc < 64:
                    if cc >= 2:
                        sc.wait_ge(sst2[cc % 2], 16 * (cc // 2))
                    sc.activation(stg[cc % 2][:, 3:1024], cbuf[:, 3:1024],
                                  AF.Silu).then_inc(ssi, 1)
                else:
                    b = cc - 64
                    sc.activation(g_["p_bc"][:, b * TOK + 3:(b + 1) * TOK],
                                  cbuf[:, 3:1024], AF.Silu).then_inc(ssi, 1)

        @blk.sync
        def _(sy):
            for cc in range(NB):
                sy.wait_ge(ssi, cc + 1)
                if hbc:
                    if cc < 64:
                        sy.dma_start(out=g_["hid_d"][cc * 128:(cc + 1) * 128, :],
                                     in_=stg[cc % 2][:, :]
                                     ).then_inc(sst2[cc % 2], 16)
                    else:
                        sy.nop().then_inc(sst2[cc % 2], 16)
                else:
                    sy.dma_start(out=g_["gate_d"][cc * 128:(cc + 1) * 128, :],
                                 in_=stg[cc % 2][:, :]).then_inc(sst2[cc % 2], 16)
            sy.wait_ge(sst2[0], 16 * ((NB + 1) // 2))
            sy.wait_ge(sst2[1], 16 * (NB // 2))
            if hbc:
                sy.wait_ge(scv, NB + 1)
                sy.dma_start(out=g_["cc1_in"], in_=g_["p_halo"][:, :]
                             ).then_inc(shm, 16)


def _p4_sweep1(nc, env):
    g_ = env

    def sev_e64d(k, gi):
        return 56 * k + 7 * gi + 1

    def sev_hd(k, gi, hf):
        return 56 * k + 7 * gi + 2 + 3 * hf

    def sev_hdw(k, gi, hf):
        return 56 * k + 7 * gi + 3 + 3 * hf

    def sev_rec(k, gi, hf):
        return 56 * k + 7 * gi + 4 + 3 * hf

    def mm_A(k, gi, hf):
        return 64 * k + 8 * gi + 4 * hf + 1

    def mm_B(k, gi, hf):
        return 64 * k + 8 * gi + 4 * hf + 2

    def mm_S(k, gi, hf):
        return 64 * k + 8 * gi + 4 * hf + 3

    def mm_D(k, gi, hf):
        return 64 * k + 8 * gi + 4 * hf + 4

    def unq2(q2):
        q, hf = q2 // 2, q2 % 2
        return q // 8, q % 8, hf

    with nc.Block("p4") as blk, ExitStack() as ps:
        E = ps.enter_context
        T0 = [E(nc.sbuf_tensor(f"s1_t0{i}", [128, DIN], BF16)) for i in range(2)]
        T1 = [E(nc.sbuf_tensor(f"s1_t1{i}", [128, DIN], BF16)) for i in range(2)]
        T2 = E(nc.sbuf_tensor("s1_t2", [128, DIN], BF16))
        T3 = [E(nc.sbuf_tensor(f"s1_t3{i}", [128, 512], BF16)) for i in range(2)]
        P = E(nc.sbuf_tensor("s1_P", [128, DIN], F32))
        btok = [E(nc.sbuf_tensor(f"s1_btok{i}", [128, G * 128], BF16)) for i in range(2)]
        e64d = [E(nc.sbuf_tensor(f"s1_e64d{i}", [128, 1024], BF16)) for i in range(2)]
        tsb = E(nc.sbuf_tensor("s1_tsb", [128, 512], F32))
        pfx = E(nc.sbuf_tensor("s1_pfx", [128, NHBC * 3], F32))
        cat6 = E(nc.sbuf_tensor("s1_cat6", [128, NHBC * 6], F32))
        tp6t = [E(nc.sbuf_tensor(f"s1_tp{i}", [128, NHBC * 3], F32))
                for i in range(2)]
        tp6 = [t_.ap().rearrange("p (c s) -> p c s", s=3) for t_ in tp6t]
        p3t = E(nc.sbuf_tensor("s1_p3t", [128, NHBC * 3], F32))
        psA = [E(nc.psum_tensor(f"s1_psA{i}", [128, 512], F32)) for i in range(2)]
        psB = [E(nc.psum_tensor(f"s1_psB{i}", [128, 512], F32)) for i in range(2)]
        psS = [E(nc.psum_tensor(f"s1_psS{i}", [128, 512], F32)) for i in range(2)]
        psD = [E(nc.psum_tensor(f"s1_psD{i}", [128, 512], F32)) for i in range(2)]
        sld = g_["SEM"]("s1_sld")
        spfx = g_["SEM"]("s1_spfx")
        stpy = g_["SEM"]("s1_stpy")
        stpb = g_["SEM"]("s1_stpb")
        smm = g_["SEM"]("s1_smm")
        sev = g_["SEM"]("s1_sev")
        spl = g_["SEM"]("s1_spl")
        shd = g_["SEM"]("s1_shd")
        spt = g_["SEM"]("s1_spt")
        spz = g_["SEM"]("s1_spz")
        sc2 = g_["SEM"]("s1_sc2")

        @blk.sync
        def _(sy):
            sy.wait_ge(g_["s_cc1"], 1)
            sy.dma_start(out=pfx[:, :], in_=g_["cc1_out"]).then_inc(spfx, 16)
            sy.wait_ge(spt, 3)
            sy.dma_start(
                out=g_["hid_d"].rearrange("(c p) t -> p c t", p=128)[:, :, 0:3],
                in_=g_["p_patch"].ap().rearrange("p (c l) -> p c l", c=64)
            ).then_inc(spt, 16)
            for k in range(NCH):
                if k == 0:
                    sy.wait_ge(spt, 19)
                if k >= 2:
                    sy.wait_ge(stpy, 16 * 64 * (k - 1))
                sy.dma_start(
                    out=T0[k % 2].ap().rearrange("p (c l) -> p c l", c=64),
                    in_=g_["hid_d"].rearrange("(c p) t -> p c t", p=128)
                    [:, :, k * 128:(k + 1) * 128]).then_inc(sld, 16)
                if k >= 2:
                    sy.wait_ge(sev, 56 * (k - 1))
                sy.wait_ge(sld, 16 * (k + 1))
                for cc in range(64):
                    sy.dma_start(
                        out=T1[k % 2][:, cc * 128:(cc + 1) * 128],
                        in_=T0[k % 2][:, cc * 128:(cc + 1) * 128],
                        transpose=True).then_inc(stpy, 16)
                if k >= 2:
                    sy.wait_ge(smm, 64 * (k - 1))
                for gi in range(G):
                    sy.dma_start(
                        out=btok[k % 2][:, gi * 128:(gi + 1) * 128],
                        in_=g_["p_bc"][:, gi * TOK + k * 128:gi * TOK + (k + 1) * 128],
                        transpose=True).then_inc(stpb, 16)
            sy.wait_ge(shd, 16 * NCH)
            sy.wait_ge(sev, 56 * NCH + 1)
            sy.dma_start(out=g_["cc2_in"], in_=P[:, :]).then_inc(sc2, 16)

        @blk.scalar
        def _(sc):
            sc = Trk(sc, g_["CHS"] if CHAIN else None)
            sc.wait_ge(spt, 2)
            sc.activation(g_["p_patch"][:, :], p3t[:, 0:192], AF.Silu)
            for bcc in range(16):
                sc.activation(g_["p_bc"][:, bcc * TOK:bcc * TOK + 3],
                              p3t[:, (64 + bcc) * 3:(64 + bcc) * 3 + 3], AF.Silu)
            sc.mi(spt, 1)


        @blk.vector
        def _(v):
            v = Trk(v, g_["CHV"] if CHAIN else None)
            v.memset(P[:, :], 0.0).then_inc(spz, 1)
            v.wait_ge(spfx, 16)
            v.tensor_scalar(pfx[:, :], pfx[:, :], g_["p_maskb"][:, :], None,
                            ALU.mult).then_inc(spt, 1)
            c6v = cat6.ap().rearrange("p (c s) -> p c s", s=6)
            v.tensor_copy(c6v[:, :, 0:3],
                          pfx[:, :].rearrange("p (c s) -> p c s", s=3))
            v.tensor_copy(c6v[:, :, 3:6],
                          g_["p_pre3"][:, :].rearrange("p (c s) -> p c s", s=3))
            wv = g_["p_conv_w"][:, :].rearrange("p (c t) -> p c t", t=4)
            p3v = p3t[:, :].rearrange("p (c s) -> p c s", s=3)
            v.tensor_tensor(p3v, c6v[:, :, 0:3],
                            wv[:, :, 0:1].broadcast_to([128, NHBC, 3]),
                            ALU.mult)
            for i in (1, 2, 3):
                v.tensor_tensor(tp6[i % 2], c6v[:, :, i:i + 3],
                                wv[:, :, i:i + 1].broadcast_to([128, NHBC, 3]),
                                ALU.mult)
                v.tensor_tensor(p3v, p3v, tp6[i % 2], ALU.add)
            v.tensor_tensor(p3v, p3v,
                            g_["p_conv_b"][:, :, None].broadcast_to(
                                [128, NHBC, 3]), ALU.add)
            v.mi(spt, 1)
            for k in range(NCH):
                v.wait_ge(stpy, 16 * 64 * (k + 1))
                v.wait_ge(shd, 16 * k)
                for gi in range(G):
                    q = 8 * k + gi
                    if q >= 2:
                        pk, pgi = (q - 2) // 8, (q - 2) % 8
                        v.wait_ge(smm, mm_D(pk, pgi, 1))
                    v.tensor_scalar(e64d[q % 2][:, :],
                                    g_["p_e64"][:, gi * 1024:(gi + 1) * 1024],
                                    g_["p_dcol"][:, k:k + 1], None, ALU.mult
                                    ).then_inc(sev, 1)
                    for hf in range(2):
                        q2 = 2 * q + hf
                        sl = slice(gi * 1024 + hf * 512,
                                   gi * 1024 + (hf + 1) * 512)
                        v.wait_ge(smm, mm_A(k, gi, hf))
                        v.tensor_tensor(T2[:, sl], T1[k % 2][:, sl],
                                        psA[q2 % 2][:, :], ALU.mult
                                        ).then_inc(sev, 1)
                        v.wait_ge(smm, mm_B(k, gi, hf))
                        if q2 >= 2:
                            v.wait_ge(smm, mm_S(*unq2(q2 - 2)))
                        v.tensor_tensor(T3[q2 % 2][:, :], T1[k % 2][:, sl],
                                        psB[q2 % 2][:, :], ALU.mult
                                        ).then_inc(sev, 1)
                        v.wait_ge(smm, mm_D(k, gi, hf))
                        if gi == 0 and hf == 0:
                            v.wait_ge(spl, 16 * (k + 1))
                        v.tensor_tensor(tsb[:, :], psD[q2 % 2][:, :],
                                        P[:, sl], ALU.mult)
                        v.tensor_tensor(P[:, sl], tsb[:, :], psS[q2 % 2][:, :],
                                        ALU.add).then_inc(sev, 1)
            v.tensor_scalar(P[:, :], P[:, :], g_["p_masks"][:, :], None,
                            ALU.mult).then_inc(sev, 1)

        @blk.tensor
        def _(t):
            t = Trk(t)
            for k in range(NCH):
                for gi in range(G):
                    q = 8 * k + gi
                    if gi == 0:
                        t.wait_ge(stpy, 16 * 64 * (k + 1))
                        t.wait_ge(stpb, 16 * 8 * (k + 1))
                    t.wait_ge(sev, sev_e64d(k, gi))
                    for hf in range(2):
                        q2 = 2 * q + hf
                        sl = slice(gi * 1024 + hf * 512,
                                   gi * 1024 + (hf + 1) * 512)
                        if q2 >= 2:
                            t.wait_ge(sev, sev_hd(*unq2(q2 - 2)))
                        dtfm = g_["p_dtbf"][:, k * 128:(k + 1) * 128]
                        t.matmul(psA[q2 % 2][:, :], dtfm, g_["p_e64"][:, sl]
                                 ).then_inc(smm, 1)
                        if q2 >= 2:
                            t.wait_ge(sev, sev_hdw(*unq2(q2 - 2)))
                        dwfm = g_["p_dwbf"][:, k * 128:(k + 1) * 128]
                        t.matmul(psB[q2 % 2][:, :], dwfm, g_["p_e64"][:, sl]
                                 ).then_inc(smm, 1)
                        if q2 >= 2:
                            t.wait_ge(sev, sev_rec(*unq2(q2 - 2)))
                        t.wait_ge(sev, sev_hdw(k, gi, hf))
                        bt = btok[k % 2][:, gi * 128:(gi + 1) * 128]
                        t.matmul(psS[q2 % 2][:, :], bt, T3[q2 % 2][:, :]
                                 ).then_inc(smm, 1)
                        t.matmul(psD[q2 % 2][:, :], g_["p_onessq"][:, :],
                                 e64d[q % 2][:, hf * 512:(hf + 1) * 512]
                                 ).then_inc(smm, 1)

        @blk.gpsimd
        def _(g):
            for k in range(NCH):
                if k == 0:
                    g.wait_ge(spz, 1)
                else:
                    g.wait_ge(sev, 56 * k)
                g.dma_start(out=g_["prev_d"][k * 128:(k + 1) * 128, :],
                            in_=P[:, :]).then_inc(spl, 16)
                g.wait_ge(sev, sev_hd(k, G - 1, 1))
                g.dma_start(out=g_["hd_d"][k * 128:(k + 1) * 128, :],
                            in_=T2[:, :]).then_inc(shd, 16)
            g.wait_ge(sc2, 16)
            g.collective_compute("AllReduce", ALU.add, replica_groups=RG,
                                 ins=[g_["cc2_in"].opt()],
                                 outs=[g_["cc2_out"].opt()]).then_inc(g_["s_cc2"], 1)


def _p6_sweep2(nc, env):
    g_ = env

    mmv = {}
    n = 0

    def ev(tag):
        nonlocal_ns[0] += 1
        mmv[tag] = nonlocal_ns[0]
    nonlocal_ns = [0]
    ev(("cdexp",))
    for k in range(NCH):
        for hg in range(16):
            gi, hh = hg // 2, hg % 2
            if hg == 0:
                ev(("cb", k, 0))
                ev(("st", k, 0))
            if hg < 15:
                if hh == 1:
                    ev(("cb", k, gi + 1))
                ev(("st", k, hg + 1))
            ev(("u", k, hg))
            ev(("y", k, hg))
        ev(("ep1", k))
        ev(("ep2", k))

    def ev_pt(k):
        return 44 * k + 2

    def ev_mtm(k, gi):
        return 44 * k + 3 + 5 * gi

    def ev_mtl(k, hg):
        return 44 * k + 4 + 5 * (hg // 2) + 2 * (hg % 2)

    def ev_ab(k, hg):
        return 44 * k + 5 + 5 * (hg // 2) + 2 * (hg % 2)

    def ev_dres(k):
        return 44 * k + 43

    def ev_y3(k):
        return 44 * k + 44

    def ev_yn(k):
        return 44 * k + 45

    def ac_le(k, hg):
        return 2 + 19 * k + hg

    def ac_silu(k):
        return 1 + 19 * k + 17

    def ac_ysq(k):
        return 1 + 19 * k + 18

    def ac_rows(k):
        return 1 + 19 * k + 19

    with nc.Block("p6") as blk, ExitStack() as ps:
        E = ps.enter_context
        T0 = E(nc.sbuf_tensor("s2_t0", [128, DIN], BF16))
        T1 = E(nc.sbuf_tensor("s2_t1", [128, DIN], BF16))
        Tg = E(nc.sbuf_tensor("s2_tg", [128, DIN], BF16))
        T5 = E(nc.sbuf_tensor("s2_t5", [128, DIN], BF16))
        YA = E(nc.sbuf_tensor("s2_ya", [128, DIN], F32))
        init = E(nc.sbuf_tensor("s2_init", [128, DIN], BF16))
        afhi = E(nc.sbuf_tensor("s2_afhi", [16, 1024], BF16))
        aflo = E(nc.sbuf_tensor("s2_aflo", [16, 1024], BF16))
        cdbf = E(nc.sbuf_tensor("s2_cdbf", [128, NCH], BF16))
        cdbf2 = E(nc.sbuf_tensor("s2_cdbf2", [128, 512], BF16))
        mtm = E(nc.sbuf_tensor("s2_mtm", [128, 128], BF16))
        rlf = E(nc.sbuf_tensor("s2_rlf", [128, 1024], F32))
        exps = [E(nc.sbuf_tensor(f"s2_exps{i}", [128, 1024], BF16)) for i in range(2)]
        mtl = [E(nc.sbuf_tensor(f"s2_mtl{i}", [128, 1024], BF16)) for i in range(2)]
        tpb = E(nc.sbuf_tensor("s2_tpb", [128, 1024], BF16))
        uexp = E(nc.sbuf_tensor("s2_uexp", [128, 512], BF16))
        tmp5 = E(nc.sbuf_tensor("s2_tmp5", [128, 512], BF16))
        rrow = E(nc.sbuf_tensor("s2_rrow", [1, 1024], F32))
        rrbf = E(nc.sbuf_tensor("s2_rrbf", [1, 1024], BF16))
        psS = [E(nc.psum_tensor(f"s2_pss{i}", [128, 1024], F32)) for i in range(2)]
        psMT = E(nc.psum_tensor("s2_psmt", [128, 128], F32))
        psU = E(nc.psum_tensor("s2_psu", [128, 512], F32))
        psYd = E(nc.psum_tensor("s2_psyd", [128, 512], F32))
        psYo = E(nc.psum_tensor("s2_psyo", [128, 512], F32))
        sld = g_["SEM"]("s2_sld")
        smm = g_["SEM"]("s2_smm")
        sev = g_["SEM"]("s2_sev")
        sac = g_["SEM"]("s2_sac")
        sst = g_["SEM"]("s2_sst")
        sini = g_["SEM"]("s2_sini")

        @blk.gpsimd
        def _(g):
            g.wait_ge(g_["s_cc2"], 1)
            g.dma_start(out=init[:, :], in_=g_["cc2_out"]).then_inc(sini, 16)

        @blk.sync
        def _(sy):
            for k in range(NCH):
                if k > 0:
                    sy.wait_ge(sev, ev_dres(k - 1))
                sy.dma_start(
                    out=T0.ap().rearrange("p (c l) -> p c l", c=64),
                    in_=g_["hid_d"].rearrange("(c p) t -> p c t", p=128)
                    [:, :, k * 128:(k + 1) * 128]).then_inc(sld, 16)
                if k > 0:
                    sy.wait_ge(smm, mmv[("y", k - 1, 15)])
                    sy.wait_ge(sst, 16 * k)
                sy.dma_start(out=T1[:, :], in_=g_["hd_d"][k * 128:(k + 1) * 128, :]
                             ).then_inc(sld, 16)
                sy.dma_start(out=T5[:, :], in_=g_["prev_d"][k * 128:(k + 1) * 128, :]
                             ).then_inc(sld, 16)
                if k > 0:
                    sy.wait_ge(smm, mmv[("st", k - 1, 15)])
                sy.dma_start(out=afhi[:, :],
                             in_=bass.AP(g_["ahi_d"].tensor, k * 128,
                                         [[8 * TOK, 16], [TOK, 8], [1, 128]])
                             ).then_inc(sld, 16)
                sy.dma_start(out=aflo[:, :],
                             in_=bass.AP(g_["alo_d"].tensor, k * 128,
                                         [[8 * TOK, 16], [TOK, 8], [1, 128]])
                             ).then_inc(sld, 16)
                if k > 0:
                    sy.wait_ge(smm, mmv[("ep1", k - 1)])
                sy.dma_start(
                    out=Tg.ap().rearrange("p (c l) -> p c l", c=64),
                    in_=g_["gate_d"].rearrange("(c p) t -> p c t", p=128)
                    [:, :, k * 128:(k + 1) * 128]).then_inc(sld, 16)
                sy.wait_ge(sev, ev_yn(k))
                sy.dma_start(
                    out=g_["ynorm_d"].rearrange("(c p) t -> p c t", p=128)
                    [:, :, k * 128:(k + 1) * 128],
                    in_=T1.ap().rearrange("p (c l) -> p c l", c=64)
                ).then_inc(sst, 16)
            sy.wait_ge(sst, 16 * NCH)

        @blk.scalar
        def _(sc):
            sc = Trk(sc, g_["CHS"] if CHAIN else None)
            sc.activation(cdbf[:, :], g_["p_cdcol"][:, :], AF.Copy).then_inc(sac, 1)
            for k in range(NCH):
                for hg in range(16):
                    sc.wait_ge(smm, mmv[("st", k, hg)])
                    if 16 * k + hg > 0:
                        phg = 16 * k + hg - 1
                        sc.wait_ge(sev, ev_mtl(phg // 16, phg % 16))
                    sc.activation(rlf[:, :], psS[hg % 2][:, :], AF.Relu,
                                  scale=-1.0)
                    sc.activation(exps[hg % 2][:, :], rlf[:, :], AF.Exp,
                                  scale=-1.0).then_inc(sac, 1)
                sc.wait_ge(sev, ev_dres(k))
                sc.activation(Tg[:, :], Tg[:, :], AF.Silu).then_inc(sac, 1)
                sc.wait_ge(sev, ev_y3(k))
                sc.activation(Tg[:, :], YA[:, :], AF.Square).then_inc(sac, 1)
                sc.wait_ge(smm, mmv[("ep1", k)])
                sc.activation(rrow[0:1, 0:512], psYd[0:1, 0:512], AF.Ln,
                              scale=1.0 / 1024, bias=g_["p_eps"][0:1, :])
                sc.activation(rrow[0:1, 512:1024], psYo[0:1, 0:512], AF.Ln,
                              scale=1.0 / 1024, bias=g_["p_eps"][0:1, :])
                sc.activation(rrow[:, :], rrow[:, :], AF.Exp, scale=-0.5)
                sc.activation(rrbf[:, :], rrow[:, :], AF.Copy).then_inc(sac, 1)

        @blk.vector
        def _(v):
            v = Trk(v, g_["CHV"] if CHAIN else None)
            v.wait_ge(smm, 1)
            v.tensor_copy(cdbf2[:, :], psS[0][:, 0:512]).then_inc(sev, 1)
            v.wait_ge(sini, 16)
            v.tensor_scalar(init[:, :], init[:, :], g_["p_maskb"][:, :], None,
                            ALU.mult)
            cdv = cdbf2.ap().rearrange("p (r s) -> p r s", s=NCH)
            tp3 = tpb.ap().rearrange("p (r w) -> p r w", w=128)
            in3 = init.ap().rearrange("p (r w) -> p r w", w=128)
            for k in range(NCH):
                v.wait_ge(sld, 16 * 6 * (k + 1))
                if k > 0:
                    v.wait_ge(smm, mmv[("y", k - 1, 15)])
                for qq in range(17):
                    if qq < 16:
                        v.tensor_tensor(
                            tp3[:, (qq % 2) * 4:(qq % 2) * 4 + 4, :],
                            in3[:, qq * 4:(qq + 1) * 4, :],
                            cdv[:, qq * 4:(qq + 1) * 4, k:k + 1]
                            .broadcast_to([128, 4, 128]), ALU.mult)
                    if qq > 0:
                        qs = slice((qq - 1) * 512, qq * 512)
                        v.tensor_tensor(T5[:, qs], T5[:, qs],
                                        tpb[:, ((qq - 1) % 2) * 512:
                                            ((qq - 1) % 2) * 512 + 512],
                                        ALU.add)
                v.mi(sev, 1)
                for gi in range(G):
                    v.wait_ge(smm, mmv[("cb", k, gi)])
                    v.tensor_tensor(mtm[:, :], psMT[:, :], g_["p_mask01"][:, :],
                                    ALU.mult).then_inc(sev, 1)
                    for hh in range(2):
                        hg = 2 * gi + hh
                        v.wait_ge(sac, ac_le(k, hg))
                        v.tensor_tensor(
                            mtl[hg % 2].ap().rearrange("p (a b) -> p a b", a=8),
                            exps[hg % 2].ap().rearrange("p (a b) -> p a b", a=8),
                            mtm[:, None, :].broadcast_to([128, 8, 128]),
                            ALU.mult).then_inc(sev, 1)
                        v.wait_ge(smm, mmv[("u", k, hg)])
                        v.tensor_copy(uexp[:, :], psU[:, :])
                        v.wait_ge(smm, mmv[("y", k, hg)])
                        v.tensor_tensor(tmp5[:, :], psYo[:, :], uexp[:, :],
                                        ALU.mult)
                        ysl = slice(hg * 512, (hg + 1) * 512)
                        v.tensor_tensor(YA[:, ysl], tmp5[:, :], psYd[:, :],
                                        ALU.add).then_inc(sev, 1)
                for pr in range(NPAIR):
                    sl = slice(pr * 128, (pr + 1) * 128)
                    v.scalar_tensor_tensor(YA[:, sl], T0[:, sl],
                                           g_["p_dp"][:, pr:pr + 1],
                                           YA[:, sl], ALU.mult, ALU.add)
                v.mi(sev, 1)
                v.wait_ge(sac, ac_silu(k))
                v.tensor_tensor(YA[:, :], YA[:, :], Tg[:, :], ALU.mult
                                ).then_inc(sev, 1)
                v.wait_ge(smm, mmv[("ep2", k)])
                for pr in range(NPAIR):
                    sl = slice(pr * 128, (pr + 1) * 128)
                    gidx = pr // 8
                    bsrc = psYd[:, (gidx % 4) * 128:(gidx % 4 + 1) * 128] \
                        if gidx < 4 else \
                        psYo[:, (gidx - 4) * 128:(gidx - 3) * 128]
                    v.scalar_tensor_tensor(T1[:, sl], YA[:, sl],
                                           g_["p_gnw"][:, pr:pr + 1],
                                           bsrc, ALU.mult, ALU.mult)
                v.mi(sev, 1)

        @blk.tensor
        def _(t):
            t = Trk(t)
            t.wait_ge(sac, 1)
            for pr in range(NPAIR):
                t.matmul(psS[0][:, pr * 8:(pr + 1) * 8],
                         g_["p_e64"][:, pr * 128:(pr + 1) * 128], cdbf[:, :])
            t.mi(smm, 1)

            def emit_cb(k, gi):
                if (k, gi) != (0, 0):
                    pk, pgi = (k, gi - 1) if gi > 0 else (k - 1, G - 1)
                    t.wait_ge(sev, ev_mtm(pk, pgi))
                bsl = slice(gi * TOK + k * 128, gi * TOK + (k + 1) * 128)
                csl = slice((8 + gi) * TOK + k * 128,
                            (8 + gi) * TOK + (k + 1) * 128)
                t.matmul(psMT[:, :], g_["p_bc"][:, bsl], g_["p_bc"][:, csl]
                         ).then_inc(smm, 1)

            def emit_st(k, hg):
                gi, hh = hg // 2, hg % 2
                khg = 16 * k + hg
                if khg >= 2:
                    phg = khg - 2
                    t.wait_ge(sac, ac_le(phg // 16, phg % 16))
                if hg == 0 and k == 0:
                    t.wait_ge(sev, 1)
                h0 = (gi * 16 + hh * 8) * 128
                nh = g_["p_nahi"][:, k * 128:(k + 1) * 128]
                nl = g_["p_nalo"][:, k * 128:(k + 1) * 128]
                hrow = 2 * gi + hh
                oh = g_["p_onehot"][:, hrow * 128:(hrow + 1) * 128]
                for reg in range(2):
                    rs = slice(reg * 512, (reg + 1) * 512)
                    habs = gi * 16 + hh * 8 + reg * 4
                    idb = g_["p_idbf"][:, habs:habs + 4][:, :, None] \
                        .broadcast_to([128, 4, 128])
                    out3 = psS[hg % 2].ap().rearrange(
                        "p (a b) -> p a b", a=2)[:, reg:reg + 1, :] \
                        .rearrange("p a (c d) -> p (a c) d", c=4)
                    t.matmul(psS[hg % 2][:, rs], oh, afhi[:, rs],
                             start=True, stop=False)
                    t.matmul(psS[hg % 2][:, rs], oh, aflo[:, rs],
                             start=False, stop=False)
                    t.matmul(out3, nh, idb, start=False, stop=False)
                    t.matmul(out3, nl, idb, start=False, stop=True)
                t.mi(smm, 1)

            def emit_u(k, hg):
                gi, hh = hg // 2, hg % 2
                khg = 16 * k + hg
                if khg >= 1:
                    phg = khg - 1
                    t.wait_ge(sev, ev_ab(phg // 16, phg % 16))
                for pp in range(4):
                    pr = gi * 8 + hh * 4 + pp
                    t.matmul(psU[:, pp * 128:(pp + 1) * 128],
                             g_["p_e64"][:, pr * 128:(pr + 1) * 128],
                             g_["p_ubf"][:, k * 128:(k + 1) * 128])
                t.mi(smm, 1)

            def emit_y(k, hg):
                gi, hh = hg // 2, hg % 2
                t.wait_ge(sev, ev_mtl(k, hg))
                t.wait_ge(sev, ev_pt(k))
                csl = slice((8 + gi) * TOK + k * 128,
                            (8 + gi) * TOK + (k + 1) * 128)
                for pp in range(4):
                    for par in range(2):
                        m = 2 * pp + par
                        hgl = gi * 16 + hh * 8 + m
                        t.matmul(psYd[par * 64:(par + 1) * 64,
                                      pp * 128:(pp + 1) * 128],
                                 T1[:, hgl * 64:(hgl + 1) * 64],
                                 mtl[hg % 2][:, m * 128:(m + 1) * 128],
                                 tile_position=(0, par * 64))
                for pp in range(4):
                    for par in range(2):
                        m = 2 * pp + par
                        hgl = gi * 16 + hh * 8 + m
                        t.matmul(psYo[par * 64:(par + 1) * 64,
                                      pp * 128:(pp + 1) * 128],
                                 T5[:, hgl * 64:(hgl + 1) * 64],
                                 g_["p_bc"][:, csl],
                                 tile_position=(0, par * 64))
                t.mi(smm, 1)

            for k in range(NCH):
                t.wait_ge(sld, 16 * (6 * k + 6))
                for hg in range(16):
                    if hg == 0:
                        emit_cb(k, 0)
                        emit_st(k, 0)
                    if hg < 15:
                        if hg % 2 == 1:
                            emit_cb(k, hg // 2 + 1)
                        emit_st(k, hg + 1)
                    emit_u(k, hg)
                    emit_y(k, hg)
                t.wait_ge(sac, ac_ysq(k))
                for pr in range(NPAIR):
                    gidx = pr // 8
                    dst = psYd[0:1, (gidx % 4) * 128:(gidx % 4) * 128 + 128] \
                        if gidx < 4 else \
                        psYo[0:1, (gidx - 4) * 128:(gidx - 4) * 128 + 128]
                    t.matmul(dst, g_["p_onescol"][:, :],
                             Tg[:, pr * 128:(pr + 1) * 128],
                             start=(pr % 8 == 0), stop=(pr % 8 == 7))
                t.mi(smm, 1)
                t.wait_ge(sac, ac_rows(k))
                for gidx in range(G):
                    dst = psYd[:, (gidx % 4) * 128:(gidx % 4) * 128 + 128] \
                        if gidx < 4 else \
                        psYo[:, (gidx - 4) * 128:(gidx - 4) * 128 + 128]
                    t.matmul(dst, g_["p_ones1"][:, :],
                             rrbf[:, gidx * 128:(gidx + 1) * 128])
                t.mi(smm, 1)


def _p7_outproj(nc, env):
    g_ = env
    p_x1 = g_["p_x1"]
    NOCB = 8

    def it_(tt, ocb):
        return tt * NOCB + ocb

    def mm_piece(n, piece):
        return 8 * n + piece + 1

    def mm_mm(tt, ocb):
        return 8 * it_(tt, ocb) + 4

    def mm_stats(tt, ocb, oc):
        return 8 * it_(tt, ocb) + 5 + oc

    MM_BCAST = 129

    def ev_evac(tt, ocb, oc):
        return 4 * it_(tt, ocb) + oc + 1

    EV_MV, EV_VAR, EV_B = 65, 66, 67

    def ac_sq(tt, ocb, oc):
        return 4 * it_(tt, ocb) + oc + 1

    AC_M2, AC_RSTD = 65, 66

    ps7 = ExitStack()
    E7 = ps7.enter_context
    mrow = E7(nc.sbuf_tensor("p7_mrow", [1, TOK], F32))
    vrow = E7(nc.sbuf_tensor("p7_vrow", [1, TOK], F32))
    m2r = E7(nc.sbuf_tensor("p7_m2r", [1, TOK], F32))
    rstdr = E7(nc.sbuf_tensor("p7_rstdr", [1, TOK], F32))
    arow = E7(nc.sbuf_tensor("p7_arow", [1, TOK], BF16))
    browf = E7(nc.sbuf_tensor("p7_browf", [1, TOK], F32))
    brow = E7(nc.sbuf_tensor("p7_brow", [1, TOK], BF16))
    smm = g_["SEM"]("p7_smm")
    sev = g_["SEM"]("p7_sev")
    sac = g_["SEM"]("p7_sac")
    with nc.Block("p7") as blk, ExitStack() as ps:
        E = ps.enter_context
        ynh = E(nc.sbuf_tensor("p7_ynh", [128, NPAIR * 512], BF16))
        wst = [E(nc.sbuf_tensor(f"p7_w{i}", [128, 16 * 512], BF16)) for i in range(2)]
        xbl = E(nc.sbuf_tensor("p7_xbl", [128, 4 * 512], F32))
        rtmp = E(nc.sbuf_tensor("p7_rtmp", [128, 512], F32))
        sqb = E(nc.sbuf_tensor("p7_sqb", [128, 512], BF16))
        ps4 = [E(nc.psum_tensor("p7_ps0", [128, 2048], F32))] * 2
        psM = E(nc.psum_tensor("p7_psM", [1, TOK], F32))
        psQ = E(nc.psum_tensor("p7_psQ", [1, TOK], F32))
        syl = g_["SEM"]("p7_syl")
        sx = g_["SEM"]("p7_sx")
        sw2 = [g_["SEM"]("p7_sw0"), g_["SEM"]("p7_sw1")]

        @blk.sync
        def _(sy):
            for tt in range(2):
                if tt > 0:
                    sy.wait_ge(smm, mm_mm(0, NOCB - 1))
                sy.dma_start(
                    out=ynh.ap().rearrange("p (c t) -> p c t", c=NPAIR),
                    in_=g_["ynorm_d"].rearrange("(c p) t -> p c t", p=128)
                    [:, :, tt * 512:(tt + 1) * 512]).then_inc(syl, 16)
                for ocb in range(NOCB):
                    n = it_(tt, ocb)
                    if n >= 1:
                        sy.wait_ge(sev, 4 * n)
                    sy.dma_start(
                        out=xbl.ap().rearrange("p (c t) -> p c t", c=4),
                        in_=g_["x_fm"].rearrange("(c p) t -> p c t", p=128)
                        [:, ocb * 4:(ocb + 1) * 4, tt * 512:(tt + 1) * 512]
                    ).then_inc(sx, 16)

        @blk.gpsimd
        def _(g):
            for tt in range(2):
                for ocb in range(NOCB):
                    for piece in range(4):
                        n = it_(tt, ocb) * 4 + piece
                        if n >= 2:
                            g.wait_ge(smm, mm_piece((n - 2) // 4, (n - 2) % 4))
                        r0 = piece * 16 * 128
                        g.dma_start(
                            out=wst[n % 2].ap().rearrange("p (k c) -> p k c", k=16),
                            in_=g_["w_out_t"][r0:r0 + 16 * 128,
                                              ocb * 512:(ocb + 1) * 512]
                            .rearrange("(k p) c -> p k c", p=128)
                        ).then_inc(sw2[n % 2], 16)

        @blk.tensor
        def _(t):
            t = Trk(t)
            for tt in range(2):
                for ocb in range(NOCB):
                    n = it_(tt, ocb)
                    t.wait_ge(syl, 16 * (tt + 1))
                    if n >= 1:
                        t.wait_ge(sev, 4 * n)
                    pp = ps4[n % 2]
                    for piece in range(4):
                        ns = 4 * n + piece
                        t.wait_ge(sw2[ns % 2], 16 * (ns // 2 + 1))
                        for kk in range(16):
                            kc = piece * 16 + kk
                            for oc in range(4):
                                t.matmul(pp[:, oc * 512:(oc + 1) * 512],
                                         wst[ns % 2]
                                         [:, kk * 512 + oc * 128:kk * 512 + (oc + 1) * 128],
                                         ynh[:, kc * 512:(kc + 1) * 512],
                                         start=(kc == 0), stop=(kc == 63))
                        t.mi(smm, 1)
                    for oc in range(4):
                        t.wait_ge(sev, ev_evac(tt, ocb, oc))
                        t.wait_ge(sac, ac_sq(tt, ocb, oc))
                        gcol = slice(tt * 512, (tt + 1) * 512)
                        first = (ocb == 0 and oc == 0)
                        last = (ocb == NOCB - 1 and oc == 3)
                        occ = ocb * 4 + oc
                        t.matmul(psM[0:1, gcol], g_["p_onescol"][:, :],
                                 p_x1[:, occ * TOK + tt * 512:occ * TOK + (tt + 1) * 512],
                                 start=first, stop=last)
                        t.matmul(psQ[0:1, gcol], g_["p_onescol"][:, :],
                                 sqb[:, :], start=first, stop=last).then_inc(smm, 1)

        @blk.vector
        def _(v):
            v = Trk(v, g_["CHV"] if CHAIN else None)
            for tt in range(2):
                for ocb in range(NOCB):
                    n = it_(tt, ocb)
                    v.wait_ge(smm, mm_mm(tt, ocb))
                    v.wait_ge(sx, 16 * (n + 1))
                    pp = ps4[n % 2]
                    for oc in range(4):
                        occ = ocb * 4 + oc
                        pn = 4 * n + oc
                        if pn > 0:
                            v.wait_ge(sac, pn)
                        v.scalar_tensor_tensor(rtmp[:, :],
                                               xbl[:, oc * 512:(oc + 1) * 512],
                                               2.0, pp[:, oc * 512:(oc + 1) * 512],
                                               ALU.mult, ALU.add)
                        v.tensor_copy(
                            p_x1[:, occ * TOK + tt * 512:occ * TOK + (tt + 1) * 512],
                            rtmp[:, :]).then_inc(sev, 1)
            v.wait_ge(smm, mm_stats(1, NOCB - 1, 3))
            v.tensor_scalar(mrow[:, :], psM[0:1, :], 1.0 / D, None, ALU.mult)
            v.tensor_scalar(vrow[:, :], psQ[0:1, :], 1.0 / D, None, ALU.mult
                            ).then_inc(sev, 1)

        @blk.scalar
        def _(sc):
            sc = Trk(sc, g_["CHS"] if CHAIN else None)
            for tt in range(2):
                for ocb in range(NOCB):
                    for oc in range(4):
                        sc.wait_ge(sev, ev_evac(tt, ocb, oc))
                        pn = 4 * it_(tt, ocb) + oc
                        if pn > 0:
                            ptt, pocb, poc = pn - 1, None, None
                            pi, poc = divmod(pn - 1, 4)
                            sc.wait_ge(smm, mm_stats(pi // NOCB, pi % NOCB, poc))
                        sc.activation(sqb[:, :], rtmp[:, :], AF.Square
                                      ).then_inc(sac, 1)
            sc.wait_ge(sev, EV_MV)
            sc.activation(m2r[:, :], mrow[:, :], AF.Square).then_inc(sac, 1)


    with nc.Block("p7b") as blk, ExitStack() as ps:
        E = ps.enter_context
        psAB = E(nc.psum_tensor("p7_psAB", [128, 2 * TOK], F32))
        ttmp = E(nc.sbuf_tensor("p7b_ttmp", [128, TOK], F32))

        @blk.vector
        def _(v):
            v = Trk(v, g_["CHV"] if CHAIN else None)
            v.wait_ge(sac, AC_M2)
            v.tensor_tensor(vrow[:, :], vrow[:, :], m2r[:, :], ALU.subtract
                            ).then_inc(sev, 1)
            v.wait_ge(sac, AC_RSTD)
            v.tensor_tensor(browf[:, :], mrow[:, :], rstdr[:, :], ALU.mult)
            v.tensor_scalar(brow[:, :], browf[:, :], -1.0, None, ALU.mult
                            ).then_inc(sev, 1)
            v.wait_ge(smm, MM_BCAST)
            for kc in range(KD):
                v.tensor_tensor(ttmp[:, :], p_x1[:, kc * TOK:(kc + 1) * TOK],
                                psAB[:, 0:TOK], ALU.mult)
                v.tensor_tensor(p_x1[:, kc * TOK:(kc + 1) * TOK], ttmp[:, :],
                                psAB[:, TOK:2 * TOK], ALU.add).then_inc(sev, 1)

        @blk.scalar
        def _(sc):
            sc = Trk(sc, g_["CHS"] if CHAIN else None)
            sc.wait_ge(sev, EV_VAR)
            sc.activation(rstdr[:, :], vrow[:, :], AF.Ln,
                          bias=g_["p_eps"][0:1, :])
            sc.activation(rstdr[:, :], rstdr[:, :], AF.Exp, scale=-0.5)
            sc.activation(arow[:, :], rstdr[:, :], AF.Copy).then_inc(sac, 1)

        @blk.tensor
        def _(t):
            t = Trk(t)
            t.wait_ge(sac, AC_RSTD)
            t.wait_ge(sev, EV_B)
            for r in range(2):
                t.matmul(psAB[:, r * 512:(r + 1) * 512], g_["p_ones1"][:, :],
                         arow[:, r * 512:(r + 1) * 512])
                t.matmul(psAB[:, TOK + r * 512:TOK + (r + 1) * 512],
                         g_["p_ones1"][:, :], brow[:, r * 512:(r + 1) * 512])
            t.mi(smm, 1)
    ps7.close()


def _p8_ffn(nc, env):
    g_ = env
    p_x1 = g_["p_x1"]

    def mm_is(kc):
        return 4 + kc

    def mm_fp(ocb, piece):
        return 36 + 6 * ocb + piece

    def mm_os(ocb):
        return 36 + 6 * ocb + 4

    def mm_stats(ocb):
        return 36 + 6 * ocb + 5

    MM_BCAST = 228

    def ev_xin(kc):
        return 6 + kc

    def ev_ose(ocb):
        return 38 + 2 * ocb

    def ev_o2(ocb):
        return 39 + 2 * ocb

    EV_MV, EV_VAR, EV_B = 102, 103, 104

    def ev_fin(kc):
        return 105 + kc

    def ac_sq(ocb):
        return 2 + ocb

    AC_M2, AC_RSTD = 34, 35

    with nc.Block("p8") as blk, ExitStack() as ps:
        E = ps.enter_context
        xin = E(nc.sbuf_tensor("p8_xin", [128, KD * TOK], BF16))
        gwbf = E(nc.sbuf_tensor("p8_gwbf", [128, KD * 8], BF16))
        icsbf = E(nc.sbuf_tensor("p8_icsbf", [8, D], BF16))
        ocsbf = E(nc.sbuf_tensor("p8_ocsbf", [8, D], BF16))
        lrow = E(nc.sbuf_tensor("p8_lrow", [8, TOK], F32))
        stok = E(nc.sbuf_tensor("p8_stok", [128, 64], F32))
        estok = E(nc.sbuf_tensor("p8_estok", [128, 64], F32))
        nmx = E(nc.sbuf_tensor("p8_nmx", [128, 8], F32))
        ssum = E(nc.sbuf_tensor("p8_ssum", [128, 8], F32))
        srec = E(nc.sbuf_tensor("p8_srec", [128, 8], F32))
        rtok = E(nc.sbuf_tensor("p8_rtok", [128, 64], F32))
        routg = E(nc.sbuf_tensor("p8_routg", [8, TOK], BF16))
        wstg = [E(nc.sbuf_tensor(f"p8_wf{i}", [128, 8 * 128], BF16)) for i in range(2)]
        wsgn = [E(nc.sbuf_tensor(f"p8_ws{i}", [128, 8 * 128], BF16)) for i in range(2)]
        osb = E(nc.sbuf_tensor("p8_osb", [128, TOK], F32))
        ftmp = E(nc.sbuf_tensor("p8_ftmp", [128, TOK], F32))
        o2b = [E(nc.sbuf_tensor(f"p8_o2{i}", [128, TOK], BF16)) for i in range(2)]
        sqf = E(nc.sbuf_tensor("p8_sqf", [128, TOK], BF16))
        fob = [E(nc.sbuf_tensor(f"p8_fo{i}", [128, TOK], BF16)) for i in range(2)]
        outf2 = E(nc.sbuf_tensor("p8_of1", [128, TOK], F32))
        outf = [osb, outf2]
        mrow = E(nc.sbuf_tensor("p8_mrow", [1, TOK], F32))
        vrow = E(nc.sbuf_tensor("p8_vrow", [1, TOK], F32))
        m2r = E(nc.sbuf_tensor("p8_m2r", [1, TOK], F32))
        rstdr = vrow
        arow = E(nc.sbuf_tensor("p8_arow", [1, TOK], BF16))
        browf = m2r
        brow = E(nc.sbuf_tensor("p8_brow", [1, TOK], BF16))
        PS1 = E(nc.psum_tensor("p8_PS1", [128, TOK], F32))
        PS2 = E(nc.psum_tensor("p8_PS2", [128, TOK], F32))
        PS3 = E(nc.psum_tensor("p8_PS3", [128, TOK], F32))
        PS4 = E(nc.psum_tensor("p8_PS4", [128, TOK], F32))
        sgld = g_["SEM"]("p8_sgld")
        ssw2 = [g_["SEM"]("p8_ssw0"), g_["SEM"]("p8_ssw1")]
        ssg = g_["SEM"]("p8_ssg")
        smm = g_["SEM"]("p8_smm")
        sev = g_["SEM"]("p8_sev")
        sac = g_["SEM"]("p8_sac")
        sld2 = [g_["SEM"]("p8_sld0"), g_["SEM"]("p8_sld1")]
        s2s2 = [g_["SEM"]("p8_s2sa"), g_["SEM"]("p8_s2sb")]
        sst2 = [g_["SEM"]("p8_ssta"), g_["SEM"]("p8_sstb")]

        @blk.gpsimd
        def _(g):
            g.dma_start(out=gwbf.ap().rearrange("p (k c) -> p k c", k=KD),
                        in_=g_["gw_t"].rearrange("(k p) c -> p k c", p=128)
                        ).then_inc(sgld, 16)
            g.dma_start(out=icsbf[:, :], in_=g_["ics_i"]).then_inc(sgld, 16)
            g.dma_start(out=ocsbf[:, :], in_=g_["ocs_i"]).then_inc(sgld, 16)
            for ocb in range(32):
                for piece in range(4):
                    n = 4 * ocb + piece
                    if n >= 2:
                        pocb, ppc = divmod(n - 2, 4)
                        g.wait_ge(smm, mm_fp(pocb, ppc))
                    r0 = piece * 8 * 128
                    g.dma_start(
                        out=wstg[n % 2].ap().rearrange("p (k c) -> p k c", k=8),
                        in_=g_["w_ffn_t"][r0:r0 + 8 * 128, ocb * 128:(ocb + 1) * 128]
                        .rearrange("(k p) c -> p k c", p=128)
                    ).then_inc(ssw2[n % 2], 16)

        @blk.scalar
        def _(sc):
            sc = Trk(sc, g_["CHS"] if CHAIN else None)
            sc.wait_ge(sev, 3)
            for tc in range(8):
                sc.activation(estok[:, tc * 8:(tc + 1) * 8],
                              stok[:, tc * 8:(tc + 1) * 8], AF.Exp,
                              bias=nmx[:, tc:tc + 1],
                              accum_out=ssum[:, tc:tc + 1])
            sc.mi(sac, 1)
            for ocb in range(32):
                for piece in range(4):
                    n = 4 * ocb + piece
                    sc.wait_ge(ssw2[n % 2], 16 * (n // 2 + 1))
                    if n >= 2:
                        pocb, ppc = divmod(n - 2, 4)
                        sc.wait_ge(smm, mm_fp(pocb, ppc))
                    sc.activation(wsgn[n % 2][:, :], wstg[n % 2][:, :],
                                  AF.Sign).then_inc(ssg, 1)
                sc.wait_ge(sev, ev_o2(ocb))
                if ocb > 0:
                    sc.wait_ge(smm, mm_stats(ocb - 1))
                sc.activation(sqf[:, :], o2b[ocb % 2][:, :], AF.Square
                              ).then_inc(sac, 1)
            sc.wait_ge(sev, EV_MV)
            sc.activation(m2r[:, :], mrow[:, :], AF.Square).then_inc(sac, 1)
            sc.wait_ge(sev, EV_VAR)
            sc.activation(rstdr[:, :], vrow[:, :], AF.Ln,
                          bias=g_["p_eps"][0:1, :])
            sc.activation(rstdr[:, :], rstdr[:, :], AF.Exp, scale=-0.5)
            sc.activation(arow[:, :], rstdr[:, :], AF.Copy).then_inc(sac, 1)

        @blk.tensor
        def _(t):
            t = Trk(t)
            t.wait_ge(sgld, 48)
            for kc in range(KD):
                t.matmul(PS1[0:8, 0:512], gwbf[:, kc * 8:(kc + 1) * 8],
                         p_x1[:, kc * TOK:kc * TOK + 512],
                         start=(kc == 0), stop=(kc == KD - 1))
                t.matmul(PS1[0:8, 512:1024], gwbf[:, kc * 8:(kc + 1) * 8],
                         p_x1[:, kc * TOK + 512:(kc + 1) * TOK],
                         start=(kc == 0), stop=(kc == KD - 1))
            t.mi(smm, 1)
            t.wait_ge(sev, 1)
            for tc in range(8):
                t.transpose(PS2[:, tc * 8:(tc + 1) * 8],
                            lrow[:, tc * 128:(tc + 1) * 128], g_["p_iden8"][:, :])
            t.mi(smm, 1)
            t.wait_ge(sev, 4)
            for tc in range(8):
                t.transpose(PS3[0:8, tc * 128:(tc + 1) * 128],
                            rtok[:, tc * 8:(tc + 1) * 8], g_["p_iden128"][:, :])
            t.mi(smm, 1)
            t.wait_ge(sev, 5)
            for kc in range(KD):
                if kc >= 1:
                    t.wait_ge(sev, ev_xin(kc - 1))
                t.matmul(PS4[:, 0:512], icsbf[:, kc * 128:(kc + 1) * 128],
                         routg[:, 0:512])
                t.matmul(PS4[:, 512:1024], icsbf[:, kc * 128:(kc + 1) * 128],
                         routg[:, 512:1024]).then_inc(smm, 1)
            for ocb in range(32):
                if ocb == 0:
                    t.wait_ge(sev, ev_xin(KD - 1))
                else:
                    t.wait_ge(sev, ev_o2(ocb - 1))
                for piece in range(4):
                    n = 4 * ocb + piece
                    t.wait_ge(ssg, n + 1)
                    for kk in range(8):
                        kc = piece * 8 + kk
                        for half in range(2):
                            t.matmul(
                                PS1[:, half * 512:(half + 1) * 512],
                                wsgn[n % 2][:, kk * 128:(kk + 1) * 128],
                                xin[:, kc * TOK + half * 512:kc * TOK + (half + 1) * 512],
                                start=(kc == 0), stop=(kc == KD - 1))
                    t.mi(smm, 1)
                occ = ocb
                for half in range(2):
                    t.matmul(PS2[:, half * 512:(half + 1) * 512],
                             ocsbf[:, occ * 128:(occ + 1) * 128],
                             routg[:, half * 512:(half + 1) * 512])
                t.mi(smm, 1)
                t.wait_ge(sev, ev_o2(ocb))
                t.wait_ge(sac, ac_sq(ocb))
                first, last = (ocb == 0), (ocb == 31)
                for half in range(2):
                    hc = slice(half * 512, (half + 1) * 512)
                    t.matmul(PS3[0:1, hc], g_["p_onescol"][:, :],
                             o2b[ocb % 2][:, hc], start=first, stop=last)
                    t.matmul(PS4[0:1, hc], g_["p_onescol"][:, :],
                             sqf[:, hc], start=first, stop=last)
                t.mi(smm, 1)
            t.wait_ge(sac, AC_RSTD)
            t.wait_ge(sev, EV_B)
            for r in range(2):
                t.matmul(PS3[:, r * 512:(r + 1) * 512], g_["p_ones1"][:, :],
                         arow[:, r * 512:(r + 1) * 512])
                t.matmul(PS4[:, r * 512:(r + 1) * 512], g_["p_ones1"][:, :],
                         brow[:, r * 512:(r + 1) * 512])
            t.mi(smm, 1)

        @blk.vector
        def _(v):
            v = Trk(v, g_["CHV"] if CHAIN else None)
            v.wait_ge(smm, 1)
            v.tensor_copy(lrow[:, :], PS1[0:8, :]).then_inc(sev, 1)
            v.wait_ge(smm, 2)
            v.tensor_copy(stok[:, :], PS2[:, 0:64]).then_inc(sev, 1)
            v.tensor_reduce(nmx.ap().rearrange("p (a b) -> p a b", b=1),
                            stok.ap().rearrange("p (a b) -> p a b", a=8),
                            mybir.AxisListType.X, ALU.max, negate=True
                            ).then_inc(sev, 1)
            v.wait_ge(sac, 1)
            v.reciprocal(srec[:, :], ssum[:, :])
            v.tensor_tensor(rtok.ap().rearrange("p (a b) -> p a b", a=8),
                            estok.ap().rearrange("p (a b) -> p a b", a=8),
                            srec[:, :, None].broadcast_to([128, 8, 8]),
                            ALU.mult).then_inc(sev, 1)
            v.wait_ge(smm, 3)
            v.tensor_copy(routg[:, :], PS3[0:8, :]).then_inc(sev, 1)
            for kc in range(KD):
                v.wait_ge(smm, mm_is(kc))
                v.tensor_tensor(xin[:, kc * TOK:(kc + 1) * TOK],
                                p_x1[:, kc * TOK:(kc + 1) * TOK],
                                PS4[:, :], ALU.mult).then_inc(sev, 1)
            for ocb in range(32):
                v.wait_ge(smm, mm_os(ocb))
                v.tensor_copy(osb[:, :], PS2[:, :]).then_inc(sev, 1)
                occ = ocb
                v.tensor_tensor(ftmp[:, :], PS1[:, :], osb[:, :], ALU.mult)
                if occ >= 2:
                    v.wait_ge(s2s2[occ % 2], 16 * (occ // 2))
                    v.wait_ge(smm, mm_stats(occ - 2))
                v.scalar_tensor_tensor(o2b[ocb % 2][:, :], ftmp[:, :],
                                       g_["p_ffnb"][:, occ:occ + 1],
                                       p_x1[:, occ * TOK:(occ + 1) * TOK],
                                       ALU.add, ALU.add).then_inc(sev, 1)
            v.wait_ge(smm, mm_stats(31))
            v.tensor_scalar(mrow[:, :], PS3[0:1, :], 1.0 / D, None, ALU.mult)
            v.tensor_scalar(vrow[:, :], PS4[0:1, :], 1.0 / D, None, ALU.mult
                            ).then_inc(sev, 1)
            v.wait_ge(sac, AC_M2)
            v.tensor_tensor(vrow[:, :], vrow[:, :], m2r[:, :], ALU.subtract
                            ).then_inc(sev, 1)
            v.wait_ge(sac, AC_RSTD)
            v.tensor_tensor(browf[:, :], mrow[:, :], rstdr[:, :], ALU.mult)
            v.tensor_scalar(brow[:, :], browf[:, :], -1.0, None, ALU.mult
                            ).then_inc(sev, 1)
            v.wait_ge(smm, MM_BCAST)
            for kc in range(KD):
                v.wait_ge(sld2[kc % 2], 16 * (kc // 2 + 1))
                if kc >= 2:
                    v.wait_ge(sst2[kc % 2], 16 * (kc // 2))
                v.tensor_tensor(ftmp[:, :], fob[kc % 2][:, :], PS3[:, :],
                                ALU.mult)
                v.tensor_tensor(outf[kc % 2][:, :], ftmp[:, :], PS4[:, :],
                                ALU.add).then_inc(sev, 1)

        @blk.sync
        def _(sy):
            for ocb in range(32):
                sy.wait_ge(sev, ev_o2(ocb))
                sy.dma_start(out=g_["out2_d"][ocb * 128:(ocb + 1) * 128, :],
                             in_=o2b[ocb % 2][:, :]).then_inc(s2s2[ocb % 2], 16)
            for kc in range(KD):
                sy.wait_ge(s2s2[kc % 2], 16 * (kc // 2 + 1))
                if kc >= 2:
                    sy.wait_ge(sev, ev_fin(kc - 2))
                sy.dma_start(out=fob[kc % 2][:, :],
                             in_=g_["out2_d"][kc * 128:(kc + 1) * 128, :]
                             ).then_inc(sld2[kc % 2], 16)
                if kc >= 1:
                    sy.wait_ge(sev, ev_fin(kc - 1))
                    sy.dma_start(out=g_["out_d"][(kc - 1) * 128:kc * 128, :],
                                 in_=outf[(kc - 1) % 2][:, :]
                                 ).then_inc(sst2[(kc - 1) % 2], 16)
            sy.wait_ge(sev, ev_fin(KD - 1))
            sy.dma_start(out=g_["out_d"][(KD - 1) * 128:KD * 128, :],
                         in_=outf[(KD - 1) % 2][:, :]
                         ).then_inc(sst2[(KD - 1) % 2], 16)
            sy.wait_ge(sst2[0], 16 * (KD // 2))
            sy.wait_ge(sst2[1], 16 * (KD // 2))
            sy.wait_ge(s2s2[0], 16 * 16)
            sy.wait_ge(s2s2[1], 16 * 16)


def _finish_dummy(nc, out_d, s_fin):
    with nc.Block("fin") as blk, ExitStack() as ps:
        z = ps.enter_context(nc.sbuf_tensor("zfin", [128, TOK], F32))

        sz = ps.enter_context(nc.semaphore("fin_sz"))

        @blk.vector
        def _(v):
            v.memset(z[:, :], 0.0).then_inc(sz, 1)

        @blk.sync
        def _(sy):
            sy.wait_ge(sz, 1)
            for k in range(KD):
                sy.dma_start(out=out_d[k * 128:(k + 1) * 128, :], in_=z[:, :]
                             ).then_inc(s_fin, 16)
            sy.wait_ge(s_fin, 16 * KD)


def make_consts():
    c = {}
    c["e64_i"] = (np.arange(DIN)[None, :] // PD == np.arange(128)[:, None]).astype(BF)
    c["idbf_i"] = np.eye(128, dtype=np.float32).astype(BF)
    jj, ii = np.meshgrid(np.arange(128), np.arange(128), indexing="ij")
    c["mask01_i"] = (jj <= ii).astype(BF)
    c["ones1_i"] = np.ones((1, 128), dtype=BF)
    c["onescol_i"] = np.ones((128, 1), dtype=BF)
    c["onessq_i"] = np.ones((128, 128), dtype=BF)
    c["iden8_i"] = np.eye(8, dtype=np.float32)
    oh = np.zeros((16, 16, 128), dtype=np.float32)
    for t in range(16):
        oh[t, t, :] = 1.0
    c["onehot16_i"] = oh.transpose(1, 0, 2).reshape(16, 2048).astype(BF)
    c["iden128_i"] = np.eye(128, dtype=np.float32)
    return c


def prep_inputs(inp):
    g = {}
    g["w_in_t"] = np.ascontiguousarray(inp["L0_attn_in_proj_w"].T)
    g["w_out_t"] = np.ascontiguousarray(inp["L0_attn_out_proj_w"].T)
    g["w_ffn_t"] = np.ascontiguousarray(inp["L0_ffn_weight"].T)
    g["gw_t"] = np.ascontiguousarray(inp["L0_ffn_gate_w"].T)
    g["ics_i"] = np.ascontiguousarray(inp["L0_ffn_in_channel_scale"])
    g["ocs_i"] = np.ascontiguousarray(inp["L0_ffn_out_channel_scale"])
    g["norm_w_i"] = np.ascontiguousarray(inp["L0_attn_norm_w"].reshape(KD, 128).T)
    g["conv_w_i"] = np.ascontiguousarray(
        inp["L0_attn_conv_w"].reshape(NHBC, 128, KCONV).transpose(1, 0, 2)
        .reshape(128, NHBC * KCONV))
    g["conv_b_i"] = np.ascontiguousarray(inp["L0_attn_conv_b"].reshape(NHBC, 128).T)
    g["dtb_i"] = np.ascontiguousarray(inp["L0_attn_dt_bias"].reshape(128, 1))
    g["alog_i"] = np.ascontiguousarray(inp["L0_attn_A_log"].reshape(128, 1))
    g["ffnb_i"] = np.ascontiguousarray(inp["L0_ffn_bias"].reshape(KD, 128).T)
    g["gnw_i"] = np.ascontiguousarray(inp["L0_attn_gnorm_w"].reshape(NPAIR, 128).T)
    dp = inp["L0_attn_Dp"]
    dpc = np.empty((128, NPAIR), dtype=np.float32)
    dpc[:64, :] = dp[0::2][None, :]
    dpc[64:, :] = dp[1::2][None, :]
    g["dp_i"] = np.ascontiguousarray(dpc)
    g.update(make_consts())
    return g


def core_inputs(inp, g, b, j):
    m = dict(g)
    x = inp["L0_attn_x"][b, j * TOK:(j + 1) * TOK, :]
    m["x_fm"] = np.ascontiguousarray(x.T)
    m["masks_i"] = np.full((128, 1), 1.0 if j == 0 else 0.0, dtype=np.float32)
    m["maskb_i"] = np.full((128, 1), 1.0 if j == 1 else 0.0, dtype=np.float32)
    return m


def run(inp, stage=99, debug=()):
    key = (stage, tuple(n for n, _, _ in debug))
    if key not in _CACHE:
        _CACHE[key] = build_program(stage=stage, debug=debug)
    nc = _CACHE[key]
    g = prep_inputs(inp)
    in_maps = [core_inputs(inp, g, c // 2, c % 2) for c in range(8)]
    return run_bass_kernel_spmd(nc, in_maps, core_ids=list(range(8)))


def kernel(**inputs):
    res = run(inputs, stage=99)
    out = np.empty((B, S, D), dtype=np.float32)
    for c in range(8):
        b, j = c // 2, c % 2
        out[b, j * TOK:(j + 1) * TOK, :] = res.results[c]["out_d"].T
    return out


# version 12
# speedup vs baseline: 1.0090x; 1.0090x over previous
import numpy as np
import ml_dtypes
from contextlib import ExitStack

import concourse.bass as bass
import concourse.mybir as mybir
from concourse.bass_utils import run_bass_kernel_spmd

F32 = mybir.dt.float32
BF16 = mybir.dt.bfloat16
AF = mybir.ActivationFunctionType
ALU = mybir.AluOpType
BF = ml_dtypes.bfloat16

B, S, D = 4, 2048, 4096
H, PD, G, ND = 128, 64, 8, 128
KCONV, CHUNK = 4, 128
DIN = H * PD
CONVD = DIN + 2 * G * ND
PROJ = DIN + CONVD + H
EPS = 1e-5
TOK = 1024
NCH = TOK // CHUNK
KD = D // 128
NPAIR = DIN // 128
NHBC = CONVD // 128
NGATE = DIN // 128
HBC0 = DIN
DT0 = DIN + CONVD
RG = [[0, 1], [2, 3], [4, 5], [6, 7]]

_CACHE = {}
CHAIN = "deps"


def _box(o):
    th = o.tensor
    name = th.name
    try:
        shp = list(th.shape)
        rowlen = 1
        for s_ in shp[1:]:
            rowlen *= int(s_)
        pat = [(int(p[0]), int(p[1])) for p in o.ap]
        if len(pat) < 2:
            raise ValueError
        off = int(o.offset)
        p0, f0 = divmod(off, rowlen)
        s0, n0 = pat[0]
        if n0 > 1 and s0 != 0:
            if s0 % rowlen:
                raise ValueError
            pext = 1 + (s0 // rowlen) * (n0 - 1)
        else:
            pext = 1
        fext = 1
        for s_, n_ in pat[1:]:
            if s_ < 0:
                raise ValueError
            fext += s_ * (n_ - 1)
        if f0 + fext > rowlen:
            raise ValueError
        return (name, p0, p0 + pext, f0, f0 + fext)
    except Exception:
        return (name, 0, 1 << 40, 0, 1 << 40)


def _op_boxes(args, kw):
    outs, ins = [], []
    first = True
    for o in args:
        if hasattr(o, "ap") and hasattr(o, "tensor"):
            (outs if first else ins).append(_box(o))
            first = False
    for knm, o in kw.items():
        if hasattr(o, "ap") and hasattr(o, "tensor"):
            b = _box(o)
            ins.append(b)
            if knm == "accum_out":
                outs.append(b)
    return outs, ins


def _inter(a, b):
    return (a[0] == b[0] and a[1] < b[2] and b[1] < a[2]
            and a[3] < b[4] and b[3] < a[4])


def _clash(bo, bi, ro, ri):
    for x in bo:
        for y in ro:
            if _inter(x, y):
                return True
        for y in ri:
            if _inter(x, y):
                return True
    for x in bi:
        for y in ro:
            if _inter(x, y):
                return True
    return False


class _IP:

    def __init__(self, trk):
        self._t = trk

    def then_inc(self, sem, k=1):
        t = self._t
        t._wait_floor(t._chain[1])
        return t._e.nop().then_inc(sem, k)


class Trk:
    _SKIP = {"wait_ge", "nop", "engine_nop", "then_inc", "dma_start",
             "collective_compute", "register", "register64", "reg_load",
             "reg_save", "reg_alu", "reg_add", "reg_mov", "partition_id",
             "tpb_base_ld"}

    def __init__(self, eng, chain=None):
        object.__setattr__(self, "_e", eng)
        object.__setattr__(self, "last", None)
        object.__setattr__(self, "_chain", chain)
        object.__setattr__(self, "_mode", CHAIN if chain is not None else None)
        object.__setattr__(self, "_floor", 0)
        object.__setattr__(self, "_recs", [])

    def _wait_floor(self, v):
        if v > self._floor:
            self._e.wait_ge(self._chain[0], v)
            object.__setattr__(self, "_floor", v)
            rc = self._recs
            if rc:
                self._recs[:] = [r for r in rc if r[0] > v]

    def __getattr__(self, name):
        a = getattr(self._e, name)
        if callable(a) and name not in Trk._SKIP:
            def w(*args, **kw):
                ch = self._chain
                if ch is None:
                    r = a(*args, **kw)
                    if hasattr(r, "then_inc"):
                        object.__setattr__(self, "last", r)
                    return r
                if self._mode == "deps":
                    bo, bi = _op_boxes(args, kw)
                    need = self._floor
                    for seq, ro, ri in self._recs:
                        if seq > need and _clash(bo, bi, ro, ri):
                            need = seq
                    self._wait_floor(need)
                else:
                    self._wait_floor(ch[1])
                r = a(*args, **kw)
                if not hasattr(r, "then_inc"):
                    return r
                object.__setattr__(self, "last", r)
                r.then_inc(ch[0], 1)
                ch[1] += 1
                if self._mode == "deps":
                    self._recs.append((ch[1], bo, bi))
                return _IP(self)
            return w
        return a

    def wait_ge(self, sem, val):
        return self._e.wait_ge(sem, val)

    def nop(self):
        return self._e.nop()

    def mi(self, sem, k=1):
        ch = self._chain
        if ch is not None:
            self._wait_floor(ch[1])
            self._e.nop().then_inc(sem, k)
        else:
            self.last.then_inc(sem, k)



def build_program(stage=99, debug=()):
    nc = bass.Bass("TRN2", target_bir_lowering=False)

    I = lambda n, s_, d_: nc.dram_tensor(n, s_, d_, kind="ExternalInput").ap()
    x_fm = I("x_fm", [D, TOK], F32)
    w_in_t = I("w_in_t", [D, PROJ], F32)
    w_out_t = I("w_out_t", [DIN, D], F32)
    w_ffn_t = I("w_ffn_t", [D, D], F32)
    gw_t = I("gw_t", [D, 8], F32)
    ics_i = I("ics_i", [8, D], F32)
    ocs_i = I("ocs_i", [8, D], F32)
    norm_w_i = I("norm_w_i", [128, KD], F32)
    conv_w_i = I("conv_w_i", [128, NHBC * KCONV], F32)
    conv_b_i = I("conv_b_i", [128, NHBC], F32)
    dtb_i = I("dtb_i", [128, 1], F32)
    alog_i = I("alog_i", [128, 1], F32)
    gnw_i = I("gnw_i", [128, NPAIR], F32)
    dp_i = I("dp_i", [128, NPAIR], F32)
    ffnb_i = I("ffnb_i", [128, KD], F32)
    e64_i = I("e64_i", [128, DIN], BF16)
    mask01_i = I("mask01_i", [128, 128], BF16)
    ones1_i = I("ones1_i", [1, 128], BF16)
    onescol_i = I("onescol_i", [128, 1], BF16)
    onessq_i = I("onessq_i", [128, 128], BF16)
    iden8_i = I("iden8_i", [8, 8], F32)
    iden128_i = I("iden128_i", [128, 128], F32)
    masks_i = I("masks_i", [128, 1], F32)
    onehot16_i = I("onehot16_i", [16, 2048], BF16)
    idbf_i = I("idbf_i", [128, 128], BF16)
    maskb_i = I("maskb_i", [128, 1], F32)

    out_d = nc.dram_tensor("out_d", [D, TOK], F32, kind="ExternalOutput").ap()
    DR = lambda n, s_, d_: nc.dram_tensor(n, s_, d_).ap()
    hid_d = DR("hid_d", [DIN, TOK], BF16)
    gate_d = DR("gate_d", [DIN, TOK], BF16)
    hd_d = DR("hd_d", [TOK, DIN], BF16)
    prev_d = DR("prev_d", [NCH * 128, DIN], BF16)
    ahi_d = DR("ahi_d", [128, TOK], BF16)
    alo_d = DR("alo_d", [128, TOK], BF16)
    ynorm_d = DR("ynorm_d", [DIN, TOK], BF16)
    out2_d = DR("out2_d", [D, TOK], BF16)
    cc1_in = DR("cc1_in", [128, NHBC * 3], F32)
    cc1_out = DR("cc1_out", [128, NHBC * 3], F32)
    cc2_in = DR("cc2_in", [128, DIN], F32)
    cc2_out = DR("cc2_out", [128, DIN], F32)

    dbg = {}
    for name, shape, dt_ in debug:
        dbg[name] = nc.dram_tensor(name, shape, dt_, kind="ExternalOutput").ap()
    dbgctr = [0]

    def dump(pairs, sdbg):
        todo = [(nm, src) for nm, src in pairs if nm in dbg]
        if not todo:
            return
        with nc.Block("dbg%d" % dbgctr[0]) as blk:
            @blk.sync
            def _(sy):
                for nm, src in todo:
                    sy.dma_start(out=dbg[nm], in_=src).then_inc(sdbg, 16)
                    dbgctr[0] += 16
                sy.wait_ge(sdbg, dbgctr[0])

    st = ExitStack()
    with st:
        SB = lambda n, s_, d_: st.enter_context(nc.sbuf_tensor(n, s_, d_))
        SEM = lambda n: st.enter_context(nc.semaphore(n))

        p_ones1 = SB("p_ones1", [1, 128], BF16)
        p_onescol = SB("p_onescol", [128, 1], BF16)
        p_onessq = SB("p_onessq", [128, 128], BF16)
        p_mask01 = SB("p_mask01", [128, 128], BF16)
        p_masks = SB("p_masks", [128, 1], F32)
        p_maskb = SB("p_maskb", [128, 1], F32)
        p_iden8 = SB("p_iden8", [8, 8], F32)
        p_iden128 = SB("p_iden128", [128, 128], F32)
        p_ffnb = SB("p_ffnb", [128, KD], F32)
        p_eps = SB("p_eps", [128, 1], F32)
        CHV = [SEM("chain_v"), 0]
        CHS = [SEM("chain_s"), 0]
        s_cc1 = SEM("s_cc1")
        s_cc2 = SEM("s_cc2")
        s_fin = SEM("s_fin")
        s_dbg = SEM("s_dbg")

        env = dict(locals())

        with ExitStack() as ms:
            MSB = lambda n, s_, d_: ms.enter_context(nc.sbuf_tensor(n, s_, d_))
            p_norm_w = MSB("p_norm_w", [128, KD], F32)
            p_conv_w = MSB("p_conv_w", [128, NHBC * KCONV], F32)
            p_conv_b = MSB("p_conv_b", [128, NHBC], F32)
            p_dtbias = MSB("p_dtbias", [128, 1], F32)
            p_alog = MSB("p_alog", [128, 1], F32)
            p_negea = MSB("p_negea", [128, 1], F32)
            p_gnw = MSB("p_gnw", [128, NPAIR], F32)
            p_dp = MSB("p_dp", [128, NPAIR], F32)
            p_e64 = MSB("p_e64", [128, DIN], BF16)
            p_dtbf = MSB("p_dtbf", [128, TOK], BF16)
            p_dwbf = MSB("p_dwbf", [128, TOK], BF16)
            p_ubf = MSB("p_ubf", [128, TOK], BF16)
            p_nahi = MSB("p_nahi", [128, TOK], BF16)
            p_nalo = MSB("p_nalo", [128, TOK], BF16)
            p_dcol = MSB("p_dcol", [128, NCH], F32)
            p_cdcol = MSB("p_cdcol", [128, NCH], F32)
            p_bc = MSB("p_bc", [128, 16 * TOK], BF16)
            p_halo = MSB("p_halo", [128, NHBC * 3], F32)
            p_pre3 = MSB("p_pre3", [128, NHBC * 3], F32)
            p_patch = MSB("p_patch", [128, 192], BF16)
            p_onehot = MSB("p_onehot", [16, 2048], BF16)
            p_idbf = MSB("p_idbf", [128, 128], BF16)
            env.update(locals())

            _p0_loads(nc, env)
            with ExitStack() as hs:
                p_h = hs.enter_context(nc.sbuf_tensor("p_h", [128, KD * TOK], BF16))
                env["p_h"] = p_h
                _p1_rms(nc, env)
                dump([("dbg_h", p_h[:, 0:4 * TOK])], s_dbg)
                if stage <= 1:
                    _finish_dummy(nc, out_d, s_fin)
                    return nc
                _p2_dt(nc, env)
                dump([("dbg_ubf", p_ubf[:, :]), ("dbg_dw", p_dwbf[:, :])], s_dbg)
                if stage <= 2:
                    _finish_dummy(nc, out_d, s_fin)
                    return nc
                _p3_inproj(nc, env, hbc=True)
                dump([("dbg_bc", p_bc[:, 0:2 * TOK]), ("dbg_halo", p_halo[:, :])],
                     s_dbg)
                if stage <= 3:
                    _finish_dummy(nc, out_d, s_fin)
                    return nc
                _p3_inproj(nc, env, hbc=False)
            env.pop("p_h")
            if stage == 35:
                _finish_dummy(nc, out_d, s_fin)
                return nc

            _p4_sweep1(nc, env)
            dump([("dbg_patch", p_patch[:, :]), ("dbg_cd", p_cdcol[:, :]),
                  ("dbg_prev7", prev_d[7 * 128:8 * 128, :]),
                  ("dbg_hd0", hd_d[0:128, :]),
                  ("dbg_ccst", cc2_in)], s_dbg)
            if stage <= 4:
                _finish_dummy(nc, out_d, s_fin)
                return nc

            _p6_sweep2(nc, env)
            dump([("dbg_yn", ynorm_d.rearrange("(c p) t -> p c t", p=128)
                   [:, :, 7 * 128:8 * 128])], s_dbg)
            if stage <= 6:
                _finish_dummy(nc, out_d, s_fin)
                return nc

        p_x1 = SB("p_x1", [128, KD * TOK], BF16)
        env["p_x1"] = p_x1
        _p7_outproj(nc, env)
        dump([("dbg_x1", p_x1[:, 0:4 * TOK])], s_dbg)
        if stage <= 7:
            _finish_dummy(nc, out_d, s_fin)
            return nc

        _p8_ffn(nc, env)
    return nc


def _p0_loads(nc, env):
    g_ = env
    with nc.Block("p0") as blk:
        s0 = g_["SEM"]("p0_s")
        @blk.sync
        def _(sy):
            loads = [
                (g_["p_norm_w"], g_["norm_w_i"]), (g_["p_conv_w"], g_["conv_w_i"]),
                (g_["p_conv_b"], g_["conv_b_i"]), (g_["p_dtbias"], g_["dtb_i"]),
                (g_["p_alog"], g_["alog_i"]), (g_["p_masks"], g_["masks_i"]),
                (g_["p_maskb"], g_["maskb_i"]), (g_["p_mask01"], g_["mask01_i"]),
                (g_["p_ones1"], g_["ones1_i"]), (g_["p_onescol"], g_["onescol_i"]),
                (g_["p_onessq"], g_["onessq_i"]), (g_["p_iden8"], g_["iden8_i"]),
                (g_["p_iden128"], g_["iden128_i"]), (g_["p_gnw"], g_["gnw_i"]),
                (g_["p_dp"], g_["dp_i"]), (g_["p_ffnb"], g_["ffnb_i"]),
                (g_["p_onehot"], g_["onehot16_i"]),
                (g_["p_idbf"], g_["idbf_i"]),
            ]
            for dst, src in loads:
                sy.dma_start(out=dst[:, :], in_=src).then_inc(s0, 16)
            sy.wait_ge(s0, 16 * len(loads))

        @blk.gpsimd
        def _(g):
            s0g = g_["SEM"]("p0_sg")
            g.dma_start(out=g_["p_e64"][:, :], in_=g_["e64_i"]).then_inc(s0g, 16)
            g.wait_ge(s0g, 16)
    with nc.Block("p0b") as blk:
        @blk.scalar
        def _(sc):
            sc = Trk(sc, g_["CHS"] if CHAIN else None)
            sc.activation(g_["p_negea"][:, :], g_["p_alog"][:, :], AF.Exp)
    with nc.Block("p0c") as blk:
        @blk.vector
        def _(v):
            v = Trk(v, g_["CHV"] if CHAIN else None)
            v.tensor_scalar(g_["p_negea"][:, :], g_["p_negea"][:, :], -1.0,
                            None, ALU.mult)
            v.memset(g_["p_eps"][:, :], EPS)


def _p1_rms(nc, env):
    g_ = env
    p_h, p_onescol, p_ones1, p_norm_w = g_["p_h"], g_["p_onescol"], g_["p_ones1"], g_["p_norm_w"]
    x_fm = g_["x_fm"]
    with nc.Block("p1") as blk, ExitStack() as ps:
        E = ps.enter_context
        xa = E(nc.sbuf_tensor("p1_xa", [128, TOK], F32))
        xb = E(nc.sbuf_tensor("p1_xb", [128, TOK], F32))
        x2 = E(nc.sbuf_tensor("p1_x2", [128, TOK], BF16))
        rrow = E(nc.sbuf_tensor("p1_rrow", [1, TOK], F32))
        rrbf = E(nc.sbuf_tensor("p1_rrbf", [1, TOK], BF16))
        ps_ss = E(nc.psum_tensor("p1_psss", [1, TOK], F32))
        ps_rb = E(nc.psum_tensor("p1_psrb", [128, TOK], F32))
        sx2 = [g_["SEM"]("p1_sx0"), g_["SEM"]("p1_sx1")]
        sq = g_["SEM"]("p1_sq")
        sm = g_["SEM"]("p1_sm")
        sv = g_["SEM"]("p1_sv")
        xt = [xa, xb]

        @blk.sync
        def _(sy):
            for k in range(KD):
                if k >= 2:
                    sy.wait_ge(sq, k - 1)
                sy.dma_start(out=xt[k % 2][:, :],
                             in_=x_fm[k * 128:(k + 1) * 128, :]).then_inc(sx2[k % 2], 16)
            for k in range(KD):
                if k < 2:
                    sy.wait_ge(sq, KD)
                else:
                    sy.wait_ge(sv, k - 1)
                sy.dma_start(out=xt[k % 2][:, :],
                             in_=x_fm[k * 128:(k + 1) * 128, :]).then_inc(sx2[k % 2], 16)

        @blk.scalar
        def _(sc):
            sc = Trk(sc, g_["CHS"] if CHAIN else None)
            for k in range(KD):
                sc.wait_ge(sx2[k % 2], 16 * (k // 2 + 1))
                if k > 0:
                    sc.wait_ge(sm, k)
                sc.activation(x2[:, :], xt[k % 2][:, :], AF.Square).then_inc(sq, 1)
            sc.wait_ge(sm, KD)
            sc.activation(rrow[:, :], ps_ss[:, :], AF.Ln, scale=1.0 / D,
                          bias=g_["p_eps"][0:1, :]).then_inc(sq, 1)
            sc.wait_ge(sq, KD + 1)
            sc.activation(rrow[:, :], rrow[:, :], AF.Exp, scale=-0.5).then_inc(sq, 1)

        @blk.tensor
        def _(t):
            t = Trk(t)
            for k in range(KD):
                t.wait_ge(sq, k + 1)
                t.matmul(ps_ss[0:1, 0:512], p_onescol[:, :], x2[:, 0:512],
                         start=(k == 0), stop=(k == KD - 1))
                t.matmul(ps_ss[0:1, 512:1024], p_onescol[:, :], x2[:, 512:1024],
                         start=(k == 0), stop=(k == KD - 1)).then_inc(sm, 1)
            t.wait_ge(sq, KD + 3)
            t.matmul(ps_rb[:, 0:512], p_ones1[:, :], rrbf[:, 0:512])
            t.matmul(ps_rb[:, 512:1024], p_ones1[:, :], rrbf[:, 512:1024]
                     ).then_inc(sm, 1)

        @blk.vector
        def _(v):
            v = Trk(v, g_["CHV"] if CHAIN else None)
            v.wait_ge(sq, KD + 2)
            v.tensor_copy(rrbf[:, :], rrow[:, :]).then_inc(sq, 1)
            v.wait_ge(sm, KD + 1)
            for k in range(KD):
                v.wait_ge(sx2[k % 2], 16 * (KD // 2 + k // 2 + 1))
                v.scalar_tensor_tensor(p_h[:, k * TOK:(k + 1) * TOK],
                                       xt[k % 2][:, :], p_norm_w[:, k:k + 1],
                                       ps_rb[:, :], ALU.mult, ALU.mult
                                       ).then_inc(sv, 1)


def _p2_dt(nc, env):
    g_ = env
    p_h = g_["p_h"]
    with nc.Block("p2") as blk, ExitStack() as ps:
        E = ps.enter_context
        wdt = E(nc.sbuf_tensor("p2_wdt", [128, KD * 128], BF16))
        ps_dt = E(nc.psum_tensor("p2_psdt", [128, TOK], F32))
        t_abs = E(nc.sbuf_tensor("p2_abs", [128, TOK], F32))
        t_e = E(nc.sbuf_tensor("p2_e", [128, TOK], F32))
        t_r = E(nc.sbuf_tensor("p2_r", [128, TOK], F32))
        t_a = E(nc.sbuf_tensor("p2_a", [128, TOK], F32))
        t_w = E(nc.sbuf_tensor("p2_w", [128, TOK], F32))
        t_hi = E(nc.sbuf_tensor("p2_hi", [128, TOK], BF16))
        p_dtv = E(nc.sbuf_tensor("p2_dtv", [128, TOK], F32))
        p_acs = E(nc.sbuf_tensor("p2_acs", [128, TOK], F32))
        t_lo = E(nc.sbuf_tensor("p2_lo", [128, TOK], BF16))
        sw = g_["SEM"]("p2_sw")
        sm = g_["SEM"]("p2_sm")
        sa = g_["SEM"]("p2_sa")
        sv = g_["SEM"]("p2_sv")
        ss = g_["SEM"]("p2_ss")

        @blk.gpsimd
        def _(g):
            g.dma_start(out=wdt.ap().rearrange("p (k c) -> p k c", k=KD),
                        in_=g_["w_in_t"][:, DT0:DT0 + 128].rearrange(
                            "(k p) c -> p k c", p=128)).then_inc(sw, 16)

        @blk.tensor
        def _(t):
            t = Trk(t)
            t.wait_ge(sw, 16)
            for k in range(KD):
                t.matmul(ps_dt[:, 0:512], wdt[:, k * 128:(k + 1) * 128],
                         p_h[:, k * TOK:k * TOK + 512],
                         start=(k == 0), stop=(k == KD - 1))
                t.matmul(ps_dt[:, 512:1024], wdt[:, k * 128:(k + 1) * 128],
                         p_h[:, k * TOK + 512:(k + 1) * TOK],
                         start=(k == 0), stop=(k == KD - 1))
            t.mi(sm, 1)

        @blk.scalar
        def _(sc):
            sc = Trk(sc, g_["CHS"] if CHAIN else None)
            sch = g_["SEM"]("p2_sch")
            sc.wait_ge(sm, 1)
            sc.activation(t_abs[:, :], ps_dt[:, :], AF.Abs,
                          bias=g_["p_dtbias"][:, :]).then_inc(sch, 1)
            sc.wait_ge(sch, 1)
            sc.activation(t_e[:, :], t_abs[:, :], AF.Exp, scale=-1.0).then_inc(sch, 1)
            sc.wait_ge(sch, 2)
            sc.activation(t_e[:, :], t_e[:, :], AF.Ln, bias=1.0)
            sc.activation(t_r[:, :], ps_dt[:, :], AF.Relu,
                          bias=g_["p_dtbias"][:, :]).then_inc(sa, 1)
            for k in range(NCH):
                sc.wait_ge(sv, 3 + k)
                ck = slice(k * 128, (k + 1) * 128)
                last = k * 128 + 127
                sc.activation(t_w[:, ck], p_acs[:, ck], AF.Exp, scale=-1.0,
                              bias=p_acs[:, last:last + 1])
                sc.activation(g_["p_ubf"][:, ck], p_acs[:, ck], AF.Exp)
                sc.activation(g_["p_dcol"][:, k:k + 1],
                              p_acs[:, last:last + 1], AF.Exp).then_inc(sa, 1)

        @blk.vector
        def _(v):
            v = Trk(v, g_["CHV"] if CHAIN else None)
            v.wait_ge(sa, 1)
            v.tensor_add(p_dtv[:, :], t_r[:, :], t_e[:, :]).then_inc(sv, 1)
            v.tensor_scalar(t_a[:, :], p_dtv[:, :], g_["p_negea"][:, :],
                            None, ALU.mult).then_inc(sv, 1)
            for k in range(NCH):
                ck = slice(k * 128, (k + 1) * 128)
                v.tensor_tensor_scan(p_acs[:, ck], t_a[:, ck], t_a[:, ck],
                                     0.0, ALU.add, ALU.bypass).then_inc(sv, 1)
            v.tensor_copy(g_["p_dtbf"][:, :], p_dtv[:, :])
            v.tensor_copy(t_hi[:, :], p_acs[:, :])
            v.tensor_tensor(t_lo[:, :], p_acs[:, :], t_hi[:, :], ALU.subtract)
            v.tensor_scalar(g_["p_nahi"][:, :], t_hi[:, :], -1.0, None, ALU.mult)
            v.tensor_scalar(g_["p_nalo"][:, :], t_lo[:, :], -1.0, None, ALU.mult)
            v.wait_ge(sa, 1 + NCH)
            v.tensor_tensor(g_["p_dwbf"][:, :], p_dtv[:, :], t_w[:, :],
                            ALU.mult)
            v.memset(g_["p_cdcol"][:, 0:1], 1.0)
            for k in range(1, NCH):
                v.tensor_tensor(g_["p_cdcol"][:, k:k + 1],
                                g_["p_cdcol"][:, k - 1:k],
                                g_["p_dcol"][:, k - 1:k], ALU.mult)
            v.mi(sv, 1)

        @blk.sync
        def _(sy):
            sy.wait_ge(sv, 3 + NCH)
            sy.dma_start(out=g_["ahi_d"], in_=t_hi[:, :]).then_inc(ss, 16)
            sy.dma_start(out=g_["alo_d"], in_=t_lo[:, :]).then_inc(ss, 16)
            sy.wait_ge(ss, 32)


def _p3_inproj(nc, env, hbc):
    g_ = env
    p_h = g_["p_h"]
    tag = "p3" if hbc else "p5"
    NB = NHBC if hbc else NGATE
    col0 = HBC0 if hbc else 0
    with nc.Block(tag) as blk, ExitStack() as ps:
        E = ps.enter_context
        wb = [E(nc.sbuf_tensor(f"{tag}_w{i}", [128, KD * 128], BF16)) for i in range(3)]
        pp2 = [E(nc.psum_tensor(f"{tag}_ps{i}", [128, TOK], F32)) for i in range(2)]
        stg = [E(nc.sbuf_tensor(f"{tag}_st{i}", [128, TOK], BF16)) for i in range(2)]
        hbc_f = E(nc.sbuf_tensor(f"{tag}_hf", [128, TOK], F32)) if hbc else None
        cbuf = E(nc.sbuf_tensor(f"{tag}_cb", [128, TOK], F32)) if hbc else None
        sw3 = [g_["SEM"](f"{tag}_sw{i}") for i in range(3)]
        smm = g_["SEM"](f"{tag}_smm")
        sev = g_["SEM"](f"{tag}_sev")
        scv = g_["SEM"](f"{tag}_scv")
        ssi = g_["SEM"](f"{tag}_ssi")
        sst2 = [g_["SEM"](f"{tag}_sst0"), g_["SEM"](f"{tag}_sst1")]
        shm = g_["SEM"](f"{tag}_shm")

        @blk.gpsimd
        def _(g):
            for cc in range(NB):
                if cc >= 3:
                    g.wait_ge(smm, cc - 2)
                c0 = col0 + cc * 128
                g.dma_start(out=wb[cc % 3].ap().rearrange("p (k c) -> p k c", k=KD),
                            in_=g_["w_in_t"][:, c0:c0 + 128].rearrange(
                                "(k p) c -> p k c", p=128)).then_inc(sw3[cc % 3], 16)
            if hbc:
                g.wait_ge(shm, 16)
                g.collective_compute("AllReduce", ALU.add, replica_groups=RG,
                                     ins=[g_["cc1_in"].opt()],
                                     outs=[g_["cc1_out"].opt()]
                                     ).then_inc(g_["s_cc1"], 1)

        @blk.tensor
        def _(t):
            t = Trk(t)
            for cc in range(NB):
                t.wait_ge(sw3[cc % 3], 16 * (cc // 3 + 1))
                if cc >= 2:
                    t.wait_ge(sev if hbc else ssi, cc - 1)
                pp = pp2[cc % 2]
                w = wb[cc % 3]
                for k in range(KD):
                    t.matmul(pp[:, 0:512], w[:, k * 128:(k + 1) * 128],
                             p_h[:, k * TOK:k * TOK + 512],
                             start=(k == 0), stop=(k == KD - 1))
                    t.matmul(pp[:, 512:1024], w[:, k * 128:(k + 1) * 128],
                             p_h[:, k * TOK + 512:(k + 1) * TOK],
                             start=(k == 0), stop=(k == KD - 1))
                t.mi(smm, 1)

        @blk.vector
        def _(v):
            v = Trk(v, g_["CHV"] if CHAIN else None)
            if hbc:
                v.memset(stg[0][:, 0:3], 0.0)
                v.memset(stg[1][:, 0:3], 0.0)
                v.memset(g_["p_bc"][:, :], 0.0)
            for cc in range(NB):
                v.wait_ge(smm, cc + 1)
                pp = pp2[cc % 2]
                if hbc:
                    v.tensor_copy(hbc_f[:, 0:512], pp[:, 0:512])
                    v.tensor_copy(hbc_f[:, 512:1024], pp[:, 512:1024]
                                  ).then_inc(sev, 1)
                    v.tensor_copy(g_["p_halo"][:, cc * 3:cc * 3 + 3],
                                  hbc_f[:, 1021:1024])
                    v.tensor_copy(g_["p_pre3"][:, cc * 3:cc * 3 + 3],
                                  hbc_f[:, 0:3])
                    if cc >= 1:
                        v.wait_ge(ssi, cc)
                    cw = g_["p_conv_w"]
                    v.tensor_scalar(cbuf[:, 3:1024], hbc_f[:, 0:1021],
                                    cw[:, cc * 4:cc * 4 + 1],
                                    g_["p_conv_b"][:, cc:cc + 1],
                                    ALU.mult, ALU.add)
                    for i in (1, 2, 3):
                        v.scalar_tensor_tensor(cbuf[:, 3:1024],
                                               hbc_f[:, i:i + 1021],
                                               cw[:, cc * 4 + i:cc * 4 + i + 1],
                                               cbuf[:, 3:1024], ALU.mult, ALU.add)
                    v.mi(scv, 1)
                else:
                    if cc >= 2:
                        v.wait_ge(sst2[cc % 2], 16 * (cc // 2))
                    v.tensor_copy(stg[cc % 2][:, 0:512], pp[:, 0:512])
                    v.tensor_copy(stg[cc % 2][:, 512:1024], pp[:, 512:1024]
                                  ).then_inc(ssi, 1)
            if hbc:
                v.wait_ge(scv, NB)
                v.tensor_scalar(g_["p_halo"][:, :], g_["p_halo"][:, :],
                                g_["p_masks"][:, :], None, ALU.mult
                                ).then_inc(scv, 1)

        @blk.scalar
        def _(sc):
            sc = Trk(sc, g_["CHS"] if CHAIN else None)
            if not hbc:
                return
            for cc in range(NB):
                sc.wait_ge(scv, cc + 1)
                if cc < 64:
                    if cc >= 2:
                        sc.wait_ge(sst2[cc % 2], 16 * (cc // 2))
                    sc.activation(stg[cc % 2][:, 3:1024], cbuf[:, 3:1024],
                                  AF.Silu).then_inc(ssi, 1)
                else:
                    b = cc - 64
                    sc.activation(g_["p_bc"][:, b * TOK + 3:(b + 1) * TOK],
                                  cbuf[:, 3:1024], AF.Silu).then_inc(ssi, 1)

        @blk.sync
        def _(sy):
            for cc in range(NB):
                sy.wait_ge(ssi, cc + 1)
                if hbc:
                    if cc < 64:
                        sy.dma_start(out=g_["hid_d"][cc * 128:(cc + 1) * 128, :],
                                     in_=stg[cc % 2][:, :]
                                     ).then_inc(sst2[cc % 2], 16)
                    else:
                        sy.nop().then_inc(sst2[cc % 2], 16)
                else:
                    sy.dma_start(out=g_["gate_d"][cc * 128:(cc + 1) * 128, :],
                                 in_=stg[cc % 2][:, :]).then_inc(sst2[cc % 2], 16)
            sy.wait_ge(sst2[0], 16 * ((NB + 1) // 2))
            sy.wait_ge(sst2[1], 16 * (NB // 2))
            if hbc:
                sy.wait_ge(scv, NB + 1)
                sy.dma_start(out=g_["cc1_in"], in_=g_["p_halo"][:, :]
                             ).then_inc(shm, 16)


def _p4_sweep1(nc, env):
    g_ = env

    def sev_e64d(k, gi):
        return 56 * k + 7 * gi + 1

    def sev_hd(k, gi, hf):
        return 56 * k + 7 * gi + 2 + 3 * hf

    def sev_hdw(k, gi, hf):
        return 56 * k + 7 * gi + 3 + 3 * hf

    def sev_rec(k, gi, hf):
        return 56 * k + 7 * gi + 4 + 3 * hf

    def mm_A(k, gi, hf):
        return 64 * k + 8 * gi + 4 * hf + 1

    def mm_B(k, gi, hf):
        return 64 * k + 8 * gi + 4 * hf + 2

    def mm_S(k, gi, hf):
        return 64 * k + 8 * gi + 4 * hf + 3

    def mm_D(k, gi, hf):
        return 64 * k + 8 * gi + 4 * hf + 4

    def unq2(q2):
        q, hf = q2 // 2, q2 % 2
        return q // 8, q % 8, hf

    with nc.Block("p4") as blk, ExitStack() as ps:
        E = ps.enter_context
        T0 = [E(nc.sbuf_tensor(f"s1_t0{i}", [128, DIN], BF16)) for i in range(2)]
        T1 = [E(nc.sbuf_tensor(f"s1_t1{i}", [128, DIN], BF16)) for i in range(2)]
        T2 = E(nc.sbuf_tensor("s1_t2", [128, DIN], BF16))
        T3 = [E(nc.sbuf_tensor(f"s1_t3{i}", [128, 512], BF16)) for i in range(2)]
        P = E(nc.sbuf_tensor("s1_P", [128, DIN], F32))
        btok = [E(nc.sbuf_tensor(f"s1_btok{i}", [128, G * 128], BF16)) for i in range(2)]
        e64d = [E(nc.sbuf_tensor(f"s1_e64d{i}", [128, 1024], BF16)) for i in range(2)]
        tsb = E(nc.sbuf_tensor("s1_tsb", [128, 512], F32))
        pfx = E(nc.sbuf_tensor("s1_pfx", [128, NHBC * 3], F32))
        cat6 = E(nc.sbuf_tensor("s1_cat6", [128, NHBC * 6], F32))
        tp6t = [E(nc.sbuf_tensor(f"s1_tp{i}", [128, NHBC * 3], F32))
                for i in range(2)]
        tp6 = [t_.ap().rearrange("p (c s) -> p c s", s=3) for t_ in tp6t]
        p3t = E(nc.sbuf_tensor("s1_p3t", [128, NHBC * 3], F32))
        psA = [E(nc.psum_tensor(f"s1_psA{i}", [128, 512], F32)) for i in range(2)]
        psB = [E(nc.psum_tensor(f"s1_psB{i}", [128, 512], F32)) for i in range(2)]
        psS = [E(nc.psum_tensor(f"s1_psS{i}", [128, 512], F32)) for i in range(2)]
        psD = [E(nc.psum_tensor(f"s1_psD{i}", [128, 512], F32)) for i in range(2)]
        sld = g_["SEM"]("s1_sld")
        spfx = g_["SEM"]("s1_spfx")
        stpy = g_["SEM"]("s1_stpy")
        stpb = g_["SEM"]("s1_stpb")
        smm = g_["SEM"]("s1_smm")
        sev = g_["SEM"]("s1_sev")
        spl = g_["SEM"]("s1_spl")
        shd = g_["SEM"]("s1_shd")
        spt = g_["SEM"]("s1_spt")
        spz = g_["SEM"]("s1_spz")
        sc2 = g_["SEM"]("s1_sc2")

        @blk.sync
        def _(sy):
            sy.wait_ge(g_["s_cc1"], 1)
            sy.dma_start(out=pfx[:, :], in_=g_["cc1_out"]).then_inc(spfx, 16)
            sy.wait_ge(spt, 3)
            sy.dma_start(
                out=g_["hid_d"].rearrange("(c p) t -> p c t", p=128)[:, :, 0:3],
                in_=g_["p_patch"].ap().rearrange("p (c l) -> p c l", c=64)
            ).then_inc(spt, 16)
            for k in range(NCH):
                if k == 0:
                    sy.wait_ge(spt, 19)
                if k >= 2:
                    sy.wait_ge(stpy, 16 * (k - 1))
                sy.dma_start(
                    out=T0[k % 2].ap().rearrange("p (c l) -> p c l", c=64),
                    in_=g_["hid_d"].rearrange("(c p) t -> p c t", p=128)
                    [:, :, k * 128:(k + 1) * 128]).then_inc(sld, 16)
                if k >= 2:
                    sy.wait_ge(sev, 56 * (k - 1))
                sy.wait_ge(sld, 16 * (k + 1))
                sy.dma_start(
                    out=T1[k % 2].ap().rearrange("p (c w) -> p c w", w=128),
                    in_=T0[k % 2].ap().rearrange("p (c w) -> p c w", w=128),
                    transpose=True).then_inc(stpy, 16)
                if k >= 2:
                    sy.wait_ge(smm, 64 * (k - 1))
                for gi in range(G):
                    sy.dma_start(
                        out=btok[k % 2][:, gi * 128:(gi + 1) * 128],
                        in_=g_["p_bc"][:, gi * TOK + k * 128:gi * TOK + (k + 1) * 128],
                        transpose=True).then_inc(stpb, 16)
            sy.wait_ge(shd, 16 * NCH)
            sy.wait_ge(sev, 56 * NCH + 1)
            sy.dma_start(out=g_["cc2_in"], in_=P[:, :]).then_inc(sc2, 16)

        @blk.scalar
        def _(sc):
            sc = Trk(sc, g_["CHS"] if CHAIN else None)
            sc.wait_ge(spt, 2)
            sc.activation(g_["p_patch"][:, :], p3t[:, 0:192], AF.Silu)
            for bcc in range(16):
                sc.activation(g_["p_bc"][:, bcc * TOK:bcc * TOK + 3],
                              p3t[:, (64 + bcc) * 3:(64 + bcc) * 3 + 3], AF.Silu)
            sc.mi(spt, 1)


        @blk.vector
        def _(v):
            v = Trk(v, g_["CHV"] if CHAIN else None)
            v.memset(P[:, :], 0.0).then_inc(spz, 1)
            v.wait_ge(spfx, 16)
            v.tensor_scalar(pfx[:, :], pfx[:, :], g_["p_maskb"][:, :], None,
                            ALU.mult).then_inc(spt, 1)
            c6v = cat6.ap().rearrange("p (c s) -> p c s", s=6)
            v.tensor_copy(c6v[:, :, 0:3],
                          pfx[:, :].rearrange("p (c s) -> p c s", s=3))
            v.tensor_copy(c6v[:, :, 3:6],
                          g_["p_pre3"][:, :].rearrange("p (c s) -> p c s", s=3))
            wv = g_["p_conv_w"][:, :].rearrange("p (c t) -> p c t", t=4)
            p3v = p3t[:, :].rearrange("p (c s) -> p c s", s=3)
            v.tensor_tensor(p3v, c6v[:, :, 0:3],
                            wv[:, :, 0:1].broadcast_to([128, NHBC, 3]),
                            ALU.mult)
            for i in (1, 2, 3):
                v.tensor_tensor(tp6[i % 2], c6v[:, :, i:i + 3],
                                wv[:, :, i:i + 1].broadcast_to([128, NHBC, 3]),
                                ALU.mult)
                v.tensor_tensor(p3v, p3v, tp6[i % 2], ALU.add)
            v.tensor_tensor(p3v, p3v,
                            g_["p_conv_b"][:, :, None].broadcast_to(
                                [128, NHBC, 3]), ALU.add)
            v.mi(spt, 1)
            for k in range(NCH):
                v.wait_ge(stpy, 16 * (k + 1))
                v.wait_ge(shd, 16 * k)
                for gi in range(G):
                    q = 8 * k + gi
                    if q >= 2:
                        pk, pgi = (q - 2) // 8, (q - 2) % 8
                        v.wait_ge(smm, mm_D(pk, pgi, 1))
                    v.tensor_scalar(e64d[q % 2][:, :],
                                    g_["p_e64"][:, gi * 1024:(gi + 1) * 1024],
                                    g_["p_dcol"][:, k:k + 1], None, ALU.mult
                                    ).then_inc(sev, 1)
                    for hf in range(2):
                        q2 = 2 * q + hf
                        sl = slice(gi * 1024 + hf * 512,
                                   gi * 1024 + (hf + 1) * 512)
                        v.wait_ge(smm, mm_A(k, gi, hf))
                        v.tensor_tensor(T2[:, sl], T1[k % 2][:, sl],
                                        psA[q2 % 2][:, :], ALU.mult
                                        ).then_inc(sev, 1)
                        v.wait_ge(smm, mm_B(k, gi, hf))
                        if q2 >= 2:
                            v.wait_ge(smm, mm_S(*unq2(q2 - 2)))
                        v.tensor_tensor(T3[q2 % 2][:, :], T1[k % 2][:, sl],
                                        psB[q2 % 2][:, :], ALU.mult
                                        ).then_inc(sev, 1)
                        v.wait_ge(smm, mm_D(k, gi, hf))
                        if gi == 0 and hf == 0:
                            v.wait_ge(spl, 16 * (k + 1))
                        v.tensor_tensor(tsb[:, :], psD[q2 % 2][:, :],
                                        P[:, sl], ALU.mult)
                        v.tensor_tensor(P[:, sl], tsb[:, :], psS[q2 % 2][:, :],
                                        ALU.add).then_inc(sev, 1)
            v.tensor_scalar(P[:, :], P[:, :], g_["p_masks"][:, :], None,
                            ALU.mult).then_inc(sev, 1)

        @blk.tensor
        def _(t):
            t = Trk(t)
            for k in range(NCH):
                for gi in range(G):
                    q = 8 * k + gi
                    if gi == 0:
                        t.wait_ge(stpy, 16 * (k + 1))
                        t.wait_ge(stpb, 16 * 8 * (k + 1))
                    t.wait_ge(sev, sev_e64d(k, gi))
                    for hf in range(2):
                        q2 = 2 * q + hf
                        sl = slice(gi * 1024 + hf * 512,
                                   gi * 1024 + (hf + 1) * 512)
                        if q2 >= 2:
                            t.wait_ge(sev, sev_hd(*unq2(q2 - 2)))
                        dtfm = g_["p_dtbf"][:, k * 128:(k + 1) * 128]
                        t.matmul(psA[q2 % 2][:, :], dtfm, g_["p_e64"][:, sl]
                                 ).then_inc(smm, 1)
                        if q2 >= 2:
                            t.wait_ge(sev, sev_hdw(*unq2(q2 - 2)))
                        dwfm = g_["p_dwbf"][:, k * 128:(k + 1) * 128]
                        t.matmul(psB[q2 % 2][:, :], dwfm, g_["p_e64"][:, sl]
                                 ).then_inc(smm, 1)
                        if q2 >= 2:
                            t.wait_ge(sev, sev_rec(*unq2(q2 - 2)))
                        t.wait_ge(sev, sev_hdw(k, gi, hf))
                        bt = btok[k % 2][:, gi * 128:(gi + 1) * 128]
                        t.matmul(psS[q2 % 2][:, :], bt, T3[q2 % 2][:, :]
                                 ).then_inc(smm, 1)
                        t.matmul(psD[q2 % 2][:, :], g_["p_onessq"][:, :],
                                 e64d[q % 2][:, hf * 512:(hf + 1) * 512]
                                 ).then_inc(smm, 1)

        @blk.gpsimd
        def _(g):
            for k in range(NCH):
                if k == 0:
                    g.wait_ge(spz, 1)
                else:
                    g.wait_ge(sev, 56 * k)
                g.dma_start(out=g_["prev_d"][k * 128:(k + 1) * 128, :],
                            in_=P[:, :]).then_inc(spl, 16)
                g.wait_ge(sev, sev_hd(k, G - 1, 1))
                g.dma_start(out=g_["hd_d"][k * 128:(k + 1) * 128, :],
                            in_=T2[:, :]).then_inc(shd, 16)
            g.wait_ge(sc2, 16)
            g.collective_compute("AllReduce", ALU.add, replica_groups=RG,
                                 ins=[g_["cc2_in"].opt()],
                                 outs=[g_["cc2_out"].opt()]).then_inc(g_["s_cc2"], 1)


def _p6_sweep2(nc, env):
    g_ = env

    mmv = {}
    n = 0

    def ev(tag):
        nonlocal_ns[0] += 1
        mmv[tag] = nonlocal_ns[0]
    nonlocal_ns = [0]
    ev(("cdexp",))
    for k in range(NCH):
        for hg in range(16):
            gi, hh = hg // 2, hg % 2
            if hg == 0:
                ev(("cb", k, 0))
                ev(("st", k, 0))
            if hg < 15:
                if hh == 1:
                    ev(("cb", k, gi + 1))
                ev(("st", k, hg + 1))
            ev(("u", k, hg))
            ev(("y", k, hg))
        ev(("ep1", k))
        ev(("ep2", k))

    def ev_pt(k):
        return 44 * k + 2

    def ev_mtm(k, gi):
        return 44 * k + 3 + 5 * gi

    def ev_mtl(k, hg):
        return 44 * k + 4 + 5 * (hg // 2) + 2 * (hg % 2)

    def ev_ab(k, hg):
        return 44 * k + 5 + 5 * (hg // 2) + 2 * (hg % 2)

    def ev_dres(k):
        return 44 * k + 43

    def ev_y3(k):
        return 44 * k + 44

    def ev_yn(k):
        return 44 * k + 45

    def ac_le(k, hg):
        return 2 + 19 * k + hg

    def ac_silu(k):
        return 1 + 19 * k + 17

    def ac_ysq(k):
        return 1 + 19 * k + 18

    def ac_rows(k):
        return 1 + 19 * k + 19

    with nc.Block("p6") as blk, ExitStack() as ps:
        E = ps.enter_context
        T0 = E(nc.sbuf_tensor("s2_t0", [128, DIN], BF16))
        T1 = E(nc.sbuf_tensor("s2_t1", [128, DIN], BF16))
        Tg = E(nc.sbuf_tensor("s2_tg", [128, DIN], BF16))
        T5 = E(nc.sbuf_tensor("s2_t5", [128, DIN], BF16))
        YA = E(nc.sbuf_tensor("s2_ya", [128, DIN], F32))
        init = E(nc.sbuf_tensor("s2_init", [128, DIN], BF16))
        afhi = E(nc.sbuf_tensor("s2_afhi", [16, 1024], BF16))
        aflo = E(nc.sbuf_tensor("s2_aflo", [16, 1024], BF16))
        cdbf = E(nc.sbuf_tensor("s2_cdbf", [128, NCH], BF16))
        cdbf2 = E(nc.sbuf_tensor("s2_cdbf2", [128, 512], BF16))
        mtm = E(nc.sbuf_tensor("s2_mtm", [128, 128], BF16))
        rlf = E(nc.sbuf_tensor("s2_rlf", [128, 1024], F32))
        exps = [E(nc.sbuf_tensor(f"s2_exps{i}", [128, 1024], BF16)) for i in range(2)]
        mtl = [E(nc.sbuf_tensor(f"s2_mtl{i}", [128, 1024], BF16)) for i in range(2)]
        tpb = E(nc.sbuf_tensor("s2_tpb", [128, 1024], BF16))
        uexp = E(nc.sbuf_tensor("s2_uexp", [128, 512], BF16))
        tmp5 = E(nc.sbuf_tensor("s2_tmp5", [128, 512], BF16))
        rrow = E(nc.sbuf_tensor("s2_rrow", [1, 1024], F32))
        rrbf = E(nc.sbuf_tensor("s2_rrbf", [1, 1024], BF16))
        psS = [E(nc.psum_tensor(f"s2_pss{i}", [128, 1024], F32)) for i in range(2)]
        psMT = E(nc.psum_tensor("s2_psmt", [128, 128], F32))
        psU = E(nc.psum_tensor("s2_psu", [128, 512], F32))
        psYd = E(nc.psum_tensor("s2_psyd", [128, 512], F32))
        psYo = E(nc.psum_tensor("s2_psyo", [128, 512], F32))
        sld = g_["SEM"]("s2_sld")
        smm = g_["SEM"]("s2_smm")
        sev = g_["SEM"]("s2_sev")
        sac = g_["SEM"]("s2_sac")
        sst = g_["SEM"]("s2_sst")
        sini = g_["SEM"]("s2_sini")

        @blk.gpsimd
        def _(g):
            g.wait_ge(g_["s_cc2"], 1)
            g.dma_start(out=init[:, :], in_=g_["cc2_out"]).then_inc(sini, 16)

        @blk.sync
        def _(sy):
            for k in range(NCH):
                if k > 0:
                    sy.wait_ge(sev, ev_dres(k - 1))
                sy.dma_start(
                    out=T0.ap().rearrange("p (c l) -> p c l", c=64),
                    in_=g_["hid_d"].rearrange("(c p) t -> p c t", p=128)
                    [:, :, k * 128:(k + 1) * 128]).then_inc(sld, 16)
                if k > 0:
                    sy.wait_ge(smm, mmv[("y", k - 1, 15)])
                    sy.wait_ge(sst, 16 * k)
                sy.dma_start(out=T1[:, :], in_=g_["hd_d"][k * 128:(k + 1) * 128, :]
                             ).then_inc(sld, 16)
                sy.dma_start(out=T5[:, :], in_=g_["prev_d"][k * 128:(k + 1) * 128, :]
                             ).then_inc(sld, 16)
                if k > 0:
                    sy.wait_ge(smm, mmv[("st", k - 1, 15)])
                sy.dma_start(out=afhi[:, :],
                             in_=bass.AP(g_["ahi_d"].tensor, k * 128,
                                         [[8 * TOK, 16], [TOK, 8], [1, 128]])
                             ).then_inc(sld, 16)
                sy.dma_start(out=aflo[:, :],
                             in_=bass.AP(g_["alo_d"].tensor, k * 128,
                                         [[8 * TOK, 16], [TOK, 8], [1, 128]])
                             ).then_inc(sld, 16)
                if k > 0:
                    sy.wait_ge(smm, mmv[("ep1", k - 1)])
                sy.dma_start(
                    out=Tg.ap().rearrange("p (c l) -> p c l", c=64),
                    in_=g_["gate_d"].rearrange("(c p) t -> p c t", p=128)
                    [:, :, k * 128:(k + 1) * 128]).then_inc(sld, 16)
                sy.wait_ge(sev, ev_yn(k))
                sy.dma_start(
                    out=g_["ynorm_d"].rearrange("(c p) t -> p c t", p=128)
                    [:, :, k * 128:(k + 1) * 128],
                    in_=T1.ap().rearrange("p (c l) -> p c l", c=64)
                ).then_inc(sst, 16)
            sy.wait_ge(sst, 16 * NCH)

        @blk.scalar
        def _(sc):
            sc = Trk(sc, g_["CHS"] if CHAIN else None)
            sc.activation(cdbf[:, :], g_["p_cdcol"][:, :], AF.Copy).then_inc(sac, 1)
            for k in range(NCH):
                for hg in range(16):
                    sc.wait_ge(smm, mmv[("st", k, hg)])
                    if 16 * k + hg > 0:
                        phg = 16 * k + hg - 1
                        sc.wait_ge(sev, ev_mtl(phg // 16, phg % 16))
                    sc.activation(rlf[:, :], psS[hg % 2][:, :], AF.Relu,
                                  scale=-1.0)
                    sc.activation(exps[hg % 2][:, :], rlf[:, :], AF.Exp,
                                  scale=-1.0).then_inc(sac, 1)
                sc.wait_ge(sev, ev_dres(k))
                sc.activation(Tg[:, :], Tg[:, :], AF.Silu).then_inc(sac, 1)
                sc.wait_ge(sev, ev_y3(k))
                sc.activation(Tg[:, :], YA[:, :], AF.Square).then_inc(sac, 1)
                sc.wait_ge(smm, mmv[("ep1", k)])
                sc.activation(rrow[0:1, 0:512], psYd[0:1, 0:512], AF.Ln,
                              scale=1.0 / 1024, bias=g_["p_eps"][0:1, :])
                sc.activation(rrow[0:1, 512:1024], psYo[0:1, 0:512], AF.Ln,
                              scale=1.0 / 1024, bias=g_["p_eps"][0:1, :])
                sc.activation(rrow[:, :], rrow[:, :], AF.Exp, scale=-0.5)
                sc.activation(rrbf[:, :], rrow[:, :], AF.Copy).then_inc(sac, 1)

        @blk.vector
        def _(v):
            v = Trk(v, g_["CHV"] if CHAIN else None)
            v.wait_ge(smm, 1)
            v.tensor_copy(cdbf2[:, :], psS[0][:, 0:512]).then_inc(sev, 1)
            v.wait_ge(sini, 16)
            v.tensor_scalar(init[:, :], init[:, :], g_["p_maskb"][:, :], None,
                            ALU.mult)
            cdv = cdbf2.ap().rearrange("p (r s) -> p r s", s=NCH)
            tp3 = tpb.ap().rearrange("p (r w) -> p r w", w=128)
            in3 = init.ap().rearrange("p (r w) -> p r w", w=128)
            for k in range(NCH):
                v.wait_ge(sld, 16 * 6 * (k + 1))
                if k > 0:
                    v.wait_ge(smm, mmv[("y", k - 1, 15)])
                for qq in range(17):
                    if qq < 16:
                        v.tensor_tensor(
                            tp3[:, (qq % 2) * 4:(qq % 2) * 4 + 4, :],
                            in3[:, qq * 4:(qq + 1) * 4, :],
                            cdv[:, qq * 4:(qq + 1) * 4, k:k + 1]
                            .broadcast_to([128, 4, 128]), ALU.mult)
                    if qq > 0:
                        qs = slice((qq - 1) * 512, qq * 512)
                        v.tensor_tensor(T5[:, qs], T5[:, qs],
                                        tpb[:, ((qq - 1) % 2) * 512:
                                            ((qq - 1) % 2) * 512 + 512],
                                        ALU.add)
                v.mi(sev, 1)
                for gi in range(G):
                    v.wait_ge(smm, mmv[("cb", k, gi)])
                    v.tensor_tensor(mtm[:, :], psMT[:, :], g_["p_mask01"][:, :],
                                    ALU.mult).then_inc(sev, 1)
                    for hh in range(2):
                        hg = 2 * gi + hh
                        v.wait_ge(sac, ac_le(k, hg))
                        v.tensor_tensor(
                            mtl[hg % 2].ap().rearrange("p (a b) -> p a b", a=8),
                            exps[hg % 2].ap().rearrange("p (a b) -> p a b", a=8),
                            mtm[:, None, :].broadcast_to([128, 8, 128]),
                            ALU.mult).then_inc(sev, 1)
                        v.wait_ge(smm, mmv[("u", k, hg)])
                        v.tensor_copy(uexp[:, :], psU[:, :])
                        v.wait_ge(smm, mmv[("y", k, hg)])
                        v.tensor_tensor(tmp5[:, :], psYo[:, :], uexp[:, :],
                                        ALU.mult)
                        ysl = slice(hg * 512, (hg + 1) * 512)
                        v.tensor_tensor(YA[:, ysl], tmp5[:, :], psYd[:, :],
                                        ALU.add).then_inc(sev, 1)
                for pr in range(NPAIR):
                    sl = slice(pr * 128, (pr + 1) * 128)
                    v.scalar_tensor_tensor(YA[:, sl], T0[:, sl],
                                           g_["p_dp"][:, pr:pr + 1],
                                           YA[:, sl], ALU.mult, ALU.add)
                v.mi(sev, 1)
                v.wait_ge(sac, ac_silu(k))
                v.tensor_tensor(YA[:, :], YA[:, :], Tg[:, :], ALU.mult
                                ).then_inc(sev, 1)
                v.wait_ge(smm, mmv[("ep2", k)])
                for pr in range(NPAIR):
                    sl = slice(pr * 128, (pr + 1) * 128)
                    gidx = pr // 8
                    bsrc = psYd[:, (gidx % 4) * 128:(gidx % 4 + 1) * 128] \
                        if gidx < 4 else \
                        psYo[:, (gidx - 4) * 128:(gidx - 3) * 128]
                    v.scalar_tensor_tensor(T1[:, sl], YA[:, sl],
                                           g_["p_gnw"][:, pr:pr + 1],
                                           bsrc, ALU.mult, ALU.mult)
                v.mi(sev, 1)

        @blk.tensor
        def _(t):
            t = Trk(t)
            t.wait_ge(sac, 1)
            for pr in range(NPAIR):
                t.matmul(psS[0][:, pr * 8:(pr + 1) * 8],
                         g_["p_e64"][:, pr * 128:(pr + 1) * 128], cdbf[:, :])
            t.mi(smm, 1)

            def emit_cb(k, gi):
                if (k, gi) != (0, 0):
                    pk, pgi = (k, gi - 1) if gi > 0 else (k - 1, G - 1)
                    t.wait_ge(sev, ev_mtm(pk, pgi))
                bsl = slice(gi * TOK + k * 128, gi * TOK + (k + 1) * 128)
                csl = slice((8 + gi) * TOK + k * 128,
                            (8 + gi) * TOK + (k + 1) * 128)
                t.matmul(psMT[:, :], g_["p_bc"][:, bsl], g_["p_bc"][:, csl]
                         ).then_inc(smm, 1)

            def emit_st(k, hg):
                gi, hh = hg // 2, hg % 2
                khg = 16 * k + hg
                if khg >= 2:
                    phg = khg - 2
                    t.wait_ge(sac, ac_le(phg // 16, phg % 16))
                if hg == 0 and k == 0:
                    t.wait_ge(sev, 1)
                h0 = (gi * 16 + hh * 8) * 128
                nh = g_["p_nahi"][:, k * 128:(k + 1) * 128]
                nl = g_["p_nalo"][:, k * 128:(k + 1) * 128]
                hrow = 2 * gi + hh
                oh = g_["p_onehot"][:, hrow * 128:(hrow + 1) * 128]
                for reg in range(2):
                    rs = slice(reg * 512, (reg + 1) * 512)
                    habs = gi * 16 + hh * 8 + reg * 4
                    idb = g_["p_idbf"][:, habs:habs + 4][:, :, None] \
                        .broadcast_to([128, 4, 128])
                    out3 = psS[hg % 2].ap().rearrange(
                        "p (a b) -> p a b", a=2)[:, reg:reg + 1, :] \
                        .rearrange("p a (c d) -> p (a c) d", c=4)
                    t.matmul(psS[hg % 2][:, rs], oh, afhi[:, rs],
                             start=True, stop=False)
                    t.matmul(psS[hg % 2][:, rs], oh, aflo[:, rs],
                             start=False, stop=False)
                    t.matmul(out3, nh, idb, start=False, stop=False)
                    t.matmul(out3, nl, idb, start=False, stop=True)
                t.mi(smm, 1)

            def emit_u(k, hg):
                gi, hh = hg // 2, hg % 2
                khg = 16 * k + hg
                if khg >= 1:
                    phg = khg - 1
                    t.wait_ge(sev, ev_ab(phg // 16, phg % 16))
                for pp in range(4):
                    pr = gi * 8 + hh * 4 + pp
                    t.matmul(psU[:, pp * 128:(pp + 1) * 128],
                             g_["p_e64"][:, pr * 128:(pr + 1) * 128],
                             g_["p_ubf"][:, k * 128:(k + 1) * 128])
                t.mi(smm, 1)

            def emit_y(k, hg):
                gi, hh = hg // 2, hg % 2
                t.wait_ge(sev, ev_mtl(k, hg))
                t.wait_ge(sev, ev_pt(k))
                csl = slice((8 + gi) * TOK + k * 128,
                            (8 + gi) * TOK + (k + 1) * 128)
                for pp in range(4):
                    for par in range(2):
                        m = 2 * pp + par
                        hgl = gi * 16 + hh * 8 + m
                        t.matmul(psYd[par * 64:(par + 1) * 64,
                                      pp * 128:(pp + 1) * 128],
                                 T1[:, hgl * 64:(hgl + 1) * 64],
                                 mtl[hg % 2][:, m * 128:(m + 1) * 128],
                                 tile_position=(0, par * 64))
                for pp in range(4):
                    for par in range(2):
                        m = 2 * pp + par
                        hgl = gi * 16 + hh * 8 + m
                        t.matmul(psYo[par * 64:(par + 1) * 64,
                                      pp * 128:(pp + 1) * 128],
                                 T5[:, hgl * 64:(hgl + 1) * 64],
                                 g_["p_bc"][:, csl],
                                 tile_position=(0, par * 64))
                t.mi(smm, 1)

            for k in range(NCH):
                t.wait_ge(sld, 16 * (6 * k + 6))
                for hg in range(16):
                    if hg == 0:
                        emit_cb(k, 0)
                        emit_st(k, 0)
                    if hg < 15:
                        if hg % 2 == 1:
                            emit_cb(k, hg // 2 + 1)
                        emit_st(k, hg + 1)
                    emit_u(k, hg)
                    emit_y(k, hg)
                t.wait_ge(sac, ac_ysq(k))
                for pr in range(NPAIR):
                    gidx = pr // 8
                    dst = psYd[0:1, (gidx % 4) * 128:(gidx % 4) * 128 + 128] \
                        if gidx < 4 else \
                        psYo[0:1, (gidx - 4) * 128:(gidx - 4) * 128 + 128]
                    t.matmul(dst, g_["p_onescol"][:, :],
                             Tg[:, pr * 128:(pr + 1) * 128],
                             start=(pr % 8 == 0), stop=(pr % 8 == 7))
                t.mi(smm, 1)
                t.wait_ge(sac, ac_rows(k))
                for gidx in range(G):
                    dst = psYd[:, (gidx % 4) * 128:(gidx % 4) * 128 + 128] \
                        if gidx < 4 else \
                        psYo[:, (gidx - 4) * 128:(gidx - 4) * 128 + 128]
                    t.matmul(dst, g_["p_ones1"][:, :],
                             rrbf[:, gidx * 128:(gidx + 1) * 128])
                t.mi(smm, 1)


def _p7_outproj(nc, env):
    g_ = env
    p_x1 = g_["p_x1"]
    NOCB = 8

    def it_(tt, ocb):
        return tt * NOCB + ocb

    def mm_piece(n, piece):
        return 8 * n + piece + 1

    def mm_mm(tt, ocb):
        return 8 * it_(tt, ocb) + 4

    def mm_stats(tt, ocb, oc):
        return 8 * it_(tt, ocb) + 5 + oc

    MM_BCAST = 129

    def ev_evac(tt, ocb, oc):
        return 4 * it_(tt, ocb) + oc + 1

    EV_MV, EV_VAR, EV_B = 65, 66, 67

    def ac_sq(tt, ocb, oc):
        return 4 * it_(tt, ocb) + oc + 1

    AC_M2, AC_RSTD = 65, 66

    ps7 = ExitStack()
    E7 = ps7.enter_context
    mrow = E7(nc.sbuf_tensor("p7_mrow", [1, TOK], F32))
    vrow = E7(nc.sbuf_tensor("p7_vrow", [1, TOK], F32))
    m2r = E7(nc.sbuf_tensor("p7_m2r", [1, TOK], F32))
    rstdr = E7(nc.sbuf_tensor("p7_rstdr", [1, TOK], F32))
    arow = E7(nc.sbuf_tensor("p7_arow", [1, TOK], BF16))
    browf = E7(nc.sbuf_tensor("p7_browf", [1, TOK], F32))
    brow = E7(nc.sbuf_tensor("p7_brow", [1, TOK], BF16))
    smm = g_["SEM"]("p7_smm")
    sev = g_["SEM"]("p7_sev")
    sac = g_["SEM"]("p7_sac")
    with nc.Block("p7") as blk, ExitStack() as ps:
        E = ps.enter_context
        ynh = E(nc.sbuf_tensor("p7_ynh", [128, NPAIR * 512], BF16))
        wst = [E(nc.sbuf_tensor(f"p7_w{i}", [128, 16 * 512], BF16)) for i in range(2)]
        xbl = E(nc.sbuf_tensor("p7_xbl", [128, 4 * 512], F32))
        rtmp = E(nc.sbuf_tensor("p7_rtmp", [128, 512], F32))
        sqb = E(nc.sbuf_tensor("p7_sqb", [128, 512], BF16))
        ps4 = [E(nc.psum_tensor("p7_ps0", [128, 2048], F32))] * 2
        psM = E(nc.psum_tensor("p7_psM", [1, TOK], F32))
        psQ = E(nc.psum_tensor("p7_psQ", [1, TOK], F32))
        syl = g_["SEM"]("p7_syl")
        sx = g_["SEM"]("p7_sx")
        sw2 = [g_["SEM"]("p7_sw0"), g_["SEM"]("p7_sw1")]

        @blk.sync
        def _(sy):
            for tt in range(2):
                if tt > 0:
                    sy.wait_ge(smm, mm_mm(0, NOCB - 1))
                sy.dma_start(
                    out=ynh.ap().rearrange("p (c t) -> p c t", c=NPAIR),
                    in_=g_["ynorm_d"].rearrange("(c p) t -> p c t", p=128)
                    [:, :, tt * 512:(tt + 1) * 512]).then_inc(syl, 16)
                for ocb in range(NOCB):
                    n = it_(tt, ocb)
                    if n >= 1:
                        sy.wait_ge(sev, 4 * n)
                    sy.dma_start(
                        out=xbl.ap().rearrange("p (c t) -> p c t", c=4),
                        in_=g_["x_fm"].rearrange("(c p) t -> p c t", p=128)
                        [:, ocb * 4:(ocb + 1) * 4, tt * 512:(tt + 1) * 512]
                    ).then_inc(sx, 16)

        @blk.gpsimd
        def _(g):
            for tt in range(2):
                for ocb in range(NOCB):
                    for piece in range(4):
                        n = it_(tt, ocb) * 4 + piece
                        if n >= 2:
                            g.wait_ge(smm, mm_piece((n - 2) // 4, (n - 2) % 4))
                        r0 = piece * 16 * 128
                        g.dma_start(
                            out=wst[n % 2].ap().rearrange("p (k c) -> p k c", k=16),
                            in_=g_["w_out_t"][r0:r0 + 16 * 128,
                                              ocb * 512:(ocb + 1) * 512]
                            .rearrange("(k p) c -> p k c", p=128)
                        ).then_inc(sw2[n % 2], 16)

        @blk.tensor
        def _(t):
            t = Trk(t)
            for tt in range(2):
                for ocb in range(NOCB):
                    n = it_(tt, ocb)
                    t.wait_ge(syl, 16 * (tt + 1))
                    if n >= 1:
                        t.wait_ge(sev, 4 * n)
                    pp = ps4[n % 2]
                    for piece in range(4):
                        ns = 4 * n + piece
                        t.wait_ge(sw2[ns % 2], 16 * (ns // 2 + 1))
                        for kk in range(16):
                            kc = piece * 16 + kk
                            for oc in range(4):
                                t.matmul(pp[:, oc * 512:(oc + 1) * 512],
                                         wst[ns % 2]
                                         [:, kk * 512 + oc * 128:kk * 512 + (oc + 1) * 128],
                                         ynh[:, kc * 512:(kc + 1) * 512],
                                         start=(kc == 0), stop=(kc == 63))
                        t.mi(smm, 1)
                    for oc in range(4):
                        t.wait_ge(sev, ev_evac(tt, ocb, oc))
                        t.wait_ge(sac, ac_sq(tt, ocb, oc))
                        gcol = slice(tt * 512, (tt + 1) * 512)
                        first = (ocb == 0 and oc == 0)
                        last = (ocb == NOCB - 1 and oc == 3)
                        occ = ocb * 4 + oc
                        t.matmul(psM[0:1, gcol], g_["p_onescol"][:, :],
                                 p_x1[:, occ * TOK + tt * 512:occ * TOK + (tt + 1) * 512],
                                 start=first, stop=last)
                        t.matmul(psQ[0:1, gcol], g_["p_onescol"][:, :],
                                 sqb[:, :], start=first, stop=last).then_inc(smm, 1)

        @blk.vector
        def _(v):
            v = Trk(v, g_["CHV"] if CHAIN else None)
            for tt in range(2):
                for ocb in range(NOCB):
                    n = it_(tt, ocb)
                    v.wait_ge(smm, mm_mm(tt, ocb))
                    v.wait_ge(sx, 16 * (n + 1))
                    pp = ps4[n % 2]
                    for oc in range(4):
                        occ = ocb * 4 + oc
                        pn = 4 * n + oc
                        if pn > 0:
                            v.wait_ge(sac, pn)
                        v.scalar_tensor_tensor(rtmp[:, :],
                                               xbl[:, oc * 512:(oc + 1) * 512],
                                               2.0, pp[:, oc * 512:(oc + 1) * 512],
                                               ALU.mult, ALU.add)
                        v.tensor_copy(
                            p_x1[:, occ * TOK + tt * 512:occ * TOK + (tt + 1) * 512],
                            rtmp[:, :]).then_inc(sev, 1)
            v.wait_ge(smm, mm_stats(1, NOCB - 1, 3))
            v.tensor_scalar(mrow[:, :], psM[0:1, :], 1.0 / D, None, ALU.mult)
            v.tensor_scalar(vrow[:, :], psQ[0:1, :], 1.0 / D, None, ALU.mult
                            ).then_inc(sev, 1)

        @blk.scalar
        def _(sc):
            sc = Trk(sc, g_["CHS"] if CHAIN else None)
            for tt in range(2):
                for ocb in range(NOCB):
                    for oc in range(4):
                        sc.wait_ge(sev, ev_evac(tt, ocb, oc))
                        pn = 4 * it_(tt, ocb) + oc
                        if pn > 0:
                            ptt, pocb, poc = pn - 1, None, None
                            pi, poc = divmod(pn - 1, 4)
                            sc.wait_ge(smm, mm_stats(pi // NOCB, pi % NOCB, poc))
                        sc.activation(sqb[:, :], rtmp[:, :], AF.Square
                                      ).then_inc(sac, 1)
            sc.wait_ge(sev, EV_MV)
            sc.activation(m2r[:, :], mrow[:, :], AF.Square).then_inc(sac, 1)


    with nc.Block("p7b") as blk, ExitStack() as ps:
        E = ps.enter_context
        psAB = E(nc.psum_tensor("p7_psAB", [128, 2 * TOK], F32))
        ttmp = E(nc.sbuf_tensor("p7b_ttmp", [128, TOK], F32))

        @blk.vector
        def _(v):
            v = Trk(v, g_["CHV"] if CHAIN else None)
            v.wait_ge(sac, AC_M2)
            v.tensor_tensor(vrow[:, :], vrow[:, :], m2r[:, :], ALU.subtract
                            ).then_inc(sev, 1)
            v.wait_ge(sac, AC_RSTD)
            v.tensor_tensor(browf[:, :], mrow[:, :], rstdr[:, :], ALU.mult)
            v.tensor_scalar(brow[:, :], browf[:, :], -1.0, None, ALU.mult
                            ).then_inc(sev, 1)
            v.wait_ge(smm, MM_BCAST)
            for kc in range(KD):
                v.tensor_tensor(ttmp[:, :], p_x1[:, kc * TOK:(kc + 1) * TOK],
                                psAB[:, 0:TOK], ALU.mult)
                v.tensor_tensor(p_x1[:, kc * TOK:(kc + 1) * TOK], ttmp[:, :],
                                psAB[:, TOK:2 * TOK], ALU.add).then_inc(sev, 1)

        @blk.scalar
        def _(sc):
            sc = Trk(sc, g_["CHS"] if CHAIN else None)
            sc.wait_ge(sev, EV_VAR)
            sc.activation(rstdr[:, :], vrow[:, :], AF.Ln,
                          bias=g_["p_eps"][0:1, :])
            sc.activation(rstdr[:, :], rstdr[:, :], AF.Exp, scale=-0.5)
            sc.activation(arow[:, :], rstdr[:, :], AF.Copy).then_inc(sac, 1)

        @blk.tensor
        def _(t):
            t = Trk(t)
            t.wait_ge(sac, AC_RSTD)
            t.wait_ge(sev, EV_B)
            for r in range(2):
                t.matmul(psAB[:, r * 512:(r + 1) * 512], g_["p_ones1"][:, :],
                         arow[:, r * 512:(r + 1) * 512])
                t.matmul(psAB[:, TOK + r * 512:TOK + (r + 1) * 512],
                         g_["p_ones1"][:, :], brow[:, r * 512:(r + 1) * 512])
            t.mi(smm, 1)
    ps7.close()


def _p8_ffn(nc, env):
    g_ = env
    p_x1 = g_["p_x1"]

    def mm_is(kc):
        return 4 + kc

    def mm_fp(ocb, piece):
        return 36 + 6 * ocb + piece

    def mm_os(ocb):
        return 36 + 6 * ocb + 4

    def mm_stats(ocb):
        return 36 + 6 * ocb + 5

    MM_BCAST = 228

    def ev_xin(kc):
        return 6 + kc

    def ev_ose(ocb):
        return 38 + 2 * ocb

    def ev_o2(ocb):
        return 39 + 2 * ocb

    EV_MV, EV_VAR, EV_B = 102, 103, 104

    def ev_fin(kc):
        return 105 + kc

    def ac_sq(ocb):
        return 2 + ocb

    AC_M2, AC_RSTD = 34, 35

    with nc.Block("p8") as blk, ExitStack() as ps:
        E = ps.enter_context
        xin = E(nc.sbuf_tensor("p8_xin", [128, KD * TOK], BF16))
        gwbf = E(nc.sbuf_tensor("p8_gwbf", [128, KD * 8], BF16))
        icsbf = E(nc.sbuf_tensor("p8_icsbf", [8, D], BF16))
        ocsbf = E(nc.sbuf_tensor("p8_ocsbf", [8, D], BF16))
        lrow = E(nc.sbuf_tensor("p8_lrow", [8, TOK], F32))
        stok = E(nc.sbuf_tensor("p8_stok", [128, 64], F32))
        estok = E(nc.sbuf_tensor("p8_estok", [128, 64], F32))
        nmx = E(nc.sbuf_tensor("p8_nmx", [128, 8], F32))
        ssum = E(nc.sbuf_tensor("p8_ssum", [128, 8], F32))
        srec = E(nc.sbuf_tensor("p8_srec", [128, 8], F32))
        rtok = E(nc.sbuf_tensor("p8_rtok", [128, 64], F32))
        routg = E(nc.sbuf_tensor("p8_routg", [8, TOK], BF16))
        wstg = [E(nc.sbuf_tensor(f"p8_wf{i}", [128, 8 * 128], BF16)) for i in range(2)]
        wsgn = [E(nc.sbuf_tensor(f"p8_ws{i}", [128, 8 * 128], BF16)) for i in range(2)]
        osb = E(nc.sbuf_tensor("p8_osb", [128, TOK], F32))
        ftmp = E(nc.sbuf_tensor("p8_ftmp", [128, TOK], F32))
        o2b = [E(nc.sbuf_tensor(f"p8_o2{i}", [128, TOK], BF16)) for i in range(2)]
        sqf = E(nc.sbuf_tensor("p8_sqf", [128, TOK], BF16))
        fob = [E(nc.sbuf_tensor(f"p8_fo{i}", [128, TOK], BF16)) for i in range(2)]
        outf2 = E(nc.sbuf_tensor("p8_of1", [128, TOK], F32))
        outf = [osb, outf2]
        mrow = E(nc.sbuf_tensor("p8_mrow", [1, TOK], F32))
        vrow = E(nc.sbuf_tensor("p8_vrow", [1, TOK], F32))
        m2r = E(nc.sbuf_tensor("p8_m2r", [1, TOK], F32))
        rstdr = vrow
        arow = E(nc.sbuf_tensor("p8_arow", [1, TOK], BF16))
        browf = m2r
        brow = E(nc.sbuf_tensor("p8_brow", [1, TOK], BF16))
        PS1 = E(nc.psum_tensor("p8_PS1", [128, TOK], F32))
        PS2 = E(nc.psum_tensor("p8_PS2", [128, TOK], F32))
        PS3 = E(nc.psum_tensor("p8_PS3", [128, TOK], F32))
        PS4 = E(nc.psum_tensor("p8_PS4", [128, TOK], F32))
        sgld = g_["SEM"]("p8_sgld")
        ssw2 = [g_["SEM"]("p8_ssw0"), g_["SEM"]("p8_ssw1")]
        ssg = g_["SEM"]("p8_ssg")
        smm = g_["SEM"]("p8_smm")
        sev = g_["SEM"]("p8_sev")
        sac = g_["SEM"]("p8_sac")
        sld2 = [g_["SEM"]("p8_sld0"), g_["SEM"]("p8_sld1")]
        s2s2 = [g_["SEM"]("p8_s2sa"), g_["SEM"]("p8_s2sb")]
        sst2 = [g_["SEM"]("p8_ssta"), g_["SEM"]("p8_sstb")]

        @blk.gpsimd
        def _(g):
            g.dma_start(out=gwbf.ap().rearrange("p (k c) -> p k c", k=KD),
                        in_=g_["gw_t"].rearrange("(k p) c -> p k c", p=128)
                        ).then_inc(sgld, 16)
            g.dma_start(out=icsbf[:, :], in_=g_["ics_i"]).then_inc(sgld, 16)
            g.dma_start(out=ocsbf[:, :], in_=g_["ocs_i"]).then_inc(sgld, 16)
            for ocb in range(32):
                for piece in range(4):
                    n = 4 * ocb + piece
                    if n >= 2:
                        pocb, ppc = divmod(n - 2, 4)
                        g.wait_ge(smm, mm_fp(pocb, ppc))
                    r0 = piece * 8 * 128
                    g.dma_start(
                        out=wstg[n % 2].ap().rearrange("p (k c) -> p k c", k=8),
                        in_=g_["w_ffn_t"][r0:r0 + 8 * 128, ocb * 128:(ocb + 1) * 128]
                        .rearrange("(k p) c -> p k c", p=128)
                    ).then_inc(ssw2[n % 2], 16)

        @blk.scalar
        def _(sc):
            sc = Trk(sc, g_["CHS"] if CHAIN else None)
            sc.wait_ge(sev, 3)
            for tc in range(8):
                sc.activation(estok[:, tc * 8:(tc + 1) * 8],
                              stok[:, tc * 8:(tc + 1) * 8], AF.Exp,
                              bias=nmx[:, tc:tc + 1],
                              accum_out=ssum[:, tc:tc + 1])
            sc.mi(sac, 1)
            for ocb in range(32):
                for piece in range(4):
                    n = 4 * ocb + piece
                    sc.wait_ge(ssw2[n % 2], 16 * (n // 2 + 1))
                    if n >= 2:
                        pocb, ppc = divmod(n - 2, 4)
                        sc.wait_ge(smm, mm_fp(pocb, ppc))
                    sc.activation(wsgn[n % 2][:, :], wstg[n % 2][:, :],
                                  AF.Sign).then_inc(ssg, 1)
                sc.wait_ge(sev, ev_o2(ocb))
                if ocb > 0:
                    sc.wait_ge(smm, mm_stats(ocb - 1))
                sc.activation(sqf[:, :], o2b[ocb % 2][:, :], AF.Square
                              ).then_inc(sac, 1)
            sc.wait_ge(sev, EV_MV)
            sc.activation(m2r[:, :], mrow[:, :], AF.Square).then_inc(sac, 1)
            sc.wait_ge(sev, EV_VAR)
            sc.activation(rstdr[:, :], vrow[:, :], AF.Ln,
                          bias=g_["p_eps"][0:1, :])
            sc.activation(rstdr[:, :], rstdr[:, :], AF.Exp, scale=-0.5)
            sc.activation(arow[:, :], rstdr[:, :], AF.Copy).then_inc(sac, 1)

        @blk.tensor
        def _(t):
            t = Trk(t)
            t.wait_ge(sgld, 48)
            for kc in range(KD):
                t.matmul(PS1[0:8, 0:512], gwbf[:, kc * 8:(kc + 1) * 8],
                         p_x1[:, kc * TOK:kc * TOK + 512],
                         start=(kc == 0), stop=(kc == KD - 1))
                t.matmul(PS1[0:8, 512:1024], gwbf[:, kc * 8:(kc + 1) * 8],
                         p_x1[:, kc * TOK + 512:(kc + 1) * TOK],
                         start=(kc == 0), stop=(kc == KD - 1))
            t.mi(smm, 1)
            t.wait_ge(sev, 1)
            for tc in range(8):
                t.transpose(PS2[:, tc * 8:(tc + 1) * 8],
                            lrow[:, tc * 128:(tc + 1) * 128], g_["p_iden8"][:, :])
            t.mi(smm, 1)
            t.wait_ge(sev, 4)
            for tc in range(8):
                t.transpose(PS3[0:8, tc * 128:(tc + 1) * 128],
                            rtok[:, tc * 8:(tc + 1) * 8], g_["p_iden128"][:, :])
            t.mi(smm, 1)
            t.wait_ge(sev, 5)
            for kc in range(KD):
                if kc >= 1:
                    t.wait_ge(sev, ev_xin(kc - 1))
                t.matmul(PS4[:, 0:512], icsbf[:, kc * 128:(kc + 1) * 128],
                         routg[:, 0:512])
                t.matmul(PS4[:, 512:1024], icsbf[:, kc * 128:(kc + 1) * 128],
                         routg[:, 512:1024]).then_inc(smm, 1)
            for ocb in range(32):
                if ocb == 0:
                    t.wait_ge(sev, ev_xin(KD - 1))
                else:
                    t.wait_ge(sev, ev_o2(ocb - 1))
                for piece in range(4):
                    n = 4 * ocb + piece
                    t.wait_ge(ssg, n + 1)
                    for kk in range(8):
                        kc = piece * 8 + kk
                        for half in range(2):
                            t.matmul(
                                PS1[:, half * 512:(half + 1) * 512],
                                wsgn[n % 2][:, kk * 128:(kk + 1) * 128],
                                xin[:, kc * TOK + half * 512:kc * TOK + (half + 1) * 512],
                                start=(kc == 0), stop=(kc == KD - 1))
                    t.mi(smm, 1)
                occ = ocb
                for half in range(2):
                    t.matmul(PS2[:, half * 512:(half + 1) * 512],
                             ocsbf[:, occ * 128:(occ + 1) * 128],
                             routg[:, half * 512:(half + 1) * 512])
                t.mi(smm, 1)
                t.wait_ge(sev, ev_o2(ocb))
                t.wait_ge(sac, ac_sq(ocb))
                first, last = (ocb == 0), (ocb == 31)
                for half in range(2):
                    hc = slice(half * 512, (half + 1) * 512)
                    t.matmul(PS3[0:1, hc], g_["p_onescol"][:, :],
                             o2b[ocb % 2][:, hc], start=first, stop=last)
                    t.matmul(PS4[0:1, hc], g_["p_onescol"][:, :],
                             sqf[:, hc], start=first, stop=last)
                t.mi(smm, 1)
            t.wait_ge(sac, AC_RSTD)
            t.wait_ge(sev, EV_B)
            for r in range(2):
                t.matmul(PS3[:, r * 512:(r + 1) * 512], g_["p_ones1"][:, :],
                         arow[:, r * 512:(r + 1) * 512])
                t.matmul(PS4[:, r * 512:(r + 1) * 512], g_["p_ones1"][:, :],
                         brow[:, r * 512:(r + 1) * 512])
            t.mi(smm, 1)

        @blk.vector
        def _(v):
            v = Trk(v, g_["CHV"] if CHAIN else None)
            v.wait_ge(smm, 1)
            v.tensor_copy(lrow[:, :], PS1[0:8, :]).then_inc(sev, 1)
            v.wait_ge(smm, 2)
            v.tensor_copy(stok[:, :], PS2[:, 0:64]).then_inc(sev, 1)
            v.tensor_reduce(nmx.ap().rearrange("p (a b) -> p a b", b=1),
                            stok.ap().rearrange("p (a b) -> p a b", a=8),
                            mybir.AxisListType.X, ALU.max, negate=True
                            ).then_inc(sev, 1)
            v.wait_ge(sac, 1)
            v.reciprocal(srec[:, :], ssum[:, :])
            v.tensor_tensor(rtok.ap().rearrange("p (a b) -> p a b", a=8),
                            estok.ap().rearrange("p (a b) -> p a b", a=8),
                            srec[:, :, None].broadcast_to([128, 8, 8]),
                            ALU.mult).then_inc(sev, 1)
            v.wait_ge(smm, 3)
            v.tensor_copy(routg[:, :], PS3[0:8, :]).then_inc(sev, 1)
            for kc in range(KD):
                v.wait_ge(smm, mm_is(kc))
                v.tensor_tensor(xin[:, kc * TOK:(kc + 1) * TOK],
                                p_x1[:, kc * TOK:(kc + 1) * TOK],
                                PS4[:, :], ALU.mult).then_inc(sev, 1)
            for ocb in range(32):
                v.wait_ge(smm, mm_os(ocb))
                v.tensor_copy(osb[:, :], PS2[:, :]).then_inc(sev, 1)
                occ = ocb
                v.tensor_tensor(ftmp[:, :], PS1[:, :], osb[:, :], ALU.mult)
                if occ >= 2:
                    v.wait_ge(s2s2[occ % 2], 16 * (occ // 2))
                    v.wait_ge(smm, mm_stats(occ - 2))
                v.scalar_tensor_tensor(o2b[ocb % 2][:, :], ftmp[:, :],
                                       g_["p_ffnb"][:, occ:occ + 1],
                                       p_x1[:, occ * TOK:(occ + 1) * TOK],
                                       ALU.add, ALU.add).then_inc(sev, 1)
            v.wait_ge(smm, mm_stats(31))
            v.tensor_scalar(mrow[:, :], PS3[0:1, :], 1.0 / D, None, ALU.mult)
            v.tensor_scalar(vrow[:, :], PS4[0:1, :], 1.0 / D, None, ALU.mult
                            ).then_inc(sev, 1)
            v.wait_ge(sac, AC_M2)
            v.tensor_tensor(vrow[:, :], vrow[:, :], m2r[:, :], ALU.subtract
                            ).then_inc(sev, 1)
            v.wait_ge(sac, AC_RSTD)
            v.tensor_tensor(browf[:, :], mrow[:, :], rstdr[:, :], ALU.mult)
            v.tensor_scalar(brow[:, :], browf[:, :], -1.0, None, ALU.mult
                            ).then_inc(sev, 1)
            v.wait_ge(smm, MM_BCAST)
            for kc in range(KD):
                v.wait_ge(sld2[kc % 2], 16 * (kc // 2 + 1))
                if kc >= 2:
                    v.wait_ge(sst2[kc % 2], 16 * (kc // 2))
                v.tensor_tensor(ftmp[:, :], fob[kc % 2][:, :], PS3[:, :],
                                ALU.mult)
                v.tensor_tensor(outf[kc % 2][:, :], ftmp[:, :], PS4[:, :],
                                ALU.add).then_inc(sev, 1)

        @blk.sync
        def _(sy):
            for ocb in range(32):
                sy.wait_ge(sev, ev_o2(ocb))
                sy.dma_start(out=g_["out2_d"][ocb * 128:(ocb + 1) * 128, :],
                             in_=o2b[ocb % 2][:, :]).then_inc(s2s2[ocb % 2], 16)
            for kc in range(KD):
                sy.wait_ge(s2s2[kc % 2], 16 * (kc // 2 + 1))
                if kc >= 2:
                    sy.wait_ge(sev, ev_fin(kc - 2))
                sy.dma_start(out=fob[kc % 2][:, :],
                             in_=g_["out2_d"][kc * 128:(kc + 1) * 128, :]
                             ).then_inc(sld2[kc % 2], 16)
                if kc >= 1:
                    sy.wait_ge(sev, ev_fin(kc - 1))
                    sy.dma_start(out=g_["out_d"][(kc - 1) * 128:kc * 128, :],
                                 in_=outf[(kc - 1) % 2][:, :]
                                 ).then_inc(sst2[(kc - 1) % 2], 16)
            sy.wait_ge(sev, ev_fin(KD - 1))
            sy.dma_start(out=g_["out_d"][(KD - 1) * 128:KD * 128, :],
                         in_=outf[(KD - 1) % 2][:, :]
                         ).then_inc(sst2[(KD - 1) % 2], 16)
            sy.wait_ge(sst2[0], 16 * (KD // 2))
            sy.wait_ge(sst2[1], 16 * (KD // 2))
            sy.wait_ge(s2s2[0], 16 * 16)
            sy.wait_ge(s2s2[1], 16 * 16)


def _finish_dummy(nc, out_d, s_fin):
    with nc.Block("fin") as blk, ExitStack() as ps:
        z = ps.enter_context(nc.sbuf_tensor("zfin", [128, TOK], F32))

        sz = ps.enter_context(nc.semaphore("fin_sz"))

        @blk.vector
        def _(v):
            v.memset(z[:, :], 0.0).then_inc(sz, 1)

        @blk.sync
        def _(sy):
            sy.wait_ge(sz, 1)
            for k in range(KD):
                sy.dma_start(out=out_d[k * 128:(k + 1) * 128, :], in_=z[:, :]
                             ).then_inc(s_fin, 16)
            sy.wait_ge(s_fin, 16 * KD)


def make_consts():
    c = {}
    c["e64_i"] = (np.arange(DIN)[None, :] // PD == np.arange(128)[:, None]).astype(BF)
    c["idbf_i"] = np.eye(128, dtype=np.float32).astype(BF)
    jj, ii = np.meshgrid(np.arange(128), np.arange(128), indexing="ij")
    c["mask01_i"] = (jj <= ii).astype(BF)
    c["ones1_i"] = np.ones((1, 128), dtype=BF)
    c["onescol_i"] = np.ones((128, 1), dtype=BF)
    c["onessq_i"] = np.ones((128, 128), dtype=BF)
    c["iden8_i"] = np.eye(8, dtype=np.float32)
    oh = np.zeros((16, 16, 128), dtype=np.float32)
    for t in range(16):
        oh[t, t, :] = 1.0
    c["onehot16_i"] = oh.transpose(1, 0, 2).reshape(16, 2048).astype(BF)
    c["iden128_i"] = np.eye(128, dtype=np.float32)
    return c


def prep_inputs(inp):
    g = {}
    g["w_in_t"] = np.ascontiguousarray(inp["L0_attn_in_proj_w"].T)
    g["w_out_t"] = np.ascontiguousarray(inp["L0_attn_out_proj_w"].T)
    g["w_ffn_t"] = np.ascontiguousarray(inp["L0_ffn_weight"].T)
    g["gw_t"] = np.ascontiguousarray(inp["L0_ffn_gate_w"].T)
    g["ics_i"] = np.ascontiguousarray(inp["L0_ffn_in_channel_scale"])
    g["ocs_i"] = np.ascontiguousarray(inp["L0_ffn_out_channel_scale"])
    g["norm_w_i"] = np.ascontiguousarray(inp["L0_attn_norm_w"].reshape(KD, 128).T)
    g["conv_w_i"] = np.ascontiguousarray(
        inp["L0_attn_conv_w"].reshape(NHBC, 128, KCONV).transpose(1, 0, 2)
        .reshape(128, NHBC * KCONV))
    g["conv_b_i"] = np.ascontiguousarray(inp["L0_attn_conv_b"].reshape(NHBC, 128).T)
    g["dtb_i"] = np.ascontiguousarray(inp["L0_attn_dt_bias"].reshape(128, 1))
    g["alog_i"] = np.ascontiguousarray(inp["L0_attn_A_log"].reshape(128, 1))
    g["ffnb_i"] = np.ascontiguousarray(inp["L0_ffn_bias"].reshape(KD, 128).T)
    g["gnw_i"] = np.ascontiguousarray(inp["L0_attn_gnorm_w"].reshape(NPAIR, 128).T)
    dp = inp["L0_attn_Dp"]
    dpc = np.empty((128, NPAIR), dtype=np.float32)
    dpc[:64, :] = dp[0::2][None, :]
    dpc[64:, :] = dp[1::2][None, :]
    g["dp_i"] = np.ascontiguousarray(dpc)
    g.update(make_consts())
    return g


def core_inputs(inp, g, b, j):
    m = dict(g)
    x = inp["L0_attn_x"][b, j * TOK:(j + 1) * TOK, :]
    m["x_fm"] = np.ascontiguousarray(x.T)
    m["masks_i"] = np.full((128, 1), 1.0 if j == 0 else 0.0, dtype=np.float32)
    m["maskb_i"] = np.full((128, 1), 1.0 if j == 1 else 0.0, dtype=np.float32)
    return m


def run(inp, stage=99, debug=()):
    key = (stage, tuple(n for n, _, _ in debug))
    if key not in _CACHE:
        _CACHE[key] = build_program(stage=stage, debug=debug)
    nc = _CACHE[key]
    g = prep_inputs(inp)
    in_maps = [core_inputs(inp, g, c // 2, c % 2) for c in range(8)]
    return run_bass_kernel_spmd(nc, in_maps, core_ids=list(range(8)))


def kernel(**inputs):
    res = run(inputs, stage=99)
    out = np.empty((B, S, D), dtype=np.float32)
    for c in range(8):
        b, j = c // 2, c % 2
        out[b, j * TOK:(j + 1) * TOK, :] = res.results[c]["out_d"].T
    return out


# version 13
# speedup vs baseline: 1.1746x; 1.1642x over previous
import numpy as np
import ml_dtypes
from contextlib import ExitStack

import concourse.bass as bass
import concourse.mybir as mybir
from concourse.bass_utils import run_bass_kernel_spmd

F32 = mybir.dt.float32
BF16 = mybir.dt.bfloat16
AF = mybir.ActivationFunctionType
ALU = mybir.AluOpType
BF = ml_dtypes.bfloat16

B, S, D = 4, 2048, 4096
H, PD, G, ND = 128, 64, 8, 128
KCONV, CHUNK = 4, 128
DIN = H * PD
CONVD = DIN + 2 * G * ND
PROJ = DIN + CONVD + H
EPS = 1e-5
TOK = 1024
NCH = TOK // CHUNK
KD = D // 128
NPAIR = DIN // 128
NHBC = CONVD // 128
NGATE = DIN // 128
HBC0 = DIN
DT0 = DIN + CONVD
RG = [[0, 1], [2, 3], [4, 5], [6, 7]]

_CACHE = {}
CHAIN = "deps"


def _box(o):
    th = o.tensor
    name = th.name
    try:
        shp = list(th.shape)
        rowlen = 1
        for s_ in shp[1:]:
            rowlen *= int(s_)
        pat = [(int(p[0]), int(p[1])) for p in o.ap]
        if len(pat) < 2:
            raise ValueError
        off = int(o.offset)
        p0, f0 = divmod(off, rowlen)
        s0, n0 = pat[0]
        if n0 > 1 and s0 != 0:
            if s0 % rowlen:
                raise ValueError
            pext = 1 + (s0 // rowlen) * (n0 - 1)
        else:
            pext = 1
        fext = 1
        for s_, n_ in pat[1:]:
            if s_ < 0:
                raise ValueError
            fext += s_ * (n_ - 1)
        if f0 + fext > rowlen:
            raise ValueError
        return (name, p0, p0 + pext, f0, f0 + fext)
    except Exception:
        return (name, 0, 1 << 40, 0, 1 << 40)


def _op_boxes(args, kw):
    outs, ins = [], []
    first = True
    for o in args:
        if hasattr(o, "ap") and hasattr(o, "tensor"):
            (outs if first else ins).append(_box(o))
            first = False
    for knm, o in kw.items():
        if hasattr(o, "ap") and hasattr(o, "tensor"):
            b = _box(o)
            ins.append(b)
            if knm == "accum_out":
                outs.append(b)
    return outs, ins


def _inter(a, b):
    return (a[0] == b[0] and a[1] < b[2] and b[1] < a[2]
            and a[3] < b[4] and b[3] < a[4])


def _clash(bo, bi, ro, ri):
    for x in bo:
        for y in ro:
            if _inter(x, y):
                return True
        for y in ri:
            if _inter(x, y):
                return True
    for x in bi:
        for y in ro:
            if _inter(x, y):
                return True
    return False


class _IP:

    def __init__(self, trk):
        self._t = trk

    def then_inc(self, sem, k=1):
        t = self._t
        t._wait_floor(t._chain[1])
        return t._e.nop().then_inc(sem, k)


class Trk:
    _SKIP = {"wait_ge", "nop", "engine_nop", "then_inc", "dma_start",
             "collective_compute", "register", "register64", "reg_load",
             "reg_save", "reg_alu", "reg_add", "reg_mov", "partition_id",
             "tpb_base_ld"}

    def __init__(self, eng, chain=None):
        object.__setattr__(self, "_e", eng)
        object.__setattr__(self, "last", None)
        object.__setattr__(self, "_chain", chain)
        object.__setattr__(self, "_mode", CHAIN if chain is not None else None)
        object.__setattr__(self, "_floor", 0)
        object.__setattr__(self, "_recs", [])

    def _wait_floor(self, v):
        if v > self._floor:
            self._e.wait_ge(self._chain[0], v)
            object.__setattr__(self, "_floor", v)
            rc = self._recs
            if rc:
                self._recs[:] = [r for r in rc if r[0] > v]

    def __getattr__(self, name):
        a = getattr(self._e, name)
        if callable(a) and name not in Trk._SKIP:
            def w(*args, **kw):
                ch = self._chain
                if ch is None:
                    r = a(*args, **kw)
                    if hasattr(r, "then_inc"):
                        object.__setattr__(self, "last", r)
                    return r
                if self._mode == "deps":
                    bo, bi = _op_boxes(args, kw)
                    need = self._floor
                    for seq, ro, ri in self._recs:
                        if seq > need and _clash(bo, bi, ro, ri):
                            need = seq
                    self._wait_floor(need)
                else:
                    self._wait_floor(ch[1])
                r = a(*args, **kw)
                if not hasattr(r, "then_inc"):
                    return r
                object.__setattr__(self, "last", r)
                r.then_inc(ch[0], 1)
                ch[1] += 1
                if self._mode == "deps":
                    self._recs.append((ch[1], bo, bi))
                return _IP(self)
            return w
        return a

    def wait_ge(self, sem, val):
        return self._e.wait_ge(sem, val)

    def nop(self):
        return self._e.nop()

    def mi(self, sem, k=1):
        ch = self._chain
        if ch is not None:
            self._wait_floor(ch[1])
            self._e.nop().then_inc(sem, k)
        else:
            self.last.then_inc(sem, k)



def build_program(stage=99, debug=()):
    nc = bass.Bass("TRN2", target_bir_lowering=False)

    I = lambda n, s_, d_: nc.dram_tensor(n, s_, d_, kind="ExternalInput").ap()
    x_fm = I("x_fm", [D, TOK], F32)
    w_in_t = I("w_in_t", [D, PROJ], F32)
    w_out_t = I("w_out_t", [DIN, D], F32)
    w_ffn_t = I("w_ffn_t", [D, D], F32)
    gw_t = I("gw_t", [D, 8], F32)
    ics_i = I("ics_i", [8, D], F32)
    ocs_i = I("ocs_i", [8, D], F32)
    norm_w_i = I("norm_w_i", [128, KD], F32)
    conv_w_i = I("conv_w_i", [128, NHBC * KCONV], F32)
    conv_b_i = I("conv_b_i", [128, NHBC], F32)
    dtb_i = I("dtb_i", [128, 1], F32)
    alog_i = I("alog_i", [128, 1], F32)
    gnw_i = I("gnw_i", [128, NPAIR], F32)
    dp_i = I("dp_i", [128, NPAIR], F32)
    ffnb_i = I("ffnb_i", [128, KD], F32)
    e64_i = I("e64_i", [128, DIN], BF16)
    mask01_i = I("mask01_i", [128, 128], BF16)
    ones1_i = I("ones1_i", [1, 128], BF16)
    onescol_i = I("onescol_i", [128, 1], BF16)
    onessq_i = I("onessq_i", [128, 128], BF16)
    iden8_i = I("iden8_i", [8, 8], F32)
    iden128_i = I("iden128_i", [128, 128], F32)
    masks_i = I("masks_i", [128, 1], F32)
    onehot16_i = I("onehot16_i", [16, 2048], BF16)
    idbf_i = I("idbf_i", [128, 128], BF16)
    maskb_i = I("maskb_i", [128, 1], F32)

    out_d = nc.dram_tensor("out_d", [D, TOK], F32, kind="ExternalOutput").ap()
    DR = lambda n, s_, d_: nc.dram_tensor(n, s_, d_).ap()
    hid_d = DR("hid_d", [DIN, TOK], BF16)
    gate_d = DR("gate_d", [DIN, TOK], BF16)
    hd_d = DR("hd_d", [TOK, DIN], BF16)
    prev_d = DR("prev_d", [NCH * 128, DIN], BF16)
    ahi_d = DR("ahi_d", [128, TOK], BF16)
    alo_d = DR("alo_d", [128, TOK], BF16)
    ynorm_d = DR("ynorm_d", [DIN, TOK], BF16)
    out2_d = DR("out2_d", [D, TOK], BF16)
    cc1_in = DR("cc1_in", [128, NHBC * 3], F32)
    cc1_out = DR("cc1_out", [128, NHBC * 3], F32)
    cc2_in = DR("cc2_in", [128, DIN], F32)
    cc2_out = DR("cc2_out", [128, DIN], F32)

    dbg = {}
    for name, shape, dt_ in debug:
        dbg[name] = nc.dram_tensor(name, shape, dt_, kind="ExternalOutput").ap()
    dbgctr = [0]

    def dump(pairs, sdbg):
        todo = [(nm, src) for nm, src in pairs if nm in dbg]
        if not todo:
            return
        with nc.Block("dbg%d" % dbgctr[0]) as blk:
            @blk.sync
            def _(sy):
                for nm, src in todo:
                    sy.dma_start(out=dbg[nm], in_=src).then_inc(sdbg, 16)
                    dbgctr[0] += 16
                sy.wait_ge(sdbg, dbgctr[0])

    st = ExitStack()
    with st:
        SB = lambda n, s_, d_: st.enter_context(nc.sbuf_tensor(n, s_, d_))
        SEM = lambda n: st.enter_context(nc.semaphore(n))

        p_ones1 = SB("p_ones1", [1, 128], BF16)
        p_onescol = SB("p_onescol", [128, 1], BF16)
        p_onessq = SB("p_onessq", [128, 128], BF16)
        p_mask01 = SB("p_mask01", [128, 128], BF16)
        p_masks = SB("p_masks", [128, 1], F32)
        p_maskb = SB("p_maskb", [128, 1], F32)
        p_iden8 = SB("p_iden8", [8, 8], F32)
        p_iden128 = SB("p_iden128", [128, 128], F32)
        p_ffnb = SB("p_ffnb", [128, KD], F32)
        p_eps = SB("p_eps", [128, 1], F32)
        CHV = [SEM("chain_v"), 0]
        CHS = [SEM("chain_s"), 0]
        s_cc1 = SEM("s_cc1")
        s_cc2 = SEM("s_cc2")
        s_fin = SEM("s_fin")
        s_dbg = SEM("s_dbg")

        env = dict(locals())

        with ExitStack() as ms:
            MSB = lambda n, s_, d_: ms.enter_context(nc.sbuf_tensor(n, s_, d_))
            p_norm_w = MSB("p_norm_w", [128, KD], F32)
            p_conv_w = MSB("p_conv_w", [128, NHBC * KCONV], F32)
            p_conv_b = MSB("p_conv_b", [128, NHBC], F32)
            p_dtbias = MSB("p_dtbias", [128, 1], F32)
            p_alog = MSB("p_alog", [128, 1], F32)
            p_negea = MSB("p_negea", [128, 1], F32)
            p_gnw = MSB("p_gnw", [128, NPAIR], F32)
            p_dp = MSB("p_dp", [128, NPAIR], F32)
            p_e64 = MSB("p_e64", [128, DIN], BF16)
            p_dtbf = MSB("p_dtbf", [128, TOK], BF16)
            p_dwbf = MSB("p_dwbf", [128, TOK], BF16)
            p_ubf = MSB("p_ubf", [128, TOK], BF16)
            p_nahi = MSB("p_nahi", [128, TOK], BF16)
            p_nalo = MSB("p_nalo", [128, TOK], BF16)
            p_dcol = MSB("p_dcol", [128, NCH], F32)
            p_cdcol = MSB("p_cdcol", [128, NCH], F32)
            p_bc = MSB("p_bc", [128, 16 * TOK], BF16)
            p_halo = MSB("p_halo", [128, NHBC * 3], F32)
            p_pre3 = MSB("p_pre3", [128, NHBC * 3], F32)
            p_patch = MSB("p_patch", [128, 192], BF16)
            p_onehot = MSB("p_onehot", [16, 2048], BF16)
            p_idbf = MSB("p_idbf", [128, 128], BF16)
            env.update(locals())

            _p0_loads(nc, env)
            with ExitStack() as hs:
                p_h = hs.enter_context(nc.sbuf_tensor("p_h", [128, KD * TOK], BF16))
                env["p_h"] = p_h
                _p1_rms(nc, env)
                dump([("dbg_h", p_h[:, 0:4 * TOK])], s_dbg)
                if stage <= 1:
                    _finish_dummy(nc, out_d, s_fin)
                    return nc
                _p2_dt(nc, env)
                dump([("dbg_ubf", p_ubf[:, :]), ("dbg_dw", p_dwbf[:, :])], s_dbg)
                if stage <= 2:
                    _finish_dummy(nc, out_d, s_fin)
                    return nc
                _p3_inproj(nc, env, hbc=True)
                dump([("dbg_bc", p_bc[:, 0:2 * TOK]), ("dbg_halo", p_halo[:, :])],
                     s_dbg)
                if stage <= 3:
                    _finish_dummy(nc, out_d, s_fin)
                    return nc
                _p3_inproj(nc, env, hbc=False)
            env.pop("p_h")
            if stage == 35:
                _finish_dummy(nc, out_d, s_fin)
                return nc

            _p4_sweep1(nc, env)
            dump([("dbg_patch", p_patch[:, :]), ("dbg_cd", p_cdcol[:, :]),
                  ("dbg_prev7", prev_d[7 * 128:8 * 128, :]),
                  ("dbg_hd0", hd_d[0:128, :]),
                  ("dbg_ccst", cc2_in)], s_dbg)
            if stage <= 4:
                _finish_dummy(nc, out_d, s_fin)
                return nc

            _p6_sweep2(nc, env)
            dump([("dbg_yn", ynorm_d.rearrange("(c p) t -> p c t", p=128)
                   [:, :, 7 * 128:8 * 128])], s_dbg)
            if stage <= 6:
                _finish_dummy(nc, out_d, s_fin)
                return nc

        p_x1 = SB("p_x1", [128, KD * TOK], BF16)
        env["p_x1"] = p_x1
        _p7_outproj(nc, env)
        dump([("dbg_x1", p_x1[:, 0:4 * TOK])], s_dbg)
        if stage <= 7:
            _finish_dummy(nc, out_d, s_fin)
            return nc

        _p8_ffn(nc, env)
    return nc


def _p0_loads(nc, env):
    g_ = env
    with nc.Block("p0") as blk:
        s0 = g_["SEM"]("p0_s")
        @blk.sync
        def _(sy):
            loads = [
                (g_["p_norm_w"], g_["norm_w_i"]), (g_["p_conv_w"], g_["conv_w_i"]),
                (g_["p_conv_b"], g_["conv_b_i"]), (g_["p_dtbias"], g_["dtb_i"]),
                (g_["p_alog"], g_["alog_i"]), (g_["p_masks"], g_["masks_i"]),
                (g_["p_maskb"], g_["maskb_i"]), (g_["p_mask01"], g_["mask01_i"]),
                (g_["p_ones1"], g_["ones1_i"]), (g_["p_onescol"], g_["onescol_i"]),
                (g_["p_onessq"], g_["onessq_i"]), (g_["p_iden8"], g_["iden8_i"]),
                (g_["p_iden128"], g_["iden128_i"]), (g_["p_gnw"], g_["gnw_i"]),
                (g_["p_dp"], g_["dp_i"]), (g_["p_ffnb"], g_["ffnb_i"]),
                (g_["p_onehot"], g_["onehot16_i"]),
                (g_["p_idbf"], g_["idbf_i"]),
            ]
            for dst, src in loads:
                sy.dma_start(out=dst[:, :], in_=src).then_inc(s0, 16)
            sy.wait_ge(s0, 16 * len(loads))

        @blk.gpsimd
        def _(g):
            s0g = g_["SEM"]("p0_sg")
            g.dma_start(out=g_["p_e64"][:, :], in_=g_["e64_i"]).then_inc(s0g, 16)
            g.wait_ge(s0g, 16)
    with nc.Block("p0b") as blk:
        @blk.scalar
        def _(sc):
            sc = Trk(sc, g_["CHS"] if CHAIN else None)
            sc.activation(g_["p_negea"][:, :], g_["p_alog"][:, :], AF.Exp)
    with nc.Block("p0c") as blk:
        @blk.vector
        def _(v):
            v = Trk(v, g_["CHV"] if CHAIN else None)
            v.tensor_scalar(g_["p_negea"][:, :], g_["p_negea"][:, :], -1.0,
                            None, ALU.mult)
            v.memset(g_["p_eps"][:, :], EPS)


def _p1_rms(nc, env):
    g_ = env
    p_h, p_onescol, p_ones1, p_norm_w = g_["p_h"], g_["p_onescol"], g_["p_ones1"], g_["p_norm_w"]
    x_fm = g_["x_fm"]
    with nc.Block("p1") as blk, ExitStack() as ps:
        E = ps.enter_context
        xa = E(nc.sbuf_tensor("p1_xa", [128, TOK], F32))
        xb = E(nc.sbuf_tensor("p1_xb", [128, TOK], F32))
        x2 = E(nc.sbuf_tensor("p1_x2", [128, TOK], BF16))
        rrow = E(nc.sbuf_tensor("p1_rrow", [1, TOK], F32))
        rrbf = E(nc.sbuf_tensor("p1_rrbf", [1, TOK], BF16))
        ps_ss = E(nc.psum_tensor("p1_psss", [1, TOK], F32))
        ps_rb = E(nc.psum_tensor("p1_psrb", [128, TOK], F32))
        sx2 = [g_["SEM"]("p1_sx0"), g_["SEM"]("p1_sx1")]
        sq = g_["SEM"]("p1_sq")
        sm = g_["SEM"]("p1_sm")
        sv = g_["SEM"]("p1_sv")
        xt = [xa, xb]

        @blk.sync
        def _(sy):
            for k in range(KD):
                if k >= 2:
                    sy.wait_ge(sq, k - 1)
                sy.dma_start(out=xt[k % 2][:, :],
                             in_=x_fm[k * 128:(k + 1) * 128, :]).then_inc(sx2[k % 2], 16)
            for k in range(KD):
                if k < 2:
                    sy.wait_ge(sq, KD)
                else:
                    sy.wait_ge(sv, k - 1)
                sy.dma_start(out=xt[k % 2][:, :],
                             in_=x_fm[k * 128:(k + 1) * 128, :]).then_inc(sx2[k % 2], 16)

        @blk.scalar
        def _(sc):
            sc = Trk(sc, g_["CHS"] if CHAIN else None)
            for k in range(KD):
                sc.wait_ge(sx2[k % 2], 16 * (k // 2 + 1))
                if k > 0:
                    sc.wait_ge(sm, k)
                sc.activation(x2[:, :], xt[k % 2][:, :], AF.Square).then_inc(sq, 1)
            sc.wait_ge(sm, KD)
            sc.activation(rrow[:, :], ps_ss[:, :], AF.Ln, scale=1.0 / D,
                          bias=g_["p_eps"][0:1, :]).then_inc(sq, 1)
            sc.wait_ge(sq, KD + 1)
            sc.activation(rrow[:, :], rrow[:, :], AF.Exp, scale=-0.5).then_inc(sq, 1)

        @blk.tensor
        def _(t):
            t = Trk(t)
            for k in range(KD):
                t.wait_ge(sq, k + 1)
                t.matmul(ps_ss[0:1, 0:512], p_onescol[:, :], x2[:, 0:512],
                         start=(k == 0), stop=(k == KD - 1))
                t.matmul(ps_ss[0:1, 512:1024], p_onescol[:, :], x2[:, 512:1024],
                         start=(k == 0), stop=(k == KD - 1)).then_inc(sm, 1)
            t.wait_ge(sq, KD + 3)
            t.matmul(ps_rb[:, 0:512], p_ones1[:, :], rrbf[:, 0:512])
            t.matmul(ps_rb[:, 512:1024], p_ones1[:, :], rrbf[:, 512:1024]
                     ).then_inc(sm, 1)

        @blk.vector
        def _(v):
            v = Trk(v, g_["CHV"] if CHAIN else None)
            v.wait_ge(sq, KD + 2)
            v.tensor_copy(rrbf[:, :], rrow[:, :]).then_inc(sq, 1)
            v.wait_ge(sm, KD + 1)
            for k in range(KD):
                v.wait_ge(sx2[k % 2], 16 * (KD // 2 + k // 2 + 1))
                v.scalar_tensor_tensor(p_h[:, k * TOK:(k + 1) * TOK],
                                       xt[k % 2][:, :], p_norm_w[:, k:k + 1],
                                       ps_rb[:, :], ALU.mult, ALU.mult
                                       ).then_inc(sv, 1)


def _p2_dt(nc, env):
    g_ = env
    p_h = g_["p_h"]
    with nc.Block("p2") as blk, ExitStack() as ps:
        E = ps.enter_context
        wdt = E(nc.sbuf_tensor("p2_wdt", [128, KD * 128], BF16))
        ps_dt = E(nc.psum_tensor("p2_psdt", [128, TOK], F32))
        t_abs = E(nc.sbuf_tensor("p2_abs", [128, TOK], F32))
        t_e = E(nc.sbuf_tensor("p2_e", [128, TOK], F32))
        t_r = E(nc.sbuf_tensor("p2_r", [128, TOK], F32))
        t_a = E(nc.sbuf_tensor("p2_a", [128, TOK], F32))
        t_w = E(nc.sbuf_tensor("p2_w", [128, TOK], F32))
        t_hi = E(nc.sbuf_tensor("p2_hi", [128, TOK], BF16))
        p_dtv = E(nc.sbuf_tensor("p2_dtv", [128, TOK], F32))
        p_acs = E(nc.sbuf_tensor("p2_acs", [128, TOK], F32))
        t_lo = E(nc.sbuf_tensor("p2_lo", [128, TOK], BF16))
        sw = g_["SEM"]("p2_sw")
        sm = g_["SEM"]("p2_sm")
        sa = g_["SEM"]("p2_sa")
        sv = g_["SEM"]("p2_sv")
        ss = g_["SEM"]("p2_ss")

        @blk.gpsimd
        def _(g):
            g.dma_start(out=wdt.ap().rearrange("p (k c) -> p k c", k=KD),
                        in_=g_["w_in_t"][:, DT0:DT0 + 128].rearrange(
                            "(k p) c -> p k c", p=128)).then_inc(sw, 16)

        @blk.tensor
        def _(t):
            t = Trk(t)
            t.wait_ge(sw, 16)
            for k in range(KD):
                t.matmul(ps_dt[:, 0:512], wdt[:, k * 128:(k + 1) * 128],
                         p_h[:, k * TOK:k * TOK + 512],
                         start=(k == 0), stop=(k == KD - 1))
                t.matmul(ps_dt[:, 512:1024], wdt[:, k * 128:(k + 1) * 128],
                         p_h[:, k * TOK + 512:(k + 1) * TOK],
                         start=(k == 0), stop=(k == KD - 1))
            t.mi(sm, 1)

        @blk.scalar
        def _(sc):
            sc = Trk(sc, g_["CHS"] if CHAIN else None)
            sch = g_["SEM"]("p2_sch")
            sc.wait_ge(sm, 1)
            sc.activation(t_abs[:, :], ps_dt[:, :], AF.Abs,
                          bias=g_["p_dtbias"][:, :]).then_inc(sch, 1)
            sc.wait_ge(sch, 1)
            sc.activation(t_e[:, :], t_abs[:, :], AF.Exp, scale=-1.0).then_inc(sch, 1)
            sc.wait_ge(sch, 2)
            sc.activation(t_e[:, :], t_e[:, :], AF.Ln, bias=1.0)
            sc.activation(t_r[:, :], ps_dt[:, :], AF.Relu,
                          bias=g_["p_dtbias"][:, :]).then_inc(sa, 1)
            for k in range(NCH):
                sc.wait_ge(sv, 3 + k)
                ck = slice(k * 128, (k + 1) * 128)
                last = k * 128 + 127
                sc.activation(t_w[:, ck], p_acs[:, ck], AF.Exp, scale=-1.0,
                              bias=p_acs[:, last:last + 1])
                sc.activation(g_["p_ubf"][:, ck], p_acs[:, ck], AF.Exp)
                sc.activation(g_["p_dcol"][:, k:k + 1],
                              p_acs[:, last:last + 1], AF.Exp).then_inc(sa, 1)

        @blk.vector
        def _(v):
            v = Trk(v, g_["CHV"] if CHAIN else None)
            v.wait_ge(sa, 1)
            v.tensor_add(p_dtv[:, :], t_r[:, :], t_e[:, :]).then_inc(sv, 1)
            v.tensor_scalar(t_a[:, :], p_dtv[:, :], g_["p_negea"][:, :],
                            None, ALU.mult).then_inc(sv, 1)
            for k in range(NCH):
                ck = slice(k * 128, (k + 1) * 128)
                v.tensor_tensor_scan(p_acs[:, ck], t_a[:, ck], t_a[:, ck],
                                     0.0, ALU.add, ALU.bypass).then_inc(sv, 1)
            v.tensor_copy(g_["p_dtbf"][:, :], p_dtv[:, :])
            v.tensor_copy(t_hi[:, :], p_acs[:, :])
            v.tensor_tensor(t_lo[:, :], p_acs[:, :], t_hi[:, :], ALU.subtract)
            v.tensor_scalar(g_["p_nahi"][:, :], t_hi[:, :], -1.0, None, ALU.mult)
            v.tensor_scalar(g_["p_nalo"][:, :], t_lo[:, :], -1.0, None, ALU.mult)
            v.wait_ge(sa, 1 + NCH)
            v.tensor_tensor(g_["p_dwbf"][:, :], p_dtv[:, :], t_w[:, :],
                            ALU.mult)
            v.memset(g_["p_cdcol"][:, 0:1], 1.0)
            for k in range(1, NCH):
                v.tensor_tensor(g_["p_cdcol"][:, k:k + 1],
                                g_["p_cdcol"][:, k - 1:k],
                                g_["p_dcol"][:, k - 1:k], ALU.mult)
            v.mi(sv, 1)

        @blk.sync
        def _(sy):
            sy.wait_ge(sv, 3 + NCH)
            sy.dma_start(out=g_["ahi_d"], in_=t_hi[:, :]).then_inc(ss, 16)
            sy.dma_start(out=g_["alo_d"], in_=t_lo[:, :]).then_inc(ss, 16)
            sy.wait_ge(ss, 32)


def _p3_inproj(nc, env, hbc):
    g_ = env
    p_h = g_["p_h"]
    tag = "p3" if hbc else "p5"
    NB = NHBC if hbc else NGATE
    col0 = HBC0 if hbc else 0
    with nc.Block(tag) as blk, ExitStack() as ps:
        E = ps.enter_context
        wb = [E(nc.sbuf_tensor(f"{tag}_w{i}", [128, KD * 128], BF16)) for i in range(3)]
        pp2 = [E(nc.psum_tensor(f"{tag}_ps{i}", [128, TOK], F32)) for i in range(2)]
        stg = [E(nc.sbuf_tensor(f"{tag}_st{i}", [128, TOK], BF16)) for i in range(2)]
        hbc_f = E(nc.sbuf_tensor(f"{tag}_hf", [128, TOK], F32)) if hbc else None
        cbuf = E(nc.sbuf_tensor(f"{tag}_cb", [128, TOK], F32)) if hbc else None
        sw3 = [g_["SEM"](f"{tag}_sw{i}") for i in range(3)]
        smm = g_["SEM"](f"{tag}_smm")
        sev = g_["SEM"](f"{tag}_sev")
        scv = g_["SEM"](f"{tag}_scv")
        ssi = g_["SEM"](f"{tag}_ssi")
        sst2 = [g_["SEM"](f"{tag}_sst0"), g_["SEM"](f"{tag}_sst1")]
        shm = g_["SEM"](f"{tag}_shm")

        @blk.gpsimd
        def _(g):
            for cc in range(NB):
                if cc >= 3:
                    g.wait_ge(smm, cc - 2)
                c0 = col0 + cc * 128
                g.dma_start(out=wb[cc % 3].ap().rearrange("p (k c) -> p k c", k=KD),
                            in_=g_["w_in_t"][:, c0:c0 + 128].rearrange(
                                "(k p) c -> p k c", p=128)).then_inc(sw3[cc % 3], 16)
            if hbc:
                g.wait_ge(shm, 16)
                g.collective_compute("AllReduce", ALU.add, replica_groups=RG,
                                     ins=[g_["cc1_in"].opt()],
                                     outs=[g_["cc1_out"].opt()]
                                     ).then_inc(g_["s_cc1"], 1)

        @blk.tensor
        def _(t):
            t = Trk(t)
            for cc in range(NB):
                t.wait_ge(sw3[cc % 3], 16 * (cc // 3 + 1))
                if cc >= 2:
                    t.wait_ge(sev if hbc else ssi, cc - 1)
                pp = pp2[cc % 2]
                w = wb[cc % 3]
                for k in range(KD):
                    t.matmul(pp[:, 0:512], w[:, k * 128:(k + 1) * 128],
                             p_h[:, k * TOK:k * TOK + 512],
                             start=(k == 0), stop=(k == KD - 1))
                    t.matmul(pp[:, 512:1024], w[:, k * 128:(k + 1) * 128],
                             p_h[:, k * TOK + 512:(k + 1) * TOK],
                             start=(k == 0), stop=(k == KD - 1))
                t.mi(smm, 1)

        @blk.vector
        def _(v):
            v = Trk(v, g_["CHV"] if CHAIN else None)
            if hbc:
                v.memset(stg[0][:, 0:3], 0.0)
                v.memset(stg[1][:, 0:3], 0.0)
                v.memset(g_["p_bc"][:, :], 0.0)
            for cc in range(NB):
                v.wait_ge(smm, cc + 1)
                pp = pp2[cc % 2]
                if hbc:
                    v.tensor_copy(hbc_f[:, 0:512], pp[:, 0:512])
                    v.tensor_copy(hbc_f[:, 512:1024], pp[:, 512:1024]
                                  ).then_inc(sev, 1)
                    v.tensor_copy(g_["p_halo"][:, cc * 3:cc * 3 + 3],
                                  hbc_f[:, 1021:1024])
                    v.tensor_copy(g_["p_pre3"][:, cc * 3:cc * 3 + 3],
                                  hbc_f[:, 0:3])
                    if cc >= 1:
                        v.wait_ge(ssi, cc)
                    cw = g_["p_conv_w"]
                    v.tensor_scalar(cbuf[:, 3:1024], hbc_f[:, 0:1021],
                                    cw[:, cc * 4:cc * 4 + 1],
                                    g_["p_conv_b"][:, cc:cc + 1],
                                    ALU.mult, ALU.add)
                    for i in (1, 2, 3):
                        v.scalar_tensor_tensor(cbuf[:, 3:1024],
                                               hbc_f[:, i:i + 1021],
                                               cw[:, cc * 4 + i:cc * 4 + i + 1],
                                               cbuf[:, 3:1024], ALU.mult, ALU.add)
                    v.mi(scv, 1)
                else:
                    if cc >= 2:
                        v.wait_ge(sst2[cc % 2], 16 * (cc // 2))
                    v.tensor_copy(stg[cc % 2][:, 0:512], pp[:, 0:512])
                    v.tensor_copy(stg[cc % 2][:, 512:1024], pp[:, 512:1024]
                                  ).then_inc(ssi, 1)
            if hbc:
                v.wait_ge(scv, NB)
                v.tensor_scalar(g_["p_halo"][:, :], g_["p_halo"][:, :],
                                g_["p_masks"][:, :], None, ALU.mult
                                ).then_inc(scv, 1)

        @blk.scalar
        def _(sc):
            sc = Trk(sc, g_["CHS"] if CHAIN else None)
            if not hbc:
                return
            for cc in range(NB):
                sc.wait_ge(scv, cc + 1)
                if cc < 64:
                    if cc >= 2:
                        sc.wait_ge(sst2[cc % 2], 16 * (cc // 2))
                    sc.activation(stg[cc % 2][:, 3:1024], cbuf[:, 3:1024],
                                  AF.Silu).then_inc(ssi, 1)
                else:
                    b = cc - 64
                    sc.activation(g_["p_bc"][:, b * TOK + 3:(b + 1) * TOK],
                                  cbuf[:, 3:1024], AF.Silu).then_inc(ssi, 1)

        @blk.sync
        def _(sy):
            for cc in range(NB):
                sy.wait_ge(ssi, cc + 1)
                if hbc:
                    if cc < 64:
                        sy.dma_start(out=g_["hid_d"][cc * 128:(cc + 1) * 128, :],
                                     in_=stg[cc % 2][:, :]
                                     ).then_inc(sst2[cc % 2], 16)
                    else:
                        sy.nop().then_inc(sst2[cc % 2], 16)
                else:
                    sy.dma_start(out=g_["gate_d"][cc * 128:(cc + 1) * 128, :],
                                 in_=stg[cc % 2][:, :]).then_inc(sst2[cc % 2], 16)
            sy.wait_ge(sst2[0], 16 * ((NB + 1) // 2))
            sy.wait_ge(sst2[1], 16 * (NB // 2))
            if hbc:
                sy.wait_ge(scv, NB + 1)
                sy.dma_start(out=g_["cc1_in"], in_=g_["p_halo"][:, :]
                             ).then_inc(shm, 16)


def _p4_sweep1(nc, env):
    g_ = env

    def sev_e64d(k, gi):
        return 56 * k + 7 * gi + 1

    def sev_hd(k, gi, hf):
        return 56 * k + 7 * gi + 2 + 3 * hf

    def sev_hdw(k, gi, hf):
        return 56 * k + 7 * gi + 3 + 3 * hf

    def sev_rec(k, gi, hf):
        return 56 * k + 7 * gi + 4 + 3 * hf

    def mm_S(k, gi, hf):
        return 32 * k + 4 * gi + 2 * hf + 1

    def mm_D(k, gi, hf):
        return 32 * k + 4 * gi + 2 * hf + 2

    def unq2(q2):
        q, hf = q2 // 2, q2 % 2
        return q // 8, q % 8, hf

    with nc.Block("p4") as blk, ExitStack() as ps:
        E = ps.enter_context
        T0 = [E(nc.sbuf_tensor(f"s1_t0{i}", [128, DIN], BF16)) for i in range(2)]
        T1 = [E(nc.sbuf_tensor(f"s1_t1{i}", [128, DIN], BF16)) for i in range(2)]
        T2 = E(nc.sbuf_tensor("s1_t2", [128, DIN], BF16))
        T3 = [E(nc.sbuf_tensor(f"s1_t3{i}", [128, 512], BF16)) for i in range(2)]
        P = E(nc.sbuf_tensor("s1_P", [128, DIN], F32))
        btok = [E(nc.sbuf_tensor(f"s1_btok{i}", [128, G * 128], BF16)) for i in range(2)]
        e64d = [E(nc.sbuf_tensor(f"s1_e64d{i}", [128, 1024], BF16)) for i in range(2)]
        tsb = E(nc.sbuf_tensor("s1_tsb", [128, 512], F32))
        pfx = E(nc.sbuf_tensor("s1_pfx", [128, NHBC * 3], F32))
        cat6 = E(nc.sbuf_tensor("s1_cat6", [128, NHBC * 6], F32))
        tp6t = [E(nc.sbuf_tensor(f"s1_tp{i}", [128, NHBC * 3], F32))
                for i in range(2)]
        tp6 = [t_.ap().rearrange("p (c s) -> p c s", s=3) for t_ in tp6t]
        p3t = E(nc.sbuf_tensor("s1_p3t", [128, NHBC * 3], F32))
        dtT = [E(nc.sbuf_tensor(f"s1_dtT{i}", [128, 128], BF16)) for i in range(2)]
        wT = [E(nc.sbuf_tensor(f"s1_wT{i}", [128, 128], BF16)) for i in range(2)]
        psS = [E(nc.psum_tensor(f"s1_psS{i}", [128, 512], F32)) for i in range(2)]
        psD = [E(nc.psum_tensor(f"s1_psD{i}", [128, 512], F32)) for i in range(2)]
        sld = g_["SEM"]("s1_sld")
        spfx = g_["SEM"]("s1_spfx")
        stpy = g_["SEM"]("s1_stpy")
        stpd = g_["SEM"]("s1_stpd")
        stpb = g_["SEM"]("s1_stpb")
        smm = g_["SEM"]("s1_smm")
        sev = g_["SEM"]("s1_sev")
        spl = g_["SEM"]("s1_spl")
        shd = g_["SEM"]("s1_shd")
        spt = g_["SEM"]("s1_spt")
        spz = g_["SEM"]("s1_spz")
        sc2 = g_["SEM"]("s1_sc2")

        @blk.sync
        def _(sy):
            sy.wait_ge(g_["s_cc1"], 1)
            sy.dma_start(out=pfx[:, :], in_=g_["cc1_out"]).then_inc(spfx, 16)
            sy.wait_ge(spt, 3)
            sy.dma_start(
                out=g_["hid_d"].rearrange("(c p) t -> p c t", p=128)[:, :, 0:3],
                in_=g_["p_patch"].ap().rearrange("p (c l) -> p c l", c=64)
            ).then_inc(spt, 16)
            for k in range(NCH):
                if k == 0:
                    sy.wait_ge(spt, 19)
                if k >= 2:
                    sy.wait_ge(stpy, 16 * (k - 1))
                sy.dma_start(
                    out=T0[k % 2].ap().rearrange("p (c l) -> p c l", c=64),
                    in_=g_["hid_d"].rearrange("(c p) t -> p c t", p=128)
                    [:, :, k * 128:(k + 1) * 128]).then_inc(sld, 16)
                if k >= 2:
                    sy.wait_ge(sev, 56 * (k - 1))
                sy.wait_ge(sld, 16 * (k + 1))
                sy.dma_start(
                    out=T1[k % 2].ap().rearrange("p (c w) -> p c w", w=128),
                    in_=T0[k % 2].ap().rearrange("p (c w) -> p c w", w=128),
                    transpose=True).then_inc(stpy, 16)
                sy.dma_start(out=dtT[k % 2][:, :],
                             in_=g_["p_dtbf"][:, k * 128:(k + 1) * 128],
                             transpose=True).then_inc(stpd, 16)
                sy.dma_start(out=wT[k % 2][:, :],
                             in_=g_["p_dwbf"][:, k * 128:(k + 1) * 128],
                             transpose=True).then_inc(stpd, 16)
                if k >= 2:
                    sy.wait_ge(smm, 32 * (k - 1))
                for gi in range(G):
                    sy.dma_start(
                        out=btok[k % 2][:, gi * 128:(gi + 1) * 128],
                        in_=g_["p_bc"][:, gi * TOK + k * 128:gi * TOK + (k + 1) * 128],
                        transpose=True).then_inc(stpb, 16)
            sy.wait_ge(shd, 16 * NCH)
            sy.wait_ge(sev, 56 * NCH + 1)
            sy.dma_start(out=g_["cc2_in"], in_=P[:, :]).then_inc(sc2, 16)

        @blk.scalar
        def _(sc):
            sc = Trk(sc, g_["CHS"] if CHAIN else None)
            sc.wait_ge(spt, 2)
            sc.activation(g_["p_patch"][:, :], p3t[:, 0:192], AF.Silu)
            for bcc in range(16):
                sc.activation(g_["p_bc"][:, bcc * TOK:bcc * TOK + 3],
                              p3t[:, (64 + bcc) * 3:(64 + bcc) * 3 + 3], AF.Silu)
            sc.mi(spt, 1)


        @blk.vector
        def _(v):
            v = Trk(v, g_["CHV"] if CHAIN else None)
            v.memset(P[:, :], 0.0).then_inc(spz, 1)
            v.wait_ge(spfx, 16)
            v.tensor_scalar(pfx[:, :], pfx[:, :], g_["p_maskb"][:, :], None,
                            ALU.mult).then_inc(spt, 1)
            c6v = cat6.ap().rearrange("p (c s) -> p c s", s=6)
            v.tensor_copy(c6v[:, :, 0:3],
                          pfx[:, :].rearrange("p (c s) -> p c s", s=3))
            v.tensor_copy(c6v[:, :, 3:6],
                          g_["p_pre3"][:, :].rearrange("p (c s) -> p c s", s=3))
            wv = g_["p_conv_w"][:, :].rearrange("p (c t) -> p c t", t=4)
            p3v = p3t[:, :].rearrange("p (c s) -> p c s", s=3)
            v.tensor_tensor(p3v, c6v[:, :, 0:3],
                            wv[:, :, 0:1].broadcast_to([128, NHBC, 3]),
                            ALU.mult)
            for i in (1, 2, 3):
                v.tensor_tensor(tp6[i % 2], c6v[:, :, i:i + 3],
                                wv[:, :, i:i + 1].broadcast_to([128, NHBC, 3]),
                                ALU.mult)
                v.tensor_tensor(p3v, p3v, tp6[i % 2], ALU.add)
            v.tensor_tensor(p3v, p3v,
                            g_["p_conv_b"][:, :, None].broadcast_to(
                                [128, NHBC, 3]), ALU.add)
            v.mi(spt, 1)
            T2h = T2.ap().rearrange("p (h w) -> p h w", w=64)
            for k in range(NCH):
                v.wait_ge(stpy, 16 * (k + 1))
                v.wait_ge(stpd, 32 * (k + 1))
                v.wait_ge(shd, 16 * k)
                T1h = T1[k % 2].ap().rearrange("p (h w) -> p h w", w=64)
                for gi in range(G):
                    q = 8 * k + gi
                    if q >= 2:
                        pk, pgi = (q - 2) // 8, (q - 2) % 8
                        v.wait_ge(smm, mm_D(pk, pgi, 1))
                    v.tensor_scalar(e64d[q % 2][:, :],
                                    g_["p_e64"][:, gi * 1024:(gi + 1) * 1024],
                                    g_["p_dcol"][:, k:k + 1], None, ALU.mult
                                    ).then_inc(sev, 1)
                    for hf in range(2):
                        q2 = 2 * q + hf
                        h0 = gi * 16 + hf * 8
                        sl = slice(gi * 1024 + hf * 512,
                                   gi * 1024 + (hf + 1) * 512)
                        v.tensor_tensor(T2h[:, h0:h0 + 8, :],
                                        T1h[:, h0:h0 + 8, :],
                                        dtT[k % 2][:, h0:h0 + 8][:, :, None]
                                        .broadcast_to([128, 8, 64]), ALU.mult
                                        ).then_inc(sev, 1)
                        if q2 >= 2:
                            v.wait_ge(smm, mm_S(*unq2(q2 - 2)))
                        v.tensor_tensor(T3[q2 % 2].ap().rearrange(
                                            "p (h w) -> p h w", w=64),
                                        T1h[:, h0:h0 + 8, :],
                                        wT[k % 2][:, h0:h0 + 8][:, :, None]
                                        .broadcast_to([128, 8, 64]), ALU.mult
                                        ).then_inc(sev, 1)
                        v.wait_ge(smm, mm_D(k, gi, hf))
                        if gi == 0 and hf == 0:
                            v.wait_ge(spl, 16 * (k + 1))
                        v.tensor_tensor(tsb[:, :], psD[q2 % 2][:, :],
                                        P[:, sl], ALU.mult)
                        v.tensor_tensor(P[:, sl], tsb[:, :], psS[q2 % 2][:, :],
                                        ALU.add).then_inc(sev, 1)
            v.tensor_scalar(P[:, :], P[:, :], g_["p_masks"][:, :], None,
                            ALU.mult).then_inc(sev, 1)

        @blk.tensor
        def _(t):
            t = Trk(t)
            for k in range(NCH):
                for gi in range(G):
                    q = 8 * k + gi
                    if gi == 0:
                        t.wait_ge(stpy, 16 * (k + 1))
                        t.wait_ge(stpb, 16 * 8 * (k + 1))
                    t.wait_ge(sev, sev_e64d(k, gi))
                    for hf in range(2):
                        q2 = 2 * q + hf
                        if q2 >= 2:
                            t.wait_ge(sev, sev_rec(*unq2(q2 - 2)))
                        t.wait_ge(sev, sev_hdw(k, gi, hf))
                        bt = btok[k % 2][:, gi * 128:(gi + 1) * 128]
                        t.matmul(psS[q2 % 2][:, :], bt, T3[q2 % 2][:, :]
                                 ).then_inc(smm, 1)
                        t.matmul(psD[q2 % 2][:, :], g_["p_onessq"][:, :],
                                 e64d[q % 2][:, hf * 512:(hf + 1) * 512]
                                 ).then_inc(smm, 1)

        @blk.gpsimd
        def _(g):
            for k in range(NCH):
                if k == 0:
                    g.wait_ge(spz, 1)
                else:
                    g.wait_ge(sev, 56 * k)
                g.dma_start(out=g_["prev_d"][k * 128:(k + 1) * 128, :],
                            in_=P[:, :]).then_inc(spl, 16)
                g.wait_ge(sev, sev_hd(k, G - 1, 1))
                g.dma_start(out=g_["hd_d"][k * 128:(k + 1) * 128, :],
                            in_=T2[:, :]).then_inc(shd, 16)
            g.wait_ge(sc2, 16)
            g.collective_compute("AllReduce", ALU.add, replica_groups=RG,
                                 ins=[g_["cc2_in"].opt()],
                                 outs=[g_["cc2_out"].opt()]).then_inc(g_["s_cc2"], 1)


def _p6_sweep2(nc, env):
    g_ = env

    mmv = {}
    n = 0

    def ev(tag):
        nonlocal_ns[0] += 1
        mmv[tag] = nonlocal_ns[0]
    nonlocal_ns = [0]
    ev(("cdexp",))
    for k in range(NCH):
        for hg in range(16):
            gi, hh = hg // 2, hg % 2
            if hg == 0:
                ev(("cb", k, 0))
                ev(("st", k, 0))
            if hg < 15:
                if hh == 1:
                    ev(("cb", k, gi + 1))
                ev(("st", k, hg + 1))
            ev(("u", k, hg))
            ev(("y", k, hg))
        ev(("ep1", k))
        ev(("ep2", k))

    def ev_pt(k):
        return 44 * k + 2

    def ev_mtm(k, gi):
        return 44 * k + 3 + 5 * gi

    def ev_mtl(k, hg):
        return 44 * k + 4 + 5 * (hg // 2) + 2 * (hg % 2)

    def ev_ab(k, hg):
        return 44 * k + 5 + 5 * (hg // 2) + 2 * (hg % 2)

    def ev_dres(k):
        return 44 * k + 43

    def ev_y3(k):
        return 44 * k + 44

    def ev_yn(k):
        return 44 * k + 45

    def ac_le(k, hg):
        return 2 + 19 * k + hg

    def ac_silu(k):
        return 1 + 19 * k + 17

    def ac_ysq(k):
        return 1 + 19 * k + 18

    def ac_rows(k):
        return 1 + 19 * k + 19

    with nc.Block("p6") as blk, ExitStack() as ps:
        E = ps.enter_context
        T0 = E(nc.sbuf_tensor("s2_t0", [128, DIN], BF16))
        T1 = E(nc.sbuf_tensor("s2_t1", [128, DIN], BF16))
        Tg = E(nc.sbuf_tensor("s2_tg", [128, DIN], BF16))
        T5 = E(nc.sbuf_tensor("s2_t5", [128, DIN], BF16))
        YA = E(nc.sbuf_tensor("s2_ya", [128, DIN], F32))
        init = E(nc.sbuf_tensor("s2_init", [128, DIN], BF16))
        afhi = E(nc.sbuf_tensor("s2_afhi", [16, 1024], BF16))
        aflo = E(nc.sbuf_tensor("s2_aflo", [16, 1024], BF16))
        cdbf = E(nc.sbuf_tensor("s2_cdbf", [128, NCH], BF16))
        cdbf2 = E(nc.sbuf_tensor("s2_cdbf2", [128, 512], BF16))
        mtm = E(nc.sbuf_tensor("s2_mtm", [128, 128], BF16))
        rlf = E(nc.sbuf_tensor("s2_rlf", [128, 1024], F32))
        exps = [E(nc.sbuf_tensor(f"s2_exps{i}", [128, 1024], BF16)) for i in range(2)]
        mtl = [E(nc.sbuf_tensor(f"s2_mtl{i}", [128, 1024], BF16)) for i in range(2)]
        tpb = E(nc.sbuf_tensor("s2_tpb", [128, 1024], BF16))
        uexp = E(nc.sbuf_tensor("s2_uexp", [128, 512], BF16))
        tmp5 = E(nc.sbuf_tensor("s2_tmp5", [128, 512], BF16))
        rrow = E(nc.sbuf_tensor("s2_rrow", [1, 1024], F32))
        rrbf = E(nc.sbuf_tensor("s2_rrbf", [1, 1024], BF16))
        psS = [E(nc.psum_tensor(f"s2_pss{i}", [128, 1024], F32)) for i in range(2)]
        psMT = E(nc.psum_tensor("s2_psmt", [128, 128], F32))
        psU = E(nc.psum_tensor("s2_psu", [128, 512], F32))
        psYd = E(nc.psum_tensor("s2_psyd", [128, 512], F32))
        psYo = E(nc.psum_tensor("s2_psyo", [128, 512], F32))
        sld = g_["SEM"]("s2_sld")
        smm = g_["SEM"]("s2_smm")
        sev = g_["SEM"]("s2_sev")
        sac = g_["SEM"]("s2_sac")
        sst = g_["SEM"]("s2_sst")
        sini = g_["SEM"]("s2_sini")

        @blk.gpsimd
        def _(g):
            g.wait_ge(g_["s_cc2"], 1)
            g.dma_start(out=init[:, :], in_=g_["cc2_out"]).then_inc(sini, 16)

        @blk.sync
        def _(sy):
            for k in range(NCH):
                if k > 0:
                    sy.wait_ge(sev, ev_dres(k - 1))
                sy.dma_start(
                    out=T0.ap().rearrange("p (c l) -> p c l", c=64),
                    in_=g_["hid_d"].rearrange("(c p) t -> p c t", p=128)
                    [:, :, k * 128:(k + 1) * 128]).then_inc(sld, 16)
                if k > 0:
                    sy.wait_ge(smm, mmv[("y", k - 1, 15)])
                    sy.wait_ge(sst, 16 * k)
                sy.dma_start(out=T1[:, :], in_=g_["hd_d"][k * 128:(k + 1) * 128, :]
                             ).then_inc(sld, 16)
                sy.dma_start(out=T5[:, :], in_=g_["prev_d"][k * 128:(k + 1) * 128, :]
                             ).then_inc(sld, 16)
                if k > 0:
                    sy.wait_ge(smm, mmv[("st", k - 1, 15)])
                sy.dma_start(out=afhi[:, :],
                             in_=bass.AP(g_["ahi_d"].tensor, k * 128,
                                         [[8 * TOK, 16], [TOK, 8], [1, 128]])
                             ).then_inc(sld, 16)
                sy.dma_start(out=aflo[:, :],
                             in_=bass.AP(g_["alo_d"].tensor, k * 128,
                                         [[8 * TOK, 16], [TOK, 8], [1, 128]])
                             ).then_inc(sld, 16)
                if k > 0:
                    sy.wait_ge(smm, mmv[("ep1", k - 1)])
                sy.dma_start(
                    out=Tg.ap().rearrange("p (c l) -> p c l", c=64),
                    in_=g_["gate_d"].rearrange("(c p) t -> p c t", p=128)
                    [:, :, k * 128:(k + 1) * 128]).then_inc(sld, 16)
                sy.wait_ge(sev, ev_yn(k))
                sy.dma_start(
                    out=g_["ynorm_d"].rearrange("(c p) t -> p c t", p=128)
                    [:, :, k * 128:(k + 1) * 128],
                    in_=T1.ap().rearrange("p (c l) -> p c l", c=64)
                ).then_inc(sst, 16)
            sy.wait_ge(sst, 16 * NCH)

        @blk.scalar
        def _(sc):
            sc = Trk(sc, g_["CHS"] if CHAIN else None)
            sc.activation(cdbf[:, :], g_["p_cdcol"][:, :], AF.Copy).then_inc(sac, 1)
            for k in range(NCH):
                for hg in range(16):
                    sc.wait_ge(smm, mmv[("st", k, hg)])
                    if 16 * k + hg > 0:
                        phg = 16 * k + hg - 1
                        sc.wait_ge(sev, ev_mtl(phg // 16, phg % 16))
                    sc.activation(rlf[:, :], psS[hg % 2][:, :], AF.Relu,
                                  scale=-1.0)
                    sc.activation(exps[hg % 2][:, :], rlf[:, :], AF.Exp,
                                  scale=-1.0).then_inc(sac, 1)
                sc.wait_ge(sev, ev_dres(k))
                sc.activation(Tg[:, :], Tg[:, :], AF.Silu).then_inc(sac, 1)
                sc.wait_ge(sev, ev_y3(k))
                sc.activation(Tg[:, :], YA[:, :], AF.Square).then_inc(sac, 1)
                sc.wait_ge(smm, mmv[("ep1", k)])
                sc.activation(rrow[0:1, 0:512], psYd[0:1, 0:512], AF.Ln,
                              scale=1.0 / 1024, bias=g_["p_eps"][0:1, :])
                sc.activation(rrow[0:1, 512:1024], psYo[0:1, 0:512], AF.Ln,
                              scale=1.0 / 1024, bias=g_["p_eps"][0:1, :])
                sc.activation(rrow[:, :], rrow[:, :], AF.Exp, scale=-0.5)
                sc.activation(rrbf[:, :], rrow[:, :], AF.Copy).then_inc(sac, 1)

        @blk.vector
        def _(v):
            v = Trk(v, g_["CHV"] if CHAIN else None)
            v.wait_ge(smm, 1)
            v.tensor_copy(cdbf2[:, :], psS[0][:, 0:512]).then_inc(sev, 1)
            v.wait_ge(sini, 16)
            v.tensor_scalar(init[:, :], init[:, :], g_["p_maskb"][:, :], None,
                            ALU.mult)
            cdv = cdbf2.ap().rearrange("p (r s) -> p r s", s=NCH)
            tp3 = tpb.ap().rearrange("p (r w) -> p r w", w=128)
            in3 = init.ap().rearrange("p (r w) -> p r w", w=128)
            for k in range(NCH):
                v.wait_ge(sld, 16 * 6 * (k + 1))
                if k > 0:
                    v.wait_ge(smm, mmv[("y", k - 1, 15)])
                for qq in range(17):
                    if qq < 16:
                        v.tensor_tensor(
                            tp3[:, (qq % 2) * 4:(qq % 2) * 4 + 4, :],
                            in3[:, qq * 4:(qq + 1) * 4, :],
                            cdv[:, qq * 4:(qq + 1) * 4, k:k + 1]
                            .broadcast_to([128, 4, 128]), ALU.mult)
                    if qq > 0:
                        qs = slice((qq - 1) * 512, qq * 512)
                        v.tensor_tensor(T5[:, qs], T5[:, qs],
                                        tpb[:, ((qq - 1) % 2) * 512:
                                            ((qq - 1) % 2) * 512 + 512],
                                        ALU.add)
                v.mi(sev, 1)
                for gi in range(G):
                    v.wait_ge(smm, mmv[("cb", k, gi)])
                    v.tensor_tensor(mtm[:, :], psMT[:, :], g_["p_mask01"][:, :],
                                    ALU.mult).then_inc(sev, 1)
                    for hh in range(2):
                        hg = 2 * gi + hh
                        v.wait_ge(sac, ac_le(k, hg))
                        v.tensor_tensor(
                            mtl[hg % 2].ap().rearrange("p (a b) -> p a b", a=8),
                            exps[hg % 2].ap().rearrange("p (a b) -> p a b", a=8),
                            mtm[:, None, :].broadcast_to([128, 8, 128]),
                            ALU.mult).then_inc(sev, 1)
                        v.wait_ge(smm, mmv[("u", k, hg)])
                        v.tensor_copy(uexp[:, :], psU[:, :])
                        v.wait_ge(smm, mmv[("y", k, hg)])
                        v.tensor_tensor(tmp5[:, :], psYo[:, :], uexp[:, :],
                                        ALU.mult)
                        ysl = slice(hg * 512, (hg + 1) * 512)
                        v.tensor_tensor(YA[:, ysl], tmp5[:, :], psYd[:, :],
                                        ALU.add).then_inc(sev, 1)
                for pr in range(NPAIR):
                    sl = slice(pr * 128, (pr + 1) * 128)
                    v.scalar_tensor_tensor(YA[:, sl], T0[:, sl],
                                           g_["p_dp"][:, pr:pr + 1],
                                           YA[:, sl], ALU.mult, ALU.add)
                v.mi(sev, 1)
                v.wait_ge(sac, ac_silu(k))
                v.tensor_tensor(YA[:, :], YA[:, :], Tg[:, :], ALU.mult
                                ).then_inc(sev, 1)
                v.wait_ge(smm, mmv[("ep2", k)])
                for pr in range(NPAIR):
                    sl = slice(pr * 128, (pr + 1) * 128)
                    gidx = pr // 8
                    bsrc = psYd[:, (gidx % 4) * 128:(gidx % 4 + 1) * 128] \
                        if gidx < 4 else \
                        psYo[:, (gidx - 4) * 128:(gidx - 3) * 128]
                    v.scalar_tensor_tensor(T1[:, sl], YA[:, sl],
                                           g_["p_gnw"][:, pr:pr + 1],
                                           bsrc, ALU.mult, ALU.mult)
                v.mi(sev, 1)

        @blk.tensor
        def _(t):
            t = Trk(t)
            t.wait_ge(sac, 1)
            for pr in range(NPAIR):
                t.matmul(psS[0][:, pr * 8:(pr + 1) * 8],
                         g_["p_e64"][:, pr * 128:(pr + 1) * 128], cdbf[:, :])
            t.mi(smm, 1)

            def emit_cb(k, gi):
                if (k, gi) != (0, 0):
                    pk, pgi = (k, gi - 1) if gi > 0 else (k - 1, G - 1)
                    t.wait_ge(sev, ev_mtm(pk, pgi))
                bsl = slice(gi * TOK + k * 128, gi * TOK + (k + 1) * 128)
                csl = slice((8 + gi) * TOK + k * 128,
                            (8 + gi) * TOK + (k + 1) * 128)
                t.matmul(psMT[:, :], g_["p_bc"][:, bsl], g_["p_bc"][:, csl]
                         ).then_inc(smm, 1)

            def emit_st(k, hg):
                gi, hh = hg // 2, hg % 2
                khg = 16 * k + hg
                if khg >= 2:
                    phg = khg - 2
                    t.wait_ge(sac, ac_le(phg // 16, phg % 16))
                if hg == 0 and k == 0:
                    t.wait_ge(sev, 1)
                h0 = (gi * 16 + hh * 8) * 128
                nh = g_["p_nahi"][:, k * 128:(k + 1) * 128]
                nl = g_["p_nalo"][:, k * 128:(k + 1) * 128]
                hrow = 2 * gi + hh
                oh = g_["p_onehot"][:, hrow * 128:(hrow + 1) * 128]
                for reg in range(2):
                    rs = slice(reg * 512, (reg + 1) * 512)
                    habs = gi * 16 + hh * 8 + reg * 4
                    idb = g_["p_idbf"][:, habs:habs + 4][:, :, None] \
                        .broadcast_to([128, 4, 128])
                    out3 = psS[hg % 2].ap().rearrange(
                        "p (a b) -> p a b", a=2)[:, reg:reg + 1, :] \
                        .rearrange("p a (c d) -> p (a c) d", c=4)
                    t.matmul(psS[hg % 2][:, rs], oh, afhi[:, rs],
                             start=True, stop=False)
                    t.matmul(psS[hg % 2][:, rs], oh, aflo[:, rs],
                             start=False, stop=False)
                    t.matmul(out3, nh, idb, start=False, stop=False)
                    t.matmul(out3, nl, idb, start=False, stop=True)
                t.mi(smm, 1)

            def emit_u(k, hg):
                gi, hh = hg // 2, hg % 2
                khg = 16 * k + hg
                if khg >= 1:
                    phg = khg - 1
                    t.wait_ge(sev, ev_ab(phg // 16, phg % 16))
                for pp in range(4):
                    pr = gi * 8 + hh * 4 + pp
                    t.matmul(psU[:, pp * 128:(pp + 1) * 128],
                             g_["p_e64"][:, pr * 128:(pr + 1) * 128],
                             g_["p_ubf"][:, k * 128:(k + 1) * 128])
                t.mi(smm, 1)

            def emit_y(k, hg):
                gi, hh = hg // 2, hg % 2
                t.wait_ge(sev, ev_mtl(k, hg))
                t.wait_ge(sev, ev_pt(k))
                csl = slice((8 + gi) * TOK + k * 128,
                            (8 + gi) * TOK + (k + 1) * 128)
                for pp in range(4):
                    for par in range(2):
                        m = 2 * pp + par
                        hgl = gi * 16 + hh * 8 + m
                        t.matmul(psYd[par * 64:(par + 1) * 64,
                                      pp * 128:(pp + 1) * 128],
                                 T1[:, hgl * 64:(hgl + 1) * 64],
                                 mtl[hg % 2][:, m * 128:(m + 1) * 128],
                                 tile_position=(0, par * 64))
                for pp in range(4):
                    for par in range(2):
                        m = 2 * pp + par
                        hgl = gi * 16 + hh * 8 + m
                        t.matmul(psYo[par * 64:(par + 1) * 64,
                                      pp * 128:(pp + 1) * 128],
                                 T5[:, hgl * 64:(hgl + 1) * 64],
                                 g_["p_bc"][:, csl],
                                 tile_position=(0, par * 64))
                t.mi(smm, 1)

            for k in range(NCH):
                t.wait_ge(sld, 16 * (6 * k + 6))
                for hg in range(16):
                    if hg == 0:
                        emit_cb(k, 0)
                        emit_st(k, 0)
                    if hg < 15:
                        if hg % 2 == 1:
                            emit_cb(k, hg // 2 + 1)
                        emit_st(k, hg + 1)
                    emit_u(k, hg)
                    emit_y(k, hg)
                t.wait_ge(sac, ac_ysq(k))
                for pr in range(NPAIR):
                    gidx = pr // 8
                    dst = psYd[0:1, (gidx % 4) * 128:(gidx % 4) * 128 + 128] \
                        if gidx < 4 else \
                        psYo[0:1, (gidx - 4) * 128:(gidx - 4) * 128 + 128]
                    t.matmul(dst, g_["p_onescol"][:, :],
                             Tg[:, pr * 128:(pr + 1) * 128],
                             start=(pr % 8 == 0), stop=(pr % 8 == 7))
                t.mi(smm, 1)
                t.wait_ge(sac, ac_rows(k))
                for gidx in range(G):
                    dst = psYd[:, (gidx % 4) * 128:(gidx % 4) * 128 + 128] \
                        if gidx < 4 else \
                        psYo[:, (gidx - 4) * 128:(gidx - 4) * 128 + 128]
                    t.matmul(dst, g_["p_ones1"][:, :],
                             rrbf[:, gidx * 128:(gidx + 1) * 128])
                t.mi(smm, 1)


def _p7_outproj(nc, env):
    g_ = env
    p_x1 = g_["p_x1"]
    NOCB = 8

    def it_(tt, ocb):
        return tt * NOCB + ocb

    def mm_piece(n, piece):
        return 8 * n + piece + 1

    def mm_mm(tt, ocb):
        return 8 * it_(tt, ocb) + 4

    def mm_stats(tt, ocb, oc):
        return 8 * it_(tt, ocb) + 5 + oc

    MM_BCAST = 129

    def ev_evac(tt, ocb, oc):
        return 4 * it_(tt, ocb) + oc + 1

    EV_MV, EV_VAR, EV_B = 65, 66, 67

    def ac_sq(tt, ocb, oc):
        return 4 * it_(tt, ocb) + oc + 1

    AC_M2, AC_RSTD = 65, 66

    ps7 = ExitStack()
    E7 = ps7.enter_context
    mrow = E7(nc.sbuf_tensor("p7_mrow", [1, TOK], F32))
    vrow = E7(nc.sbuf_tensor("p7_vrow", [1, TOK], F32))
    m2r = E7(nc.sbuf_tensor("p7_m2r", [1, TOK], F32))
    rstdr = E7(nc.sbuf_tensor("p7_rstdr", [1, TOK], F32))
    arow = E7(nc.sbuf_tensor("p7_arow", [1, TOK], BF16))
    browf = E7(nc.sbuf_tensor("p7_browf", [1, TOK], F32))
    brow = E7(nc.sbuf_tensor("p7_brow", [1, TOK], BF16))
    smm = g_["SEM"]("p7_smm")
    sev = g_["SEM"]("p7_sev")
    sac = g_["SEM"]("p7_sac")
    with nc.Block("p7") as blk, ExitStack() as ps:
        E = ps.enter_context
        ynh = E(nc.sbuf_tensor("p7_ynh", [128, NPAIR * 512], BF16))
        wst = [E(nc.sbuf_tensor(f"p7_w{i}", [128, 16 * 512], BF16)) for i in range(2)]
        xbl = E(nc.sbuf_tensor("p7_xbl", [128, 4 * 512], F32))
        rtmp = E(nc.sbuf_tensor("p7_rtmp", [128, 512], F32))
        sqb = E(nc.sbuf_tensor("p7_sqb", [128, 512], BF16))
        ps4 = [E(nc.psum_tensor("p7_ps0", [128, 2048], F32))] * 2
        psM = E(nc.psum_tensor("p7_psM", [1, TOK], F32))
        psQ = E(nc.psum_tensor("p7_psQ", [1, TOK], F32))
        syl = g_["SEM"]("p7_syl")
        sx = g_["SEM"]("p7_sx")
        sw2 = [g_["SEM"]("p7_sw0"), g_["SEM"]("p7_sw1")]

        @blk.sync
        def _(sy):
            for tt in range(2):
                if tt > 0:
                    sy.wait_ge(smm, mm_mm(0, NOCB - 1))
                sy.dma_start(
                    out=ynh.ap().rearrange("p (c t) -> p c t", c=NPAIR),
                    in_=g_["ynorm_d"].rearrange("(c p) t -> p c t", p=128)
                    [:, :, tt * 512:(tt + 1) * 512]).then_inc(syl, 16)
                for ocb in range(NOCB):
                    n = it_(tt, ocb)
                    if n >= 1:
                        sy.wait_ge(sev, 4 * n)
                    sy.dma_start(
                        out=xbl.ap().rearrange("p (c t) -> p c t", c=4),
                        in_=g_["x_fm"].rearrange("(c p) t -> p c t", p=128)
                        [:, ocb * 4:(ocb + 1) * 4, tt * 512:(tt + 1) * 512]
                    ).then_inc(sx, 16)

        @blk.gpsimd
        def _(g):
            for tt in range(2):
                for ocb in range(NOCB):
                    for piece in range(4):
                        n = it_(tt, ocb) * 4 + piece
                        if n >= 2:
                            g.wait_ge(smm, mm_piece((n - 2) // 4, (n - 2) % 4))
                        r0 = piece * 16 * 128
                        g.dma_start(
                            out=wst[n % 2].ap().rearrange("p (k c) -> p k c", k=16),
                            in_=g_["w_out_t"][r0:r0 + 16 * 128,
                                              ocb * 512:(ocb + 1) * 512]
                            .rearrange("(k p) c -> p k c", p=128)
                        ).then_inc(sw2[n % 2], 16)

        @blk.tensor
        def _(t):
            t = Trk(t)
            for tt in range(2):
                for ocb in range(NOCB):
                    n = it_(tt, ocb)
                    t.wait_ge(syl, 16 * (tt + 1))
                    if n >= 1:
                        t.wait_ge(sev, 4 * n)
                    pp = ps4[n % 2]
                    for piece in range(4):
                        ns = 4 * n + piece
                        t.wait_ge(sw2[ns % 2], 16 * (ns // 2 + 1))
                        for kk in range(16):
                            kc = piece * 16 + kk
                            for oc in range(4):
                                t.matmul(pp[:, oc * 512:(oc + 1) * 512],
                                         wst[ns % 2]
                                         [:, kk * 512 + oc * 128:kk * 512 + (oc + 1) * 128],
                                         ynh[:, kc * 512:(kc + 1) * 512],
                                         start=(kc == 0), stop=(kc == 63))
                        t.mi(smm, 1)
                    for oc in range(4):
                        t.wait_ge(sev, ev_evac(tt, ocb, oc))
                        t.wait_ge(sac, ac_sq(tt, ocb, oc))
                        gcol = slice(tt * 512, (tt + 1) * 512)
                        first = (ocb == 0 and oc == 0)
                        last = (ocb == NOCB - 1 and oc == 3)
                        occ = ocb * 4 + oc
                        t.matmul(psM[0:1, gcol], g_["p_onescol"][:, :],
                                 p_x1[:, occ * TOK + tt * 512:occ * TOK + (tt + 1) * 512],
                                 start=first, stop=last)
                        t.matmul(psQ[0:1, gcol], g_["p_onescol"][:, :],
                                 sqb[:, :], start=first, stop=last).then_inc(smm, 1)

        @blk.vector
        def _(v):
            v = Trk(v, g_["CHV"] if CHAIN else None)
            for tt in range(2):
                for ocb in range(NOCB):
                    n = it_(tt, ocb)
                    v.wait_ge(smm, mm_mm(tt, ocb))
                    v.wait_ge(sx, 16 * (n + 1))
                    pp = ps4[n % 2]
                    for oc in range(4):
                        occ = ocb * 4 + oc
                        pn = 4 * n + oc
                        if pn > 0:
                            v.wait_ge(sac, pn)
                        v.scalar_tensor_tensor(rtmp[:, :],
                                               xbl[:, oc * 512:(oc + 1) * 512],
                                               2.0, pp[:, oc * 512:(oc + 1) * 512],
                                               ALU.mult, ALU.add)
                        v.tensor_copy(
                            p_x1[:, occ * TOK + tt * 512:occ * TOK + (tt + 1) * 512],
                            rtmp[:, :]).then_inc(sev, 1)
            v.wait_ge(smm, mm_stats(1, NOCB - 1, 3))
            v.tensor_scalar(mrow[:, :], psM[0:1, :], 1.0 / D, None, ALU.mult)
            v.tensor_scalar(vrow[:, :], psQ[0:1, :], 1.0 / D, None, ALU.mult
                            ).then_inc(sev, 1)

        @blk.scalar
        def _(sc):
            sc = Trk(sc, g_["CHS"] if CHAIN else None)
            for tt in range(2):
                for ocb in range(NOCB):
                    for oc in range(4):
                        sc.wait_ge(sev, ev_evac(tt, ocb, oc))
                        pn = 4 * it_(tt, ocb) + oc
                        if pn > 0:
                            ptt, pocb, poc = pn - 1, None, None
                            pi, poc = divmod(pn - 1, 4)
                            sc.wait_ge(smm, mm_stats(pi // NOCB, pi % NOCB, poc))
                        sc.activation(sqb[:, :], rtmp[:, :], AF.Square
                                      ).then_inc(sac, 1)
            sc.wait_ge(sev, EV_MV)
            sc.activation(m2r[:, :], mrow[:, :], AF.Square).then_inc(sac, 1)


    with nc.Block("p7b") as blk, ExitStack() as ps:
        E = ps.enter_context
        psAB = E(nc.psum_tensor("p7_psAB", [128, 2 * TOK], F32))
        ttmp = E(nc.sbuf_tensor("p7b_ttmp", [128, TOK], F32))

        @blk.vector
        def _(v):
            v = Trk(v, g_["CHV"] if CHAIN else None)
            v.wait_ge(sac, AC_M2)
            v.tensor_tensor(vrow[:, :], vrow[:, :], m2r[:, :], ALU.subtract
                            ).then_inc(sev, 1)
            v.wait_ge(sac, AC_RSTD)
            v.tensor_tensor(browf[:, :], mrow[:, :], rstdr[:, :], ALU.mult)
            v.tensor_scalar(brow[:, :], browf[:, :], -1.0, None, ALU.mult
                            ).then_inc(sev, 1)
            v.wait_ge(smm, MM_BCAST)
            for kc in range(KD):
                v.tensor_tensor(ttmp[:, :], p_x1[:, kc * TOK:(kc + 1) * TOK],
                                psAB[:, 0:TOK], ALU.mult)
                v.tensor_tensor(p_x1[:, kc * TOK:(kc + 1) * TOK], ttmp[:, :],
                                psAB[:, TOK:2 * TOK], ALU.add).then_inc(sev, 1)

        @blk.scalar
        def _(sc):
            sc = Trk(sc, g_["CHS"] if CHAIN else None)
            sc.wait_ge(sev, EV_VAR)
            sc.activation(rstdr[:, :], vrow[:, :], AF.Ln,
                          bias=g_["p_eps"][0:1, :])
            sc.activation(rstdr[:, :], rstdr[:, :], AF.Exp, scale=-0.5)
            sc.activation(arow[:, :], rstdr[:, :], AF.Copy).then_inc(sac, 1)

        @blk.tensor
        def _(t):
            t = Trk(t)
            t.wait_ge(sac, AC_RSTD)
            t.wait_ge(sev, EV_B)
            for r in range(2):
                t.matmul(psAB[:, r * 512:(r + 1) * 512], g_["p_ones1"][:, :],
                         arow[:, r * 512:(r + 1) * 512])
                t.matmul(psAB[:, TOK + r * 512:TOK + (r + 1) * 512],
                         g_["p_ones1"][:, :], brow[:, r * 512:(r + 1) * 512])
            t.mi(smm, 1)
    ps7.close()


def _p8_ffn(nc, env):
    g_ = env
    p_x1 = g_["p_x1"]

    def mm_is(kc):
        return 4 + kc

    def mm_fp(ocb, piece):
        return 36 + 6 * ocb + piece

    def mm_os(ocb):
        return 36 + 6 * ocb + 4

    def mm_stats(ocb):
        return 36 + 6 * ocb + 5

    MM_BCAST = 228

    def ev_xin(kc):
        return 6 + kc

    def ev_ose(ocb):
        return 38 + 2 * ocb

    def ev_o2(ocb):
        return 39 + 2 * ocb

    EV_MV, EV_VAR, EV_B = 102, 103, 104

    def ev_fin(kc):
        return 105 + kc

    def ac_sq(ocb):
        return 2 + ocb

    AC_M2, AC_RSTD = 34, 35

    with nc.Block("p8") as blk, ExitStack() as ps:
        E = ps.enter_context
        xin = E(nc.sbuf_tensor("p8_xin", [128, KD * TOK], BF16))
        gwbf = E(nc.sbuf_tensor("p8_gwbf", [128, KD * 8], BF16))
        icsbf = E(nc.sbuf_tensor("p8_icsbf", [8, D], BF16))
        ocsbf = E(nc.sbuf_tensor("p8_ocsbf", [8, D], BF16))
        lrow = E(nc.sbuf_tensor("p8_lrow", [8, TOK], F32))
        stok = E(nc.sbuf_tensor("p8_stok", [128, 64], F32))
        estok = E(nc.sbuf_tensor("p8_estok", [128, 64], F32))
        nmx = E(nc.sbuf_tensor("p8_nmx", [128, 8], F32))
        ssum = E(nc.sbuf_tensor("p8_ssum", [128, 8], F32))
        srec = E(nc.sbuf_tensor("p8_srec", [128, 8], F32))
        rtok = E(nc.sbuf_tensor("p8_rtok", [128, 64], F32))
        routg = E(nc.sbuf_tensor("p8_routg", [8, TOK], BF16))
        wstg = [E(nc.sbuf_tensor(f"p8_wf{i}", [128, 8 * 128], BF16)) for i in range(2)]
        wsgn = [E(nc.sbuf_tensor(f"p8_ws{i}", [128, 8 * 128], BF16)) for i in range(2)]
        osb = E(nc.sbuf_tensor("p8_osb", [128, TOK], F32))
        ftmp = E(nc.sbuf_tensor("p8_ftmp", [128, TOK], F32))
        o2b = [E(nc.sbuf_tensor(f"p8_o2{i}", [128, TOK], BF16)) for i in range(2)]
        sqf = E(nc.sbuf_tensor("p8_sqf", [128, TOK], BF16))
        fob = [E(nc.sbuf_tensor(f"p8_fo{i}", [128, TOK], BF16)) for i in range(2)]
        outf2 = E(nc.sbuf_tensor("p8_of1", [128, TOK], F32))
        outf = [osb, outf2]
        mrow = E(nc.sbuf_tensor("p8_mrow", [1, TOK], F32))
        vrow = E(nc.sbuf_tensor("p8_vrow", [1, TOK], F32))
        m2r = E(nc.sbuf_tensor("p8_m2r", [1, TOK], F32))
        rstdr = vrow
        arow = E(nc.sbuf_tensor("p8_arow", [1, TOK], BF16))
        browf = m2r
        brow = E(nc.sbuf_tensor("p8_brow", [1, TOK], BF16))
        PS1 = E(nc.psum_tensor("p8_PS1", [128, TOK], F32))
        PS2 = E(nc.psum_tensor("p8_PS2", [128, TOK], F32))
        PS3 = E(nc.psum_tensor("p8_PS3", [128, TOK], F32))
        PS4 = E(nc.psum_tensor("p8_PS4", [128, TOK], F32))
        sgld = g_["SEM"]("p8_sgld")
        ssw2 = [g_["SEM"]("p8_ssw0"), g_["SEM"]("p8_ssw1")]
        ssg = g_["SEM"]("p8_ssg")
        smm = g_["SEM"]("p8_smm")
        sev = g_["SEM"]("p8_sev")
        sac = g_["SEM"]("p8_sac")
        sld2 = [g_["SEM"]("p8_sld0"), g_["SEM"]("p8_sld1")]
        s2s2 = [g_["SEM"]("p8_s2sa"), g_["SEM"]("p8_s2sb")]
        sst2 = [g_["SEM"]("p8_ssta"), g_["SEM"]("p8_sstb")]

        @blk.gpsimd
        def _(g):
            g.dma_start(out=gwbf.ap().rearrange("p (k c) -> p k c", k=KD),
                        in_=g_["gw_t"].rearrange("(k p) c -> p k c", p=128)
                        ).then_inc(sgld, 16)
            g.dma_start(out=icsbf[:, :], in_=g_["ics_i"]).then_inc(sgld, 16)
            g.dma_start(out=ocsbf[:, :], in_=g_["ocs_i"]).then_inc(sgld, 16)
            for ocb in range(32):
                for piece in range(4):
                    n = 4 * ocb + piece
                    if n >= 2:
                        pocb, ppc = divmod(n - 2, 4)
                        g.wait_ge(smm, mm_fp(pocb, ppc))
                    r0 = piece * 8 * 128
                    g.dma_start(
                        out=wstg[n % 2].ap().rearrange("p (k c) -> p k c", k=8),
                        in_=g_["w_ffn_t"][r0:r0 + 8 * 128, ocb * 128:(ocb + 1) * 128]
                        .rearrange("(k p) c -> p k c", p=128)
                    ).then_inc(ssw2[n % 2], 16)

        @blk.scalar
        def _(sc):
            sc = Trk(sc, g_["CHS"] if CHAIN else None)
            sc.wait_ge(sev, 3)
            for tc in range(8):
                sc.activation(estok[:, tc * 8:(tc + 1) * 8],
                              stok[:, tc * 8:(tc + 1) * 8], AF.Exp,
                              bias=nmx[:, tc:tc + 1],
                              accum_out=ssum[:, tc:tc + 1])
            sc.mi(sac, 1)
            for ocb in range(32):
                for piece in range(4):
                    n = 4 * ocb + piece
                    sc.wait_ge(ssw2[n % 2], 16 * (n // 2 + 1))
                    if n >= 2:
                        pocb, ppc = divmod(n - 2, 4)
                        sc.wait_ge(smm, mm_fp(pocb, ppc))
                    sc.activation(wsgn[n % 2][:, :], wstg[n % 2][:, :],
                                  AF.Sign).then_inc(ssg, 1)
                sc.wait_ge(sev, ev_o2(ocb))
                if ocb > 0:
                    sc.wait_ge(smm, mm_stats(ocb - 1))
                sc.activation(sqf[:, :], o2b[ocb % 2][:, :], AF.Square
                              ).then_inc(sac, 1)
            sc.wait_ge(sev, EV_MV)
            sc.activation(m2r[:, :], mrow[:, :], AF.Square).then_inc(sac, 1)
            sc.wait_ge(sev, EV_VAR)
            sc.activation(rstdr[:, :], vrow[:, :], AF.Ln,
                          bias=g_["p_eps"][0:1, :])
            sc.activation(rstdr[:, :], rstdr[:, :], AF.Exp, scale=-0.5)
            sc.activation(arow[:, :], rstdr[:, :], AF.Copy).then_inc(sac, 1)

        @blk.tensor
        def _(t):
            t = Trk(t)
            t.wait_ge(sgld, 48)
            for kc in range(KD):
                t.matmul(PS1[0:8, 0:512], gwbf[:, kc * 8:(kc + 1) * 8],
                         p_x1[:, kc * TOK:kc * TOK + 512],
                         start=(kc == 0), stop=(kc == KD - 1))
                t.matmul(PS1[0:8, 512:1024], gwbf[:, kc * 8:(kc + 1) * 8],
                         p_x1[:, kc * TOK + 512:(kc + 1) * TOK],
                         start=(kc == 0), stop=(kc == KD - 1))
            t.mi(smm, 1)
            t.wait_ge(sev, 1)
            for tc in range(8):
                t.transpose(PS2[:, tc * 8:(tc + 1) * 8],
                            lrow[:, tc * 128:(tc + 1) * 128], g_["p_iden8"][:, :])
            t.mi(smm, 1)
            t.wait_ge(sev, 4)
            for tc in range(8):
                t.transpose(PS3[0:8, tc * 128:(tc + 1) * 128],
                            rtok[:, tc * 8:(tc + 1) * 8], g_["p_iden128"][:, :])
            t.mi(smm, 1)
            t.wait_ge(sev, 5)
            for kc in range(KD):
                if kc >= 1:
                    t.wait_ge(sev, ev_xin(kc - 1))
                t.matmul(PS4[:, 0:512], icsbf[:, kc * 128:(kc + 1) * 128],
                         routg[:, 0:512])
                t.matmul(PS4[:, 512:1024], icsbf[:, kc * 128:(kc + 1) * 128],
                         routg[:, 512:1024]).then_inc(smm, 1)
            for ocb in range(32):
                if ocb == 0:
                    t.wait_ge(sev, ev_xin(KD - 1))
                else:
                    t.wait_ge(sev, ev_o2(ocb - 1))
                for piece in range(4):
                    n = 4 * ocb + piece
                    t.wait_ge(ssg, n + 1)
                    for kk in range(8):
                        kc = piece * 8 + kk
                        for half in range(2):
                            t.matmul(
                                PS1[:, half * 512:(half + 1) * 512],
                                wsgn[n % 2][:, kk * 128:(kk + 1) * 128],
                                xin[:, kc * TOK + half * 512:kc * TOK + (half + 1) * 512],
                                start=(kc == 0), stop=(kc == KD - 1))
                    t.mi(smm, 1)
                occ = ocb
                for half in range(2):
                    t.matmul(PS2[:, half * 512:(half + 1) * 512],
                             ocsbf[:, occ * 128:(occ + 1) * 128],
                             routg[:, half * 512:(half + 1) * 512])
                t.mi(smm, 1)
                t.wait_ge(sev, ev_o2(ocb))
                t.wait_ge(sac, ac_sq(ocb))
                first, last = (ocb == 0), (ocb == 31)
                for half in range(2):
                    hc = slice(half * 512, (half + 1) * 512)
                    t.matmul(PS3[0:1, hc], g_["p_onescol"][:, :],
                             o2b[ocb % 2][:, hc], start=first, stop=last)
                    t.matmul(PS4[0:1, hc], g_["p_onescol"][:, :],
                             sqf[:, hc], start=first, stop=last)
                t.mi(smm, 1)
            t.wait_ge(sac, AC_RSTD)
            t.wait_ge(sev, EV_B)
            for r in range(2):
                t.matmul(PS3[:, r * 512:(r + 1) * 512], g_["p_ones1"][:, :],
                         arow[:, r * 512:(r + 1) * 512])
                t.matmul(PS4[:, r * 512:(r + 1) * 512], g_["p_ones1"][:, :],
                         brow[:, r * 512:(r + 1) * 512])
            t.mi(smm, 1)

        @blk.vector
        def _(v):
            v = Trk(v, g_["CHV"] if CHAIN else None)
            v.wait_ge(smm, 1)
            v.tensor_copy(lrow[:, :], PS1[0:8, :]).then_inc(sev, 1)
            v.wait_ge(smm, 2)
            v.tensor_copy(stok[:, :], PS2[:, 0:64]).then_inc(sev, 1)
            v.tensor_reduce(nmx.ap().rearrange("p (a b) -> p a b", b=1),
                            stok.ap().rearrange("p (a b) -> p a b", a=8),
                            mybir.AxisListType.X, ALU.max, negate=True
                            ).then_inc(sev, 1)
            v.wait_ge(sac, 1)
            v.reciprocal(srec[:, :], ssum[:, :])
            v.tensor_tensor(rtok.ap().rearrange("p (a b) -> p a b", a=8),
                            estok.ap().rearrange("p (a b) -> p a b", a=8),
                            srec[:, :, None].broadcast_to([128, 8, 8]),
                            ALU.mult).then_inc(sev, 1)
            v.wait_ge(smm, 3)
            v.tensor_copy(routg[:, :], PS3[0:8, :]).then_inc(sev, 1)
            for kc in range(KD):
                v.wait_ge(smm, mm_is(kc))
                v.tensor_tensor(xin[:, kc * TOK:(kc + 1) * TOK],
                                p_x1[:, kc * TOK:(kc + 1) * TOK],
                                PS4[:, :], ALU.mult).then_inc(sev, 1)
            for ocb in range(32):
                v.wait_ge(smm, mm_os(ocb))
                v.tensor_copy(osb[:, :], PS2[:, :]).then_inc(sev, 1)
                occ = ocb
                v.tensor_tensor(ftmp[:, :], PS1[:, :], osb[:, :], ALU.mult)
                if occ >= 2:
                    v.wait_ge(s2s2[occ % 2], 16 * (occ // 2))
                    v.wait_ge(smm, mm_stats(occ - 2))
                v.scalar_tensor_tensor(o2b[ocb % 2][:, :], ftmp[:, :],
                                       g_["p_ffnb"][:, occ:occ + 1],
                                       p_x1[:, occ * TOK:(occ + 1) * TOK],
                                       ALU.add, ALU.add).then_inc(sev, 1)
            v.wait_ge(smm, mm_stats(31))
            v.tensor_scalar(mrow[:, :], PS3[0:1, :], 1.0 / D, None, ALU.mult)
            v.tensor_scalar(vrow[:, :], PS4[0:1, :], 1.0 / D, None, ALU.mult
                            ).then_inc(sev, 1)
            v.wait_ge(sac, AC_M2)
            v.tensor_tensor(vrow[:, :], vrow[:, :], m2r[:, :], ALU.subtract
                            ).then_inc(sev, 1)
            v.wait_ge(sac, AC_RSTD)
            v.tensor_tensor(browf[:, :], mrow[:, :], rstdr[:, :], ALU.mult)
            v.tensor_scalar(brow[:, :], browf[:, :], -1.0, None, ALU.mult
                            ).then_inc(sev, 1)
            v.wait_ge(smm, MM_BCAST)
            for kc in range(KD):
                v.wait_ge(sld2[kc % 2], 16 * (kc // 2 + 1))
                if kc >= 2:
                    v.wait_ge(sst2[kc % 2], 16 * (kc // 2))
                v.tensor_tensor(ftmp[:, :], fob[kc % 2][:, :], PS3[:, :],
                                ALU.mult)
                v.tensor_tensor(outf[kc % 2][:, :], ftmp[:, :], PS4[:, :],
                                ALU.add).then_inc(sev, 1)

        @blk.sync
        def _(sy):
            for ocb in range(32):
                sy.wait_ge(sev, ev_o2(ocb))
                sy.dma_start(out=g_["out2_d"][ocb * 128:(ocb + 1) * 128, :],
                             in_=o2b[ocb % 2][:, :]).then_inc(s2s2[ocb % 2], 16)
            for kc in range(KD):
                sy.wait_ge(s2s2[kc % 2], 16 * (kc // 2 + 1))
                if kc >= 2:
                    sy.wait_ge(sev, ev_fin(kc - 2))
                sy.dma_start(out=fob[kc % 2][:, :],
                             in_=g_["out2_d"][kc * 128:(kc + 1) * 128, :]
                             ).then_inc(sld2[kc % 2], 16)
                if kc >= 1:
                    sy.wait_ge(sev, ev_fin(kc - 1))
                    sy.dma_start(out=g_["out_d"][(kc - 1) * 128:kc * 128, :],
                                 in_=outf[(kc - 1) % 2][:, :]
                                 ).then_inc(sst2[(kc - 1) % 2], 16)
            sy.wait_ge(sev, ev_fin(KD - 1))
            sy.dma_start(out=g_["out_d"][(KD - 1) * 128:KD * 128, :],
                         in_=outf[(KD - 1) % 2][:, :]
                         ).then_inc(sst2[(KD - 1) % 2], 16)
            sy.wait_ge(sst2[0], 16 * (KD // 2))
            sy.wait_ge(sst2[1], 16 * (KD // 2))
            sy.wait_ge(s2s2[0], 16 * 16)
            sy.wait_ge(s2s2[1], 16 * 16)


def _finish_dummy(nc, out_d, s_fin):
    with nc.Block("fin") as blk, ExitStack() as ps:
        z = ps.enter_context(nc.sbuf_tensor("zfin", [128, TOK], F32))

        sz = ps.enter_context(nc.semaphore("fin_sz"))

        @blk.vector
        def _(v):
            v.memset(z[:, :], 0.0).then_inc(sz, 1)

        @blk.sync
        def _(sy):
            sy.wait_ge(sz, 1)
            for k in range(KD):
                sy.dma_start(out=out_d[k * 128:(k + 1) * 128, :], in_=z[:, :]
                             ).then_inc(s_fin, 16)
            sy.wait_ge(s_fin, 16 * KD)


def make_consts():
    c = {}
    c["e64_i"] = (np.arange(DIN)[None, :] // PD == np.arange(128)[:, None]).astype(BF)
    c["idbf_i"] = np.eye(128, dtype=np.float32).astype(BF)
    jj, ii = np.meshgrid(np.arange(128), np.arange(128), indexing="ij")
    c["mask01_i"] = (jj <= ii).astype(BF)
    c["ones1_i"] = np.ones((1, 128), dtype=BF)
    c["onescol_i"] = np.ones((128, 1), dtype=BF)
    c["onessq_i"] = np.ones((128, 128), dtype=BF)
    c["iden8_i"] = np.eye(8, dtype=np.float32)
    oh = np.zeros((16, 16, 128), dtype=np.float32)
    for t in range(16):
        oh[t, t, :] = 1.0
    c["onehot16_i"] = oh.transpose(1, 0, 2).reshape(16, 2048).astype(BF)
    c["iden128_i"] = np.eye(128, dtype=np.float32)
    return c


def prep_inputs(inp):
    g = {}
    g["w_in_t"] = np.ascontiguousarray(inp["L0_attn_in_proj_w"].T)
    g["w_out_t"] = np.ascontiguousarray(inp["L0_attn_out_proj_w"].T)
    g["w_ffn_t"] = np.ascontiguousarray(inp["L0_ffn_weight"].T)
    g["gw_t"] = np.ascontiguousarray(inp["L0_ffn_gate_w"].T)
    g["ics_i"] = np.ascontiguousarray(inp["L0_ffn_in_channel_scale"])
    g["ocs_i"] = np.ascontiguousarray(inp["L0_ffn_out_channel_scale"])
    g["norm_w_i"] = np.ascontiguousarray(inp["L0_attn_norm_w"].reshape(KD, 128).T)
    g["conv_w_i"] = np.ascontiguousarray(
        inp["L0_attn_conv_w"].reshape(NHBC, 128, KCONV).transpose(1, 0, 2)
        .reshape(128, NHBC * KCONV))
    g["conv_b_i"] = np.ascontiguousarray(inp["L0_attn_conv_b"].reshape(NHBC, 128).T)
    g["dtb_i"] = np.ascontiguousarray(inp["L0_attn_dt_bias"].reshape(128, 1))
    g["alog_i"] = np.ascontiguousarray(inp["L0_attn_A_log"].reshape(128, 1))
    g["ffnb_i"] = np.ascontiguousarray(inp["L0_ffn_bias"].reshape(KD, 128).T)
    g["gnw_i"] = np.ascontiguousarray(inp["L0_attn_gnorm_w"].reshape(NPAIR, 128).T)
    dp = inp["L0_attn_Dp"]
    dpc = np.empty((128, NPAIR), dtype=np.float32)
    dpc[:64, :] = dp[0::2][None, :]
    dpc[64:, :] = dp[1::2][None, :]
    g["dp_i"] = np.ascontiguousarray(dpc)
    g.update(make_consts())
    return g


def core_inputs(inp, g, b, j):
    m = dict(g)
    x = inp["L0_attn_x"][b, j * TOK:(j + 1) * TOK, :]
    m["x_fm"] = np.ascontiguousarray(x.T)
    m["masks_i"] = np.full((128, 1), 1.0 if j == 0 else 0.0, dtype=np.float32)
    m["maskb_i"] = np.full((128, 1), 1.0 if j == 1 else 0.0, dtype=np.float32)
    return m


def run(inp, stage=99, debug=()):
    key = (stage, tuple(n for n, _, _ in debug))
    if key not in _CACHE:
        _CACHE[key] = build_program(stage=stage, debug=debug)
    nc = _CACHE[key]
    g = prep_inputs(inp)
    in_maps = [core_inputs(inp, g, c // 2, c % 2) for c in range(8)]
    return run_bass_kernel_spmd(nc, in_maps, core_ids=list(range(8)))


def kernel(**inputs):
    res = run(inputs, stage=99)
    out = np.empty((B, S, D), dtype=np.float32)
    for c in range(8):
        b, j = c // 2, c % 2
        out[b, j * TOK:(j + 1) * TOK, :] = res.results[c]["out_d"].T
    return out


# version 14 (final)
# speedup vs baseline: 1.1761x; 1.0013x over previous
import numpy as np
import ml_dtypes
from contextlib import ExitStack

import concourse.bass as bass
import concourse.mybir as mybir
from concourse.bass_utils import run_bass_kernel_spmd

F32 = mybir.dt.float32
BF16 = mybir.dt.bfloat16
AF = mybir.ActivationFunctionType
ALU = mybir.AluOpType
BF = ml_dtypes.bfloat16

B, S, D = 4, 2048, 4096
H, PD, G, ND = 128, 64, 8, 128
KCONV, CHUNK = 4, 128
DIN = H * PD
CONVD = DIN + 2 * G * ND
PROJ = DIN + CONVD + H
EPS = 1e-5
TOK = 1024
NCH = TOK // CHUNK
KD = D // 128
NPAIR = DIN // 128
NHBC = CONVD // 128
NGATE = DIN // 128
HBC0 = DIN
DT0 = DIN + CONVD
RG = [[0, 1], [2, 3], [4, 5], [6, 7]]

_CACHE = {}
CHAIN = "deps"


def _box(o):
    th = o.tensor
    name = th.name
    try:
        shp = list(th.shape)
        rowlen = 1
        for s_ in shp[1:]:
            rowlen *= int(s_)
        pat = [(int(p[0]), int(p[1])) for p in o.ap]
        if len(pat) < 2:
            raise ValueError
        off = int(o.offset)
        p0, f0 = divmod(off, rowlen)
        s0, n0 = pat[0]
        if n0 > 1 and s0 != 0:
            if s0 % rowlen:
                raise ValueError
            pext = 1 + (s0 // rowlen) * (n0 - 1)
        else:
            pext = 1
        fext = 1
        for s_, n_ in pat[1:]:
            if s_ < 0:
                raise ValueError
            fext += s_ * (n_ - 1)
        if f0 + fext > rowlen:
            raise ValueError
        return (name, p0, p0 + pext, f0, f0 + fext)
    except Exception:
        return (name, 0, 1 << 40, 0, 1 << 40)


def _op_boxes(args, kw):
    outs, ins = [], []
    first = True
    for o in args:
        if hasattr(o, "ap") and hasattr(o, "tensor"):
            (outs if first else ins).append(_box(o))
            first = False
    for knm, o in kw.items():
        if hasattr(o, "ap") and hasattr(o, "tensor"):
            b = _box(o)
            ins.append(b)
            if knm == "accum_out":
                outs.append(b)
    return outs, ins


def _inter(a, b):
    return (a[0] == b[0] and a[1] < b[2] and b[1] < a[2]
            and a[3] < b[4] and b[3] < a[4])


def _clash(bo, bi, ro, ri):
    for x in bo:
        for y in ro:
            if _inter(x, y):
                return True
        for y in ri:
            if _inter(x, y):
                return True
    for x in bi:
        for y in ro:
            if _inter(x, y):
                return True
    return False


class _IP:

    def __init__(self, trk):
        self._t = trk

    def then_inc(self, sem, k=1):
        t = self._t
        t._wait_floor(t._chain[1])
        return t._e.nop().then_inc(sem, k)


class Trk:
    _SKIP = {"wait_ge", "nop", "engine_nop", "then_inc", "dma_start",
             "collective_compute", "register", "register64", "reg_load",
             "reg_save", "reg_alu", "reg_add", "reg_mov", "partition_id",
             "tpb_base_ld"}

    def __init__(self, eng, chain=None):
        object.__setattr__(self, "_e", eng)
        object.__setattr__(self, "last", None)
        object.__setattr__(self, "_chain", chain)
        object.__setattr__(self, "_mode", CHAIN if chain is not None else None)
        object.__setattr__(self, "_floor", 0)
        object.__setattr__(self, "_recs", [])

    def _wait_floor(self, v):
        if v > self._floor:
            self._e.wait_ge(self._chain[0], v)
            object.__setattr__(self, "_floor", v)
            rc = self._recs
            if rc:
                self._recs[:] = [r for r in rc if r[0] > v]

    def __getattr__(self, name):
        a = getattr(self._e, name)
        if callable(a) and name not in Trk._SKIP:
            def w(*args, **kw):
                ch = self._chain
                if ch is None:
                    r = a(*args, **kw)
                    if hasattr(r, "then_inc"):
                        object.__setattr__(self, "last", r)
                    return r
                if self._mode == "deps":
                    bo, bi = _op_boxes(args, kw)
                    need = self._floor
                    for seq, ro, ri in self._recs:
                        if seq > need and _clash(bo, bi, ro, ri):
                            need = seq
                    self._wait_floor(need)
                else:
                    self._wait_floor(ch[1])
                r = a(*args, **kw)
                if not hasattr(r, "then_inc"):
                    return r
                object.__setattr__(self, "last", r)
                r.then_inc(ch[0], 1)
                ch[1] += 1
                if self._mode == "deps":
                    self._recs.append((ch[1], bo, bi))
                return _IP(self)
            return w
        return a

    def wait_ge(self, sem, val):
        return self._e.wait_ge(sem, val)

    def nop(self):
        return self._e.nop()

    def mi(self, sem, k=1):
        ch = self._chain
        if ch is not None:
            self._wait_floor(ch[1])
            self._e.nop().then_inc(sem, k)
        else:
            self.last.then_inc(sem, k)



def build_program(stage=99, debug=()):
    nc = bass.Bass("TRN2", target_bir_lowering=False)

    I = lambda n, s_, d_: nc.dram_tensor(n, s_, d_, kind="ExternalInput").ap()
    x_fm = I("x_fm", [D, TOK], F32)
    w_in_t = I("w_in_t", [D, PROJ], F32)
    w_out_t = I("w_out_t", [DIN, D], F32)
    w_ffn_t = I("w_ffn_t", [D, D], F32)
    gw_t = I("gw_t", [D, 8], F32)
    ics_i = I("ics_i", [8, D], F32)
    ocs_i = I("ocs_i", [8, D], F32)
    norm_w_i = I("norm_w_i", [128, KD], F32)
    conv_w_i = I("conv_w_i", [128, NHBC * KCONV], F32)
    conv_b_i = I("conv_b_i", [128, NHBC], F32)
    dtb_i = I("dtb_i", [128, 1], F32)
    alog_i = I("alog_i", [128, 1], F32)
    gnw_i = I("gnw_i", [128, NPAIR], F32)
    dp_i = I("dp_i", [128, NPAIR], F32)
    ffnb_i = I("ffnb_i", [128, KD], F32)
    e64_i = I("e64_i", [128, DIN], BF16)
    mask01_i = I("mask01_i", [128, 128], BF16)
    ones1_i = I("ones1_i", [1, 128], BF16)
    onescol_i = I("onescol_i", [128, 1], BF16)
    onessq_i = I("onessq_i", [128, 128], BF16)
    iden8_i = I("iden8_i", [8, 8], F32)
    iden128_i = I("iden128_i", [128, 128], F32)
    masks_i = I("masks_i", [128, 1], F32)
    onehot16_i = I("onehot16_i", [16, 2048], BF16)
    idbf_i = I("idbf_i", [128, 128], BF16)
    maskb_i = I("maskb_i", [128, 1], F32)

    out_d = nc.dram_tensor("out_d", [D, TOK], F32, kind="ExternalOutput").ap()
    DR = lambda n, s_, d_: nc.dram_tensor(n, s_, d_).ap()
    hid_d = DR("hid_d", [DIN, TOK], BF16)
    gate_d = DR("gate_d", [DIN, TOK], BF16)
    hd_d = DR("hd_d", [TOK, DIN], BF16)
    prev_d = DR("prev_d", [NCH * 128, DIN], BF16)
    ahi_d = DR("ahi_d", [128, TOK], BF16)
    alo_d = DR("alo_d", [128, TOK], BF16)
    ynorm_d = DR("ynorm_d", [DIN, TOK], BF16)
    out2_d = DR("out2_d", [D, TOK], BF16)
    cc1_in = DR("cc1_in", [128, NHBC * 3], F32)
    cc1_out = DR("cc1_out", [128, NHBC * 3], F32)
    cc2_in = DR("cc2_in", [128, DIN], F32)
    cc2_out = DR("cc2_out", [128, DIN], F32)

    dbg = {}
    for name, shape, dt_ in debug:
        dbg[name] = nc.dram_tensor(name, shape, dt_, kind="ExternalOutput").ap()
    dbgctr = [0]

    def dump(pairs, sdbg):
        todo = [(nm, src) for nm, src in pairs if nm in dbg]
        if not todo:
            return
        with nc.Block("dbg%d" % dbgctr[0]) as blk:
            @blk.sync
            def _(sy):
                for nm, src in todo:
                    sy.dma_start(out=dbg[nm], in_=src).then_inc(sdbg, 16)
                    dbgctr[0] += 16
                sy.wait_ge(sdbg, dbgctr[0])

    st = ExitStack()
    with st:
        SB = lambda n, s_, d_: st.enter_context(nc.sbuf_tensor(n, s_, d_))
        SEM = lambda n: st.enter_context(nc.semaphore(n))

        p_ones1 = SB("p_ones1", [1, 128], BF16)
        p_onescol = SB("p_onescol", [128, 1], BF16)
        p_onessq = SB("p_onessq", [128, 128], BF16)
        p_mask01 = SB("p_mask01", [128, 128], BF16)
        p_masks = SB("p_masks", [128, 1], F32)
        p_maskb = SB("p_maskb", [128, 1], F32)
        p_iden8 = SB("p_iden8", [8, 8], F32)
        p_iden128 = SB("p_iden128", [128, 128], F32)
        p_ffnb = SB("p_ffnb", [128, KD], F32)
        p_eps = SB("p_eps", [128, 1], F32)
        CHV = [SEM("chain_v"), 0]
        CHS = [SEM("chain_s"), 0]
        s_cc1 = SEM("s_cc1")
        s_cc2 = SEM("s_cc2")
        s_fin = SEM("s_fin")
        s_dbg = SEM("s_dbg")

        env = dict(locals())

        with ExitStack() as ms:
            MSB = lambda n, s_, d_: ms.enter_context(nc.sbuf_tensor(n, s_, d_))
            p_norm_w = MSB("p_norm_w", [128, KD], F32)
            p_conv_w = MSB("p_conv_w", [128, NHBC * KCONV], F32)
            p_conv_b = MSB("p_conv_b", [128, NHBC], F32)
            p_dtbias = MSB("p_dtbias", [128, 1], F32)
            p_alog = MSB("p_alog", [128, 1], F32)
            p_negea = MSB("p_negea", [128, 1], F32)
            p_gnw = MSB("p_gnw", [128, NPAIR], F32)
            p_dp = MSB("p_dp", [128, NPAIR], F32)
            p_e64 = MSB("p_e64", [128, DIN], BF16)
            p_dtbf = MSB("p_dtbf", [128, TOK], BF16)
            p_dwbf = MSB("p_dwbf", [128, TOK], BF16)
            p_ubf = MSB("p_ubf", [128, TOK], BF16)
            p_nahi = MSB("p_nahi", [128, TOK], BF16)
            p_nalo = MSB("p_nalo", [128, TOK], BF16)
            p_dcol = MSB("p_dcol", [128, NCH], F32)
            p_cdcol = MSB("p_cdcol", [128, NCH], F32)
            p_bc = MSB("p_bc", [128, 16 * TOK], BF16)
            p_halo = MSB("p_halo", [128, NHBC * 3], F32)
            p_pre3 = MSB("p_pre3", [128, NHBC * 3], F32)
            p_patch = MSB("p_patch", [128, 192], BF16)
            p_onehot = MSB("p_onehot", [16, 2048], BF16)
            p_idbf = MSB("p_idbf", [128, 128], BF16)
            env.update(locals())

            _p0_loads(nc, env)
            with ExitStack() as hs:
                p_h = hs.enter_context(nc.sbuf_tensor("p_h", [128, KD * TOK], BF16))
                env["p_h"] = p_h
                _p1_rms(nc, env)
                dump([("dbg_h", p_h[:, 0:4 * TOK])], s_dbg)
                if stage <= 1:
                    _finish_dummy(nc, out_d, s_fin)
                    return nc
                _p2_dt(nc, env)
                dump([("dbg_ubf", p_ubf[:, :]), ("dbg_dw", p_dwbf[:, :])], s_dbg)
                if stage <= 2:
                    _finish_dummy(nc, out_d, s_fin)
                    return nc
                _p3_inproj(nc, env, hbc=True)
                dump([("dbg_bc", p_bc[:, 0:2 * TOK]), ("dbg_halo", p_halo[:, :])],
                     s_dbg)
                if stage <= 3:
                    _finish_dummy(nc, out_d, s_fin)
                    return nc
                _p3_inproj(nc, env, hbc=False)
            env.pop("p_h")
            if stage == 35:
                _finish_dummy(nc, out_d, s_fin)
                return nc

            _p4_sweep1(nc, env)
            dump([("dbg_patch", p_patch[:, :]), ("dbg_cd", p_cdcol[:, :]),
                  ("dbg_prev7", prev_d[7 * 128:8 * 128, :]),
                  ("dbg_hd0", hd_d[0:128, :]),
                  ("dbg_ccst", cc2_in)], s_dbg)
            if stage <= 4:
                _finish_dummy(nc, out_d, s_fin)
                return nc

            _p6_sweep2(nc, env)
            dump([("dbg_yn", ynorm_d.rearrange("(c p) t -> p c t", p=128)
                   [:, :, 7 * 128:8 * 128])], s_dbg)
            if stage <= 6:
                _finish_dummy(nc, out_d, s_fin)
                return nc

        p_x1 = SB("p_x1", [128, KD * TOK], BF16)
        env["p_x1"] = p_x1
        _p7_outproj(nc, env)
        dump([("dbg_x1", p_x1[:, 0:4 * TOK])], s_dbg)
        if stage <= 7:
            _finish_dummy(nc, out_d, s_fin)
            return nc

        _p8_ffn(nc, env)
    return nc


def _p0_loads(nc, env):
    g_ = env
    with nc.Block("p0") as blk:
        s0 = g_["SEM"]("p0_s")
        @blk.sync
        def _(sy):
            loads = [
                (g_["p_norm_w"], g_["norm_w_i"]), (g_["p_conv_w"], g_["conv_w_i"]),
                (g_["p_conv_b"], g_["conv_b_i"]), (g_["p_dtbias"], g_["dtb_i"]),
                (g_["p_alog"], g_["alog_i"]), (g_["p_masks"], g_["masks_i"]),
                (g_["p_maskb"], g_["maskb_i"]), (g_["p_mask01"], g_["mask01_i"]),
                (g_["p_ones1"], g_["ones1_i"]), (g_["p_onescol"], g_["onescol_i"]),
                (g_["p_onessq"], g_["onessq_i"]), (g_["p_iden8"], g_["iden8_i"]),
                (g_["p_iden128"], g_["iden128_i"]), (g_["p_gnw"], g_["gnw_i"]),
                (g_["p_dp"], g_["dp_i"]), (g_["p_ffnb"], g_["ffnb_i"]),
                (g_["p_onehot"], g_["onehot16_i"]),
                (g_["p_idbf"], g_["idbf_i"]),
            ]
            for dst, src in loads:
                sy.dma_start(out=dst[:, :], in_=src).then_inc(s0, 16)
            sy.wait_ge(s0, 16 * len(loads))

        @blk.gpsimd
        def _(g):
            s0g = g_["SEM"]("p0_sg")
            g.dma_start(out=g_["p_e64"][:, :], in_=g_["e64_i"]).then_inc(s0g, 16)
            g.wait_ge(s0g, 16)
    with nc.Block("p0b") as blk:
        @blk.scalar
        def _(sc):
            sc = Trk(sc, g_["CHS"] if CHAIN else None)
            sc.activation(g_["p_negea"][:, :], g_["p_alog"][:, :], AF.Exp)
    with nc.Block("p0c") as blk:
        @blk.vector
        def _(v):
            v = Trk(v, g_["CHV"] if CHAIN else None)
            v.tensor_scalar(g_["p_negea"][:, :], g_["p_negea"][:, :], -1.0,
                            None, ALU.mult)
            v.memset(g_["p_eps"][:, :], EPS)


def _p1_rms(nc, env):
    g_ = env
    p_h, p_onescol, p_ones1, p_norm_w = g_["p_h"], g_["p_onescol"], g_["p_ones1"], g_["p_norm_w"]
    x_fm = g_["x_fm"]
    with nc.Block("p1") as blk, ExitStack() as ps:
        E = ps.enter_context
        xa = E(nc.sbuf_tensor("p1_xa", [128, TOK], F32))
        xb = E(nc.sbuf_tensor("p1_xb", [128, TOK], F32))
        x2 = E(nc.sbuf_tensor("p1_x2", [128, TOK], BF16))
        rrow = E(nc.sbuf_tensor("p1_rrow", [1, TOK], F32))
        rrbf = E(nc.sbuf_tensor("p1_rrbf", [1, TOK], BF16))
        ps_ss = E(nc.psum_tensor("p1_psss", [1, TOK], F32))
        ps_rb = E(nc.psum_tensor("p1_psrb", [128, TOK], F32))
        sx2 = [g_["SEM"]("p1_sx0"), g_["SEM"]("p1_sx1")]
        sq = g_["SEM"]("p1_sq")
        sm = g_["SEM"]("p1_sm")
        sv = g_["SEM"]("p1_sv")
        xt = [xa, xb]

        @blk.sync
        def _(sy):
            for k in range(KD):
                if k >= 2:
                    sy.wait_ge(sq, k - 1)
                sy.dma_start(out=xt[k % 2][:, :],
                             in_=x_fm[k * 128:(k + 1) * 128, :]).then_inc(sx2[k % 2], 16)
            for k in range(KD):
                if k < 2:
                    sy.wait_ge(sq, KD)
                else:
                    sy.wait_ge(sv, k - 1)
                sy.dma_start(out=xt[k % 2][:, :],
                             in_=x_fm[k * 128:(k + 1) * 128, :]).then_inc(sx2[k % 2], 16)

        @blk.scalar
        def _(sc):
            sc = Trk(sc, g_["CHS"] if CHAIN else None)
            for k in range(KD):
                sc.wait_ge(sx2[k % 2], 16 * (k // 2 + 1))
                if k > 0:
                    sc.wait_ge(sm, k)
                sc.activation(x2[:, :], xt[k % 2][:, :], AF.Square).then_inc(sq, 1)
            sc.wait_ge(sm, KD)
            sc.activation(rrow[:, :], ps_ss[:, :], AF.Ln, scale=1.0 / D,
                          bias=g_["p_eps"][0:1, :]).then_inc(sq, 1)
            sc.wait_ge(sq, KD + 1)
            sc.activation(rrow[:, :], rrow[:, :], AF.Exp, scale=-0.5).then_inc(sq, 1)

        @blk.tensor
        def _(t):
            t = Trk(t)
            for k in range(KD):
                t.wait_ge(sq, k + 1)
                t.matmul(ps_ss[0:1, 0:512], p_onescol[:, :], x2[:, 0:512],
                         start=(k == 0), stop=(k == KD - 1))
                t.matmul(ps_ss[0:1, 512:1024], p_onescol[:, :], x2[:, 512:1024],
                         start=(k == 0), stop=(k == KD - 1)).then_inc(sm, 1)
            t.wait_ge(sq, KD + 3)
            t.matmul(ps_rb[:, 0:512], p_ones1[:, :], rrbf[:, 0:512])
            t.matmul(ps_rb[:, 512:1024], p_ones1[:, :], rrbf[:, 512:1024]
                     ).then_inc(sm, 1)

        @blk.vector
        def _(v):
            v = Trk(v, g_["CHV"] if CHAIN else None)
            v.wait_ge(sq, KD + 2)
            v.tensor_copy(rrbf[:, :], rrow[:, :]).then_inc(sq, 1)
            v.wait_ge(sm, KD + 1)
            for k in range(KD):
                v.wait_ge(sx2[k % 2], 16 * (KD // 2 + k // 2 + 1))
                v.scalar_tensor_tensor(p_h[:, k * TOK:(k + 1) * TOK],
                                       xt[k % 2][:, :], p_norm_w[:, k:k + 1],
                                       ps_rb[:, :], ALU.mult, ALU.mult
                                       ).then_inc(sv, 1)


def _p2_dt(nc, env):
    g_ = env
    p_h = g_["p_h"]
    with nc.Block("p2") as blk, ExitStack() as ps:
        E = ps.enter_context
        wdt = E(nc.sbuf_tensor("p2_wdt", [128, KD * 128], BF16))
        ps_dt = E(nc.psum_tensor("p2_psdt", [128, TOK], F32))
        t_abs = E(nc.sbuf_tensor("p2_abs", [128, TOK], F32))
        t_e = E(nc.sbuf_tensor("p2_e", [128, TOK], F32))
        t_r = E(nc.sbuf_tensor("p2_r", [128, TOK], F32))
        t_a = E(nc.sbuf_tensor("p2_a", [128, TOK], F32))
        t_w = E(nc.sbuf_tensor("p2_w", [128, TOK], F32))
        t_hi = E(nc.sbuf_tensor("p2_hi", [128, TOK], BF16))
        p_dtv = E(nc.sbuf_tensor("p2_dtv", [128, TOK], F32))
        p_acs = E(nc.sbuf_tensor("p2_acs", [128, TOK], F32))
        t_lo = E(nc.sbuf_tensor("p2_lo", [128, TOK], BF16))
        sw = g_["SEM"]("p2_sw")
        sm = g_["SEM"]("p2_sm")
        sa = g_["SEM"]("p2_sa")
        sv = g_["SEM"]("p2_sv")
        ss = g_["SEM"]("p2_ss")

        @blk.gpsimd
        def _(g):
            g.dma_start(out=wdt.ap().rearrange("p (k c) -> p k c", k=KD),
                        in_=g_["w_in_t"][:, DT0:DT0 + 128].rearrange(
                            "(k p) c -> p k c", p=128)).then_inc(sw, 16)

        @blk.tensor
        def _(t):
            t = Trk(t)
            t.wait_ge(sw, 16)
            for k in range(KD):
                t.matmul(ps_dt[:, 0:512], wdt[:, k * 128:(k + 1) * 128],
                         p_h[:, k * TOK:k * TOK + 512],
                         start=(k == 0), stop=(k == KD - 1))
                t.matmul(ps_dt[:, 512:1024], wdt[:, k * 128:(k + 1) * 128],
                         p_h[:, k * TOK + 512:(k + 1) * TOK],
                         start=(k == 0), stop=(k == KD - 1))
            t.mi(sm, 1)

        @blk.scalar
        def _(sc):
            sc = Trk(sc, g_["CHS"] if CHAIN else None)
            sch = g_["SEM"]("p2_sch")
            sc.wait_ge(sm, 1)
            sc.activation(t_abs[:, :], ps_dt[:, :], AF.Abs,
                          bias=g_["p_dtbias"][:, :]).then_inc(sch, 1)
            sc.wait_ge(sch, 1)
            sc.activation(t_e[:, :], t_abs[:, :], AF.Exp, scale=-1.0).then_inc(sch, 1)
            sc.wait_ge(sch, 2)
            sc.activation(t_e[:, :], t_e[:, :], AF.Ln, bias=1.0)
            sc.activation(t_r[:, :], ps_dt[:, :], AF.Relu,
                          bias=g_["p_dtbias"][:, :]).then_inc(sa, 1)
            for k in range(NCH):
                sc.wait_ge(sv, 3 + k)
                ck = slice(k * 128, (k + 1) * 128)
                last = k * 128 + 127
                sc.activation(t_w[:, ck], p_acs[:, ck], AF.Exp, scale=-1.0,
                              bias=p_acs[:, last:last + 1])
                sc.activation(g_["p_ubf"][:, ck], p_acs[:, ck], AF.Exp)
                sc.activation(g_["p_dcol"][:, k:k + 1],
                              p_acs[:, last:last + 1], AF.Exp).then_inc(sa, 1)

        @blk.vector
        def _(v):
            v = Trk(v, g_["CHV"] if CHAIN else None)
            v.wait_ge(sa, 1)
            v.tensor_add(p_dtv[:, :], t_r[:, :], t_e[:, :]).then_inc(sv, 1)
            v.tensor_scalar(t_a[:, :], p_dtv[:, :], g_["p_negea"][:, :],
                            None, ALU.mult).then_inc(sv, 1)
            for k in range(NCH):
                ck = slice(k * 128, (k + 1) * 128)
                v.tensor_tensor_scan(p_acs[:, ck], t_a[:, ck], t_a[:, ck],
                                     0.0, ALU.add, ALU.bypass).then_inc(sv, 1)
            v.tensor_copy(g_["p_dtbf"][:, :], p_dtv[:, :])
            v.tensor_copy(t_hi[:, :], p_acs[:, :])
            v.tensor_tensor(t_lo[:, :], p_acs[:, :], t_hi[:, :], ALU.subtract)
            v.tensor_scalar(g_["p_nahi"][:, :], t_hi[:, :], -1.0, None, ALU.mult)
            v.tensor_scalar(g_["p_nalo"][:, :], t_lo[:, :], -1.0, None, ALU.mult)
            v.wait_ge(sa, 1 + NCH)
            v.tensor_tensor(g_["p_dwbf"][:, :], p_dtv[:, :], t_w[:, :],
                            ALU.mult)
            v.memset(g_["p_cdcol"][:, 0:1], 1.0)
            for k in range(1, NCH):
                v.tensor_tensor(g_["p_cdcol"][:, k:k + 1],
                                g_["p_cdcol"][:, k - 1:k],
                                g_["p_dcol"][:, k - 1:k], ALU.mult)
            v.mi(sv, 1)

        @blk.sync
        def _(sy):
            sy.wait_ge(sv, 3 + NCH)
            sy.dma_start(out=g_["ahi_d"], in_=t_hi[:, :]).then_inc(ss, 16)
            sy.dma_start(out=g_["alo_d"], in_=t_lo[:, :]).then_inc(ss, 16)
            sy.wait_ge(ss, 32)


def _p3_inproj(nc, env, hbc):
    g_ = env
    p_h = g_["p_h"]
    tag = "p3" if hbc else "p5"
    NB = NHBC if hbc else NGATE
    col0 = HBC0 if hbc else 0
    with nc.Block(tag) as blk, ExitStack() as ps:
        E = ps.enter_context
        wb = [E(nc.sbuf_tensor(f"{tag}_w{i}", [128, KD * 128], BF16)) for i in range(3)]
        pp2 = [E(nc.psum_tensor(f"{tag}_ps{i}", [128, TOK], F32)) for i in range(2)]
        stg = [E(nc.sbuf_tensor(f"{tag}_st{i}", [128, TOK], BF16)) for i in range(2)]
        hbc_f = E(nc.sbuf_tensor(f"{tag}_hf", [128, TOK], F32)) if hbc else None
        cbuf = E(nc.sbuf_tensor(f"{tag}_cb", [128, TOK], F32)) if hbc else None
        sw3 = [g_["SEM"](f"{tag}_sw{i}") for i in range(3)]
        smm = g_["SEM"](f"{tag}_smm")
        sev = g_["SEM"](f"{tag}_sev")
        scv = g_["SEM"](f"{tag}_scv")
        ssi = g_["SEM"](f"{tag}_ssi")
        sst2 = [g_["SEM"](f"{tag}_sst0"), g_["SEM"](f"{tag}_sst1")]
        shm = g_["SEM"](f"{tag}_shm")

        @blk.gpsimd
        def _(g):
            for cc in range(NB):
                if cc >= 3:
                    g.wait_ge(smm, cc - 2)
                c0 = col0 + cc * 128
                g.dma_start(out=wb[cc % 3].ap().rearrange("p (k c) -> p k c", k=KD),
                            in_=g_["w_in_t"][:, c0:c0 + 128].rearrange(
                                "(k p) c -> p k c", p=128)).then_inc(sw3[cc % 3], 16)
            if hbc:
                g.wait_ge(shm, 16)
                g.collective_compute("AllReduce", ALU.add, replica_groups=RG,
                                     ins=[g_["cc1_in"].opt()],
                                     outs=[g_["cc1_out"].opt()]
                                     ).then_inc(g_["s_cc1"], 1)

        @blk.tensor
        def _(t):
            t = Trk(t)
            for cc in range(NB):
                t.wait_ge(sw3[cc % 3], 16 * (cc // 3 + 1))
                if cc >= 2:
                    t.wait_ge(sev if hbc else ssi, cc - 1)
                pp = pp2[cc % 2]
                w = wb[cc % 3]
                for k in range(KD):
                    t.matmul(pp[:, 0:512], w[:, k * 128:(k + 1) * 128],
                             p_h[:, k * TOK:k * TOK + 512],
                             start=(k == 0), stop=(k == KD - 1))
                    t.matmul(pp[:, 512:1024], w[:, k * 128:(k + 1) * 128],
                             p_h[:, k * TOK + 512:(k + 1) * TOK],
                             start=(k == 0), stop=(k == KD - 1))
                t.mi(smm, 1)

        @blk.vector
        def _(v):
            v = Trk(v, g_["CHV"] if CHAIN else None)
            if hbc:
                v.memset(stg[0][:, 0:3], 0.0)
                v.memset(stg[1][:, 0:3], 0.0)
                v.memset(g_["p_bc"][:, :], 0.0)
            for cc in range(NB):
                v.wait_ge(smm, cc + 1)
                pp = pp2[cc % 2]
                if hbc:
                    v.tensor_copy(hbc_f[:, 0:512], pp[:, 0:512])
                    v.tensor_copy(hbc_f[:, 512:1024], pp[:, 512:1024]
                                  ).then_inc(sev, 1)
                    v.tensor_copy(g_["p_halo"][:, cc * 3:cc * 3 + 3],
                                  hbc_f[:, 1021:1024])
                    v.tensor_copy(g_["p_pre3"][:, cc * 3:cc * 3 + 3],
                                  hbc_f[:, 0:3])
                    if cc >= 1:
                        v.wait_ge(ssi, cc)
                    cw = g_["p_conv_w"]
                    v.tensor_scalar(cbuf[:, 3:1024], hbc_f[:, 0:1021],
                                    cw[:, cc * 4:cc * 4 + 1],
                                    g_["p_conv_b"][:, cc:cc + 1],
                                    ALU.mult, ALU.add)
                    for i in (1, 2, 3):
                        v.scalar_tensor_tensor(cbuf[:, 3:1024],
                                               hbc_f[:, i:i + 1021],
                                               cw[:, cc * 4 + i:cc * 4 + i + 1],
                                               cbuf[:, 3:1024], ALU.mult, ALU.add)
                    v.mi(scv, 1)
                else:
                    if cc >= 2:
                        v.wait_ge(sst2[cc % 2], 16 * (cc // 2))
                    v.tensor_copy(stg[cc % 2][:, 0:512], pp[:, 0:512])
                    v.tensor_copy(stg[cc % 2][:, 512:1024], pp[:, 512:1024]
                                  ).then_inc(ssi, 1)
            if hbc:
                v.wait_ge(scv, NB)
                v.tensor_scalar(g_["p_halo"][:, :], g_["p_halo"][:, :],
                                g_["p_masks"][:, :], None, ALU.mult
                                ).then_inc(scv, 1)

        @blk.scalar
        def _(sc):
            sc = Trk(sc, g_["CHS"] if CHAIN else None)
            if not hbc:
                return
            for cc in range(NB):
                sc.wait_ge(scv, cc + 1)
                if cc < 64:
                    if cc >= 2:
                        sc.wait_ge(sst2[cc % 2], 16 * (cc // 2))
                    sc.activation(stg[cc % 2][:, 3:1024], cbuf[:, 3:1024],
                                  AF.Silu).then_inc(ssi, 1)
                else:
                    b = cc - 64
                    sc.activation(g_["p_bc"][:, b * TOK + 3:(b + 1) * TOK],
                                  cbuf[:, 3:1024], AF.Silu).then_inc(ssi, 1)

        @blk.sync
        def _(sy):
            for cc in range(NB):
                sy.wait_ge(ssi, cc + 1)
                if hbc:
                    if cc < 64:
                        sy.dma_start(out=g_["hid_d"][cc * 128:(cc + 1) * 128, :],
                                     in_=stg[cc % 2][:, :]
                                     ).then_inc(sst2[cc % 2], 16)
                    else:
                        sy.nop().then_inc(sst2[cc % 2], 16)
                else:
                    sy.dma_start(out=g_["gate_d"][cc * 128:(cc + 1) * 128, :],
                                 in_=stg[cc % 2][:, :]).then_inc(sst2[cc % 2], 16)
            sy.wait_ge(sst2[0], 16 * ((NB + 1) // 2))
            sy.wait_ge(sst2[1], 16 * (NB // 2))
            if hbc:
                sy.wait_ge(scv, NB + 1)
                sy.dma_start(out=g_["cc1_in"], in_=g_["p_halo"][:, :]
                             ).then_inc(shm, 16)


def _p4_sweep1(nc, env):
    g_ = env

    def sev_e64d(k, gi):
        return 56 * k + 7 * gi + 1

    def sev_hd(k, gi, hf):
        return 56 * k + 7 * gi + 2 + 3 * hf

    def sev_hdw(k, gi, hf):
        return 56 * k + 7 * gi + 3 + 3 * hf

    def sev_rec(k, gi, hf):
        return 56 * k + 7 * gi + 4 + 3 * hf

    def mm_S(k, gi, hf):
        return 32 * k + 4 * gi + 2 * hf + 1

    def mm_D(k, gi, hf):
        return 32 * k + 4 * gi + 2 * hf + 2

    def unq2(q2):
        q, hf = q2 // 2, q2 % 2
        return q // 8, q % 8, hf

    with nc.Block("p4") as blk, ExitStack() as ps:
        E = ps.enter_context
        T0 = [E(nc.sbuf_tensor(f"s1_t0{i}", [128, DIN], BF16)) for i in range(2)]
        T1 = [E(nc.sbuf_tensor(f"s1_t1{i}", [128, DIN], BF16)) for i in range(2)]
        T2 = E(nc.sbuf_tensor("s1_t2", [128, DIN], BF16))
        T3 = [E(nc.sbuf_tensor(f"s1_t3{i}", [128, 512], BF16)) for i in range(2)]
        P = E(nc.sbuf_tensor("s1_P", [128, DIN], F32))
        btok = [E(nc.sbuf_tensor(f"s1_btok{i}", [128, G * 128], BF16)) for i in range(2)]
        e64d = [E(nc.sbuf_tensor(f"s1_e64d{i}", [128, 1024], BF16)) for i in range(2)]
        tsb = E(nc.sbuf_tensor("s1_tsb", [128, 512], F32))
        pfx = E(nc.sbuf_tensor("s1_pfx", [128, NHBC * 3], F32))
        cat6 = E(nc.sbuf_tensor("s1_cat6", [128, NHBC * 6], F32))
        tp6t = [E(nc.sbuf_tensor(f"s1_tp{i}", [128, NHBC * 3], F32))
                for i in range(2)]
        tp6 = [t_.ap().rearrange("p (c s) -> p c s", s=3) for t_ in tp6t]
        p3t = E(nc.sbuf_tensor("s1_p3t", [128, NHBC * 3], F32))
        dtT = [E(nc.sbuf_tensor(f"s1_dtT{i}", [128, 128], BF16)) for i in range(2)]
        wT = [E(nc.sbuf_tensor(f"s1_wT{i}", [128, 128], BF16)) for i in range(2)]
        psS = [E(nc.psum_tensor(f"s1_psS{i}", [128, 512], F32)) for i in range(2)]
        psD = [E(nc.psum_tensor(f"s1_psD{i}", [128, 512], F32)) for i in range(2)]
        sld2 = [g_["SEM"](f"s1_sld{i}") for i in range(2)]
        spfx = g_["SEM"]("s1_spfx")
        stp2 = [g_["SEM"](f"s1_stpy{i}") for i in range(2)]
        std2 = [g_["SEM"](f"s1_stpd{i}") for i in range(2)]
        stb2 = [g_["SEM"](f"s1_stpb{i}") for i in range(2)]
        smm = g_["SEM"]("s1_smm")
        sev = g_["SEM"]("s1_sev")
        spl = g_["SEM"]("s1_spl")
        shd = g_["SEM"]("s1_shd")
        spt = g_["SEM"]("s1_spt")
        spz = g_["SEM"]("s1_spz")
        sc2 = g_["SEM"]("s1_sc2")

        @blk.sync
        def _(sy):
            sy.wait_ge(g_["s_cc1"], 1)
            sy.dma_start(out=pfx[:, :], in_=g_["cc1_out"]).then_inc(spfx, 16)
            sy.wait_ge(spt, 3)
            sy.dma_start(
                out=g_["hid_d"].rearrange("(c p) t -> p c t", p=128)[:, :, 0:3],
                in_=g_["p_patch"].ap().rearrange("p (c l) -> p c l", c=64)
            ).then_inc(spt, 16)
            for k in range(NCH):
                if k == 0:
                    sy.wait_ge(spt, 19)
                if k >= 2:
                    sy.wait_ge(stp2[k % 2], 16 * (k // 2))
                sy.dma_start(
                    out=T0[k % 2].ap().rearrange("p (c l) -> p c l", c=64),
                    in_=g_["hid_d"].rearrange("(c p) t -> p c t", p=128)
                    [:, :, k * 128:(k + 1) * 128]).then_inc(sld2[k % 2], 16)
                if k >= 2:
                    sy.wait_ge(sev, 56 * (k - 1))
                sy.wait_ge(sld2[k % 2], 16 * (k // 2 + 1))
                sy.dma_start(
                    out=T1[k % 2].ap().rearrange("p (c w) -> p c w", w=128),
                    in_=T0[k % 2].ap().rearrange("p (c w) -> p c w", w=128),
                    transpose=True).then_inc(stp2[k % 2], 16)
                sy.dma_start(out=dtT[k % 2][:, :],
                             in_=g_["p_dtbf"][:, k * 128:(k + 1) * 128],
                             transpose=True).then_inc(std2[k % 2], 16)
                sy.dma_start(out=wT[k % 2][:, :],
                             in_=g_["p_dwbf"][:, k * 128:(k + 1) * 128],
                             transpose=True).then_inc(std2[k % 2], 16)
                if k >= 2:
                    sy.wait_ge(smm, 32 * (k - 1))
                for gi in range(G):
                    sy.dma_start(
                        out=btok[k % 2][:, gi * 128:(gi + 1) * 128],
                        in_=g_["p_bc"][:, gi * TOK + k * 128:gi * TOK + (k + 1) * 128],
                        transpose=True).then_inc(stb2[k % 2], 16)
            sy.wait_ge(shd, 16 * NCH)
            sy.wait_ge(sev, 56 * NCH + 1)
            sy.dma_start(out=g_["cc2_in"], in_=P[:, :]).then_inc(sc2, 16)

        @blk.scalar
        def _(sc):
            sc = Trk(sc, g_["CHS"] if CHAIN else None)
            sc.wait_ge(spt, 2)
            sc.activation(g_["p_patch"][:, :], p3t[:, 0:192], AF.Silu)
            for bcc in range(16):
                sc.activation(g_["p_bc"][:, bcc * TOK:bcc * TOK + 3],
                              p3t[:, (64 + bcc) * 3:(64 + bcc) * 3 + 3], AF.Silu)
            sc.mi(spt, 1)


        @blk.vector
        def _(v):
            v = Trk(v, g_["CHV"] if CHAIN else None)
            v.memset(P[:, :], 0.0).then_inc(spz, 1)
            v.wait_ge(spfx, 16)
            v.tensor_scalar(pfx[:, :], pfx[:, :], g_["p_maskb"][:, :], None,
                            ALU.mult).then_inc(spt, 1)
            c6v = cat6.ap().rearrange("p (c s) -> p c s", s=6)
            v.tensor_copy(c6v[:, :, 0:3],
                          pfx[:, :].rearrange("p (c s) -> p c s", s=3))
            v.tensor_copy(c6v[:, :, 3:6],
                          g_["p_pre3"][:, :].rearrange("p (c s) -> p c s", s=3))
            wv = g_["p_conv_w"][:, :].rearrange("p (c t) -> p c t", t=4)
            p3v = p3t[:, :].rearrange("p (c s) -> p c s", s=3)
            v.tensor_tensor(p3v, c6v[:, :, 0:3],
                            wv[:, :, 0:1].broadcast_to([128, NHBC, 3]),
                            ALU.mult)
            for i in (1, 2, 3):
                v.tensor_tensor(tp6[i % 2], c6v[:, :, i:i + 3],
                                wv[:, :, i:i + 1].broadcast_to([128, NHBC, 3]),
                                ALU.mult)
                v.tensor_tensor(p3v, p3v, tp6[i % 2], ALU.add)
            v.tensor_tensor(p3v, p3v,
                            g_["p_conv_b"][:, :, None].broadcast_to(
                                [128, NHBC, 3]), ALU.add)
            v.mi(spt, 1)
            T2h = T2.ap().rearrange("p (h w) -> p h w", w=64)
            for k in range(NCH):
                v.wait_ge(stp2[k % 2], 16 * (k // 2 + 1))
                v.wait_ge(std2[k % 2], 32 * (k // 2 + 1))
                v.wait_ge(shd, 16 * k)
                T1h = T1[k % 2].ap().rearrange("p (h w) -> p h w", w=64)
                for gi in range(G):
                    q = 8 * k + gi
                    if q >= 2:
                        pk, pgi = (q - 2) // 8, (q - 2) % 8
                        v.wait_ge(smm, mm_D(pk, pgi, 1))
                    v.tensor_scalar(e64d[q % 2][:, :],
                                    g_["p_e64"][:, gi * 1024:(gi + 1) * 1024],
                                    g_["p_dcol"][:, k:k + 1], None, ALU.mult
                                    ).then_inc(sev, 1)
                    for hf in range(2):
                        q2 = 2 * q + hf
                        h0 = gi * 16 + hf * 8
                        sl = slice(gi * 1024 + hf * 512,
                                   gi * 1024 + (hf + 1) * 512)
                        v.tensor_tensor(T2h[:, h0:h0 + 8, :],
                                        T1h[:, h0:h0 + 8, :],
                                        dtT[k % 2][:, h0:h0 + 8][:, :, None]
                                        .broadcast_to([128, 8, 64]), ALU.mult
                                        ).then_inc(sev, 1)
                        if q2 >= 2:
                            v.wait_ge(smm, mm_S(*unq2(q2 - 2)))
                        v.tensor_tensor(T3[q2 % 2].ap().rearrange(
                                            "p (h w) -> p h w", w=64),
                                        T1h[:, h0:h0 + 8, :],
                                        wT[k % 2][:, h0:h0 + 8][:, :, None]
                                        .broadcast_to([128, 8, 64]), ALU.mult
                                        ).then_inc(sev, 1)
                        v.wait_ge(smm, mm_D(k, gi, hf))
                        if gi == 0 and hf == 0:
                            v.wait_ge(spl, 16 * (k + 1))
                        v.tensor_tensor(tsb[:, :], psD[q2 % 2][:, :],
                                        P[:, sl], ALU.mult)
                        v.tensor_tensor(P[:, sl], tsb[:, :], psS[q2 % 2][:, :],
                                        ALU.add).then_inc(sev, 1)
            v.tensor_scalar(P[:, :], P[:, :], g_["p_masks"][:, :], None,
                            ALU.mult).then_inc(sev, 1)

        @blk.tensor
        def _(t):
            t = Trk(t)
            for k in range(NCH):
                for gi in range(G):
                    q = 8 * k + gi
                    if gi == 0:
                        t.wait_ge(stp2[k % 2], 16 * (k // 2 + 1))
                        t.wait_ge(stb2[k % 2], 16 * 8 * (k // 2 + 1))
                    t.wait_ge(sev, sev_e64d(k, gi))
                    for hf in range(2):
                        q2 = 2 * q + hf
                        if q2 >= 2:
                            t.wait_ge(sev, sev_rec(*unq2(q2 - 2)))
                        t.wait_ge(sev, sev_hdw(k, gi, hf))
                        bt = btok[k % 2][:, gi * 128:(gi + 1) * 128]
                        t.matmul(psS[q2 % 2][:, :], bt, T3[q2 % 2][:, :]
                                 ).then_inc(smm, 1)
                        t.matmul(psD[q2 % 2][:, :], g_["p_onessq"][:, :],
                                 e64d[q % 2][:, hf * 512:(hf + 1) * 512]
                                 ).then_inc(smm, 1)

        @blk.gpsimd
        def _(g):
            for k in range(NCH):
                if k == 0:
                    g.wait_ge(spz, 1)
                else:
                    g.wait_ge(sev, 56 * k)
                g.dma_start(out=g_["prev_d"][k * 128:(k + 1) * 128, :],
                            in_=P[:, :]).then_inc(spl, 16)
                g.wait_ge(sev, sev_hd(k, G - 1, 1))
                g.dma_start(out=g_["hd_d"][k * 128:(k + 1) * 128, :],
                            in_=T2[:, :]).then_inc(shd, 16)
            g.wait_ge(sc2, 16)
            g.collective_compute("AllReduce", ALU.add, replica_groups=RG,
                                 ins=[g_["cc2_in"].opt()],
                                 outs=[g_["cc2_out"].opt()]).then_inc(g_["s_cc2"], 1)


def _p6_sweep2(nc, env):
    g_ = env

    mmv = {}
    n = 0

    def ev(tag):
        nonlocal_ns[0] += 1
        mmv[tag] = nonlocal_ns[0]
    nonlocal_ns = [0]
    ev(("cdexp",))
    for k in range(NCH):
        for hg in range(16):
            gi, hh = hg // 2, hg % 2
            if hg == 0:
                ev(("cb", k, 0))
                ev(("st", k, 0))
            if hg < 15:
                if hh == 1:
                    ev(("cb", k, gi + 1))
                ev(("st", k, hg + 1))
            ev(("u", k, hg))
            ev(("y", k, hg))
        ev(("ep1", k))
        ev(("ep2", k))

    def ev_pt(k):
        return 44 * k + 2

    def ev_mtm(k, gi):
        return 44 * k + 3 + 5 * gi

    def ev_mtl(k, hg):
        return 44 * k + 4 + 5 * (hg // 2) + 2 * (hg % 2)

    def ev_ab(k, hg):
        return 44 * k + 5 + 5 * (hg // 2) + 2 * (hg % 2)

    def ev_dres(k):
        return 44 * k + 43

    def ev_y3(k):
        return 44 * k + 44

    def ev_yn(k):
        return 44 * k + 45

    def ac_le(k, hg):
        return 2 + 19 * k + hg

    def ac_silu(k):
        return 1 + 19 * k + 17

    def ac_ysq(k):
        return 1 + 19 * k + 18

    def ac_rows(k):
        return 1 + 19 * k + 19

    with nc.Block("p6") as blk, ExitStack() as ps:
        E = ps.enter_context
        T0 = E(nc.sbuf_tensor("s2_t0", [128, DIN], BF16))
        T1 = E(nc.sbuf_tensor("s2_t1", [128, DIN], BF16))
        Tg = E(nc.sbuf_tensor("s2_tg", [128, DIN], BF16))
        T5 = E(nc.sbuf_tensor("s2_t5", [128, DIN], BF16))
        YA = E(nc.sbuf_tensor("s2_ya", [128, DIN], F32))
        init = E(nc.sbuf_tensor("s2_init", [128, DIN], BF16))
        afhi = E(nc.sbuf_tensor("s2_afhi", [16, 1024], BF16))
        aflo = E(nc.sbuf_tensor("s2_aflo", [16, 1024], BF16))
        cdbf = E(nc.sbuf_tensor("s2_cdbf", [128, NCH], BF16))
        cdbf2 = E(nc.sbuf_tensor("s2_cdbf2", [128, 512], BF16))
        mtm = E(nc.sbuf_tensor("s2_mtm", [128, 128], BF16))
        rlf = E(nc.sbuf_tensor("s2_rlf", [128, 1024], F32))
        exps = [E(nc.sbuf_tensor(f"s2_exps{i}", [128, 1024], BF16)) for i in range(2)]
        mtl = [E(nc.sbuf_tensor(f"s2_mtl{i}", [128, 1024], BF16)) for i in range(2)]
        tpb = E(nc.sbuf_tensor("s2_tpb", [128, 1024], BF16))
        uexp = E(nc.sbuf_tensor("s2_uexp", [128, 512], BF16))
        tmp5 = E(nc.sbuf_tensor("s2_tmp5", [128, 512], BF16))
        rrow = E(nc.sbuf_tensor("s2_rrow", [1, 1024], F32))
        rrbf = E(nc.sbuf_tensor("s2_rrbf", [1, 1024], BF16))
        psS = [E(nc.psum_tensor(f"s2_pss{i}", [128, 1024], F32)) for i in range(2)]
        psMT = E(nc.psum_tensor("s2_psmt", [128, 128], F32))
        psU = E(nc.psum_tensor("s2_psu", [128, 512], F32))
        psYd = E(nc.psum_tensor("s2_psyd", [128, 512], F32))
        psYo = E(nc.psum_tensor("s2_psyo", [128, 512], F32))
        sld = g_["SEM"]("s2_sld")
        smm = g_["SEM"]("s2_smm")
        sev = g_["SEM"]("s2_sev")
        sac = g_["SEM"]("s2_sac")
        sst = g_["SEM"]("s2_sst")
        sini = g_["SEM"]("s2_sini")

        @blk.gpsimd
        def _(g):
            g.wait_ge(g_["s_cc2"], 1)
            g.dma_start(out=init[:, :], in_=g_["cc2_out"]).then_inc(sini, 16)

        @blk.sync
        def _(sy):
            for k in range(NCH):
                if k > 0:
                    sy.wait_ge(sev, ev_dres(k - 1))
                sy.dma_start(
                    out=T0.ap().rearrange("p (c l) -> p c l", c=64),
                    in_=g_["hid_d"].rearrange("(c p) t -> p c t", p=128)
                    [:, :, k * 128:(k + 1) * 128]).then_inc(sld, 16)
                if k > 0:
                    sy.wait_ge(smm, mmv[("y", k - 1, 15)])
                    sy.wait_ge(sst, 16 * k)
                sy.dma_start(out=T1[:, :], in_=g_["hd_d"][k * 128:(k + 1) * 128, :]
                             ).then_inc(sld, 16)
                sy.dma_start(out=T5[:, :], in_=g_["prev_d"][k * 128:(k + 1) * 128, :]
                             ).then_inc(sld, 16)
                if k > 0:
                    sy.wait_ge(smm, mmv[("st", k - 1, 15)])
                sy.dma_start(out=afhi[:, :],
                             in_=bass.AP(g_["ahi_d"].tensor, k * 128,
                                         [[8 * TOK, 16], [TOK, 8], [1, 128]])
                             ).then_inc(sld, 16)
                sy.dma_start(out=aflo[:, :],
                             in_=bass.AP(g_["alo_d"].tensor, k * 128,
                                         [[8 * TOK, 16], [TOK, 8], [1, 128]])
                             ).then_inc(sld, 16)
                if k > 0:
                    sy.wait_ge(smm, mmv[("ep1", k - 1)])
                sy.dma_start(
                    out=Tg.ap().rearrange("p (c l) -> p c l", c=64),
                    in_=g_["gate_d"].rearrange("(c p) t -> p c t", p=128)
                    [:, :, k * 128:(k + 1) * 128]).then_inc(sld, 16)
                sy.wait_ge(sev, ev_yn(k))
                sy.dma_start(
                    out=g_["ynorm_d"].rearrange("(c p) t -> p c t", p=128)
                    [:, :, k * 128:(k + 1) * 128],
                    in_=T1.ap().rearrange("p (c l) -> p c l", c=64)
                ).then_inc(sst, 16)
            sy.wait_ge(sst, 16 * NCH)

        @blk.scalar
        def _(sc):
            sc = Trk(sc, g_["CHS"] if CHAIN else None)
            sc.activation(cdbf[:, :], g_["p_cdcol"][:, :], AF.Copy).then_inc(sac, 1)
            for k in range(NCH):
                for hg in range(16):
                    sc.wait_ge(smm, mmv[("st", k, hg)])
                    if 16 * k + hg > 0:
                        phg = 16 * k + hg - 1
                        sc.wait_ge(sev, ev_mtl(phg // 16, phg % 16))
                    sc.activation(rlf[:, :], psS[hg % 2][:, :], AF.Relu,
                                  scale=-1.0)
                    sc.activation(exps[hg % 2][:, :], rlf[:, :], AF.Exp,
                                  scale=-1.0).then_inc(sac, 1)
                sc.wait_ge(sev, ev_dres(k))
                sc.activation(Tg[:, :], Tg[:, :], AF.Silu).then_inc(sac, 1)
                sc.wait_ge(sev, ev_y3(k))
                sc.activation(Tg[:, :], YA[:, :], AF.Square).then_inc(sac, 1)
                sc.wait_ge(smm, mmv[("ep1", k)])
                sc.activation(rrow[0:1, 0:512], psYd[0:1, 0:512], AF.Ln,
                              scale=1.0 / 1024, bias=g_["p_eps"][0:1, :])
                sc.activation(rrow[0:1, 512:1024], psYo[0:1, 0:512], AF.Ln,
                              scale=1.0 / 1024, bias=g_["p_eps"][0:1, :])
                sc.activation(rrow[:, :], rrow[:, :], AF.Exp, scale=-0.5)
                sc.activation(rrbf[:, :], rrow[:, :], AF.Copy).then_inc(sac, 1)

        @blk.vector
        def _(v):
            v = Trk(v, g_["CHV"] if CHAIN else None)
            v.wait_ge(smm, 1)
            v.tensor_copy(cdbf2[:, :], psS[0][:, 0:512]).then_inc(sev, 1)
            v.wait_ge(sini, 16)
            v.tensor_scalar(init[:, :], init[:, :], g_["p_maskb"][:, :], None,
                            ALU.mult)
            cdv = cdbf2.ap().rearrange("p (r s) -> p r s", s=NCH)
            tp3 = tpb.ap().rearrange("p (r w) -> p r w", w=128)
            in3 = init.ap().rearrange("p (r w) -> p r w", w=128)
            for k in range(NCH):
                v.wait_ge(sld, 16 * 6 * (k + 1))
                if k > 0:
                    v.wait_ge(smm, mmv[("y", k - 1, 15)])
                for qq in range(17):
                    if qq < 16:
                        v.tensor_tensor(
                            tp3[:, (qq % 2) * 4:(qq % 2) * 4 + 4, :],
                            in3[:, qq * 4:(qq + 1) * 4, :],
                            cdv[:, qq * 4:(qq + 1) * 4, k:k + 1]
                            .broadcast_to([128, 4, 128]), ALU.mult)
                    if qq > 0:
                        qs = slice((qq - 1) * 512, qq * 512)
                        v.tensor_tensor(T5[:, qs], T5[:, qs],
                                        tpb[:, ((qq - 1) % 2) * 512:
                                            ((qq - 1) % 2) * 512 + 512],
                                        ALU.add)
                v.mi(sev, 1)
                for gi in range(G):
                    v.wait_ge(smm, mmv[("cb", k, gi)])
                    v.tensor_tensor(mtm[:, :], psMT[:, :], g_["p_mask01"][:, :],
                                    ALU.mult).then_inc(sev, 1)
                    for hh in range(2):
                        hg = 2 * gi + hh
                        v.wait_ge(sac, ac_le(k, hg))
                        v.tensor_tensor(
                            mtl[hg % 2].ap().rearrange("p (a b) -> p a b", a=8),
                            exps[hg % 2].ap().rearrange("p (a b) -> p a b", a=8),
                            mtm[:, None, :].broadcast_to([128, 8, 128]),
                            ALU.mult).then_inc(sev, 1)
                        v.wait_ge(smm, mmv[("u", k, hg)])
                        v.tensor_copy(uexp[:, :], psU[:, :])
                        v.wait_ge(smm, mmv[("y", k, hg)])
                        v.tensor_tensor(tmp5[:, :], psYo[:, :], uexp[:, :],
                                        ALU.mult)
                        ysl = slice(hg * 512, (hg + 1) * 512)
                        v.tensor_tensor(YA[:, ysl], tmp5[:, :], psYd[:, :],
                                        ALU.add).then_inc(sev, 1)
                for pr in range(NPAIR):
                    sl = slice(pr * 128, (pr + 1) * 128)
                    v.scalar_tensor_tensor(YA[:, sl], T0[:, sl],
                                           g_["p_dp"][:, pr:pr + 1],
                                           YA[:, sl], ALU.mult, ALU.add)
                v.mi(sev, 1)
                v.wait_ge(sac, ac_silu(k))
                v.tensor_tensor(YA[:, :], YA[:, :], Tg[:, :], ALU.mult
                                ).then_inc(sev, 1)
                v.wait_ge(smm, mmv[("ep2", k)])
                for pr in range(NPAIR):
                    sl = slice(pr * 128, (pr + 1) * 128)
                    gidx = pr // 8
                    bsrc = psYd[:, (gidx % 4) * 128:(gidx % 4 + 1) * 128] \
                        if gidx < 4 else \
                        psYo[:, (gidx - 4) * 128:(gidx - 3) * 128]
                    v.scalar_tensor_tensor(T1[:, sl], YA[:, sl],
                                           g_["p_gnw"][:, pr:pr + 1],
                                           bsrc, ALU.mult, ALU.mult)
                v.mi(sev, 1)

        @blk.tensor
        def _(t):
            t = Trk(t)
            t.wait_ge(sac, 1)
            for pr in range(NPAIR):
                t.matmul(psS[0][:, pr * 8:(pr + 1) * 8],
                         g_["p_e64"][:, pr * 128:(pr + 1) * 128], cdbf[:, :])
            t.mi(smm, 1)

            def emit_cb(k, gi):
                if (k, gi) != (0, 0):
                    pk, pgi = (k, gi - 1) if gi > 0 else (k - 1, G - 1)
                    t.wait_ge(sev, ev_mtm(pk, pgi))
                bsl = slice(gi * TOK + k * 128, gi * TOK + (k + 1) * 128)
                csl = slice((8 + gi) * TOK + k * 128,
                            (8 + gi) * TOK + (k + 1) * 128)
                t.matmul(psMT[:, :], g_["p_bc"][:, bsl], g_["p_bc"][:, csl]
                         ).then_inc(smm, 1)

            def emit_st(k, hg):
                gi, hh = hg // 2, hg % 2
                khg = 16 * k + hg
                if khg >= 2:
                    phg = khg - 2
                    t.wait_ge(sac, ac_le(phg // 16, phg % 16))
                if hg == 0 and k == 0:
                    t.wait_ge(sev, 1)
                h0 = (gi * 16 + hh * 8) * 128
                nh = g_["p_nahi"][:, k * 128:(k + 1) * 128]
                nl = g_["p_nalo"][:, k * 128:(k + 1) * 128]
                hrow = 2 * gi + hh
                oh = g_["p_onehot"][:, hrow * 128:(hrow + 1) * 128]
                for reg in range(2):
                    rs = slice(reg * 512, (reg + 1) * 512)
                    habs = gi * 16 + hh * 8 + reg * 4
                    idb = g_["p_idbf"][:, habs:habs + 4][:, :, None] \
                        .broadcast_to([128, 4, 128])
                    out3 = psS[hg % 2].ap().rearrange(
                        "p (a b) -> p a b", a=2)[:, reg:reg + 1, :] \
                        .rearrange("p a (c d) -> p (a c) d", c=4)
                    t.matmul(psS[hg % 2][:, rs], oh, afhi[:, rs],
                             start=True, stop=False)
                    t.matmul(psS[hg % 2][:, rs], oh, aflo[:, rs],
                             start=False, stop=False)
                    t.matmul(out3, nh, idb, start=False, stop=False)
                    t.matmul(out3, nl, idb, start=False, stop=True)
                t.mi(smm, 1)

            def emit_u(k, hg):
                gi, hh = hg // 2, hg % 2
                khg = 16 * k + hg
                if khg >= 1:
                    phg = khg - 1
                    t.wait_ge(sev, ev_ab(phg // 16, phg % 16))
                for pp in range(4):
                    pr = gi * 8 + hh * 4 + pp
                    t.matmul(psU[:, pp * 128:(pp + 1) * 128],
                             g_["p_e64"][:, pr * 128:(pr + 1) * 128],
                             g_["p_ubf"][:, k * 128:(k + 1) * 128])
                t.mi(smm, 1)

            def emit_y(k, hg):
                gi, hh = hg // 2, hg % 2
                t.wait_ge(sev, ev_mtl(k, hg))
                t.wait_ge(sev, ev_pt(k))
                csl = slice((8 + gi) * TOK + k * 128,
                            (8 + gi) * TOK + (k + 1) * 128)
                for pp in range(4):
                    for par in range(2):
                        m = 2 * pp + par
                        hgl = gi * 16 + hh * 8 + m
                        t.matmul(psYd[par * 64:(par + 1) * 64,
                                      pp * 128:(pp + 1) * 128],
                                 T1[:, hgl * 64:(hgl + 1) * 64],
                                 mtl[hg % 2][:, m * 128:(m + 1) * 128],
                                 tile_position=(0, par * 64))
                for pp in range(4):
                    for par in range(2):
                        m = 2 * pp + par
                        hgl = gi * 16 + hh * 8 + m
                        t.matmul(psYo[par * 64:(par + 1) * 64,
                                      pp * 128:(pp + 1) * 128],
                                 T5[:, hgl * 64:(hgl + 1) * 64],
                                 g_["p_bc"][:, csl],
                                 tile_position=(0, par * 64))
                t.mi(smm, 1)

            for k in range(NCH):
                t.wait_ge(sld, 16 * (6 * k + 6))
                for hg in range(16):
                    if hg == 0:
                        emit_cb(k, 0)
                        emit_st(k, 0)
                    if hg < 15:
                        if hg % 2 == 1:
                            emit_cb(k, hg // 2 + 1)
                        emit_st(k, hg + 1)
                    emit_u(k, hg)
                    emit_y(k, hg)
                t.wait_ge(sac, ac_ysq(k))
                for pr in range(NPAIR):
                    gidx = pr // 8
                    dst = psYd[0:1, (gidx % 4) * 128:(gidx % 4) * 128 + 128] \
                        if gidx < 4 else \
                        psYo[0:1, (gidx - 4) * 128:(gidx - 4) * 128 + 128]
                    t.matmul(dst, g_["p_onescol"][:, :],
                             Tg[:, pr * 128:(pr + 1) * 128],
                             start=(pr % 8 == 0), stop=(pr % 8 == 7))
                t.mi(smm, 1)
                t.wait_ge(sac, ac_rows(k))
                for gidx in range(G):
                    dst = psYd[:, (gidx % 4) * 128:(gidx % 4) * 128 + 128] \
                        if gidx < 4 else \
                        psYo[:, (gidx - 4) * 128:(gidx - 4) * 128 + 128]
                    t.matmul(dst, g_["p_ones1"][:, :],
                             rrbf[:, gidx * 128:(gidx + 1) * 128])
                t.mi(smm, 1)


def _p7_outproj(nc, env):
    g_ = env
    p_x1 = g_["p_x1"]
    NOCB = 8

    def it_(tt, ocb):
        return tt * NOCB + ocb

    def mm_piece(n, piece):
        return 8 * n + piece + 1

    def mm_mm(tt, ocb):
        return 8 * it_(tt, ocb) + 4

    def mm_stats(tt, ocb, oc):
        return 8 * it_(tt, ocb) + 5 + oc

    MM_BCAST = 129

    def ev_evac(tt, ocb, oc):
        return 4 * it_(tt, ocb) + oc + 1

    EV_MV, EV_VAR, EV_B = 65, 66, 67

    def ac_sq(tt, ocb, oc):
        return 4 * it_(tt, ocb) + oc + 1

    AC_M2, AC_RSTD = 65, 66

    ps7 = ExitStack()
    E7 = ps7.enter_context
    mrow = E7(nc.sbuf_tensor("p7_mrow", [1, TOK], F32))
    vrow = E7(nc.sbuf_tensor("p7_vrow", [1, TOK], F32))
    m2r = E7(nc.sbuf_tensor("p7_m2r", [1, TOK], F32))
    rstdr = E7(nc.sbuf_tensor("p7_rstdr", [1, TOK], F32))
    arow = E7(nc.sbuf_tensor("p7_arow", [1, TOK], BF16))
    browf = E7(nc.sbuf_tensor("p7_browf", [1, TOK], F32))
    brow = E7(nc.sbuf_tensor("p7_brow", [1, TOK], BF16))
    smm = g_["SEM"]("p7_smm")
    sev = g_["SEM"]("p7_sev")
    sac = g_["SEM"]("p7_sac")
    with nc.Block("p7") as blk, ExitStack() as ps:
        E = ps.enter_context
        ynh = E(nc.sbuf_tensor("p7_ynh", [128, NPAIR * 512], BF16))
        wst = [E(nc.sbuf_tensor(f"p7_w{i}", [128, 16 * 512], BF16)) for i in range(2)]
        xbl = E(nc.sbuf_tensor("p7_xbl", [128, 4 * 512], F32))
        rtmp = E(nc.sbuf_tensor("p7_rtmp", [128, 512], F32))
        sqb = E(nc.sbuf_tensor("p7_sqb", [128, 512], BF16))
        ps4 = [E(nc.psum_tensor("p7_ps0", [128, 2048], F32))] * 2
        psM = E(nc.psum_tensor("p7_psM", [1, TOK], F32))
        psQ = E(nc.psum_tensor("p7_psQ", [1, TOK], F32))
        syl = g_["SEM"]("p7_syl")
        sx = g_["SEM"]("p7_sx")
        sw2 = [g_["SEM"]("p7_sw0"), g_["SEM"]("p7_sw1")]

        @blk.sync
        def _(sy):
            for tt in range(2):
                if tt > 0:
                    sy.wait_ge(smm, mm_mm(0, NOCB - 1))
                sy.dma_start(
                    out=ynh.ap().rearrange("p (c t) -> p c t", c=NPAIR),
                    in_=g_["ynorm_d"].rearrange("(c p) t -> p c t", p=128)
                    [:, :, tt * 512:(tt + 1) * 512]).then_inc(syl, 16)
                for ocb in range(NOCB):
                    n = it_(tt, ocb)
                    if n >= 1:
                        sy.wait_ge(sev, 4 * n)
                    sy.dma_start(
                        out=xbl.ap().rearrange("p (c t) -> p c t", c=4),
                        in_=g_["x_fm"].rearrange("(c p) t -> p c t", p=128)
                        [:, ocb * 4:(ocb + 1) * 4, tt * 512:(tt + 1) * 512]
                    ).then_inc(sx, 16)

        @blk.gpsimd
        def _(g):
            for tt in range(2):
                for ocb in range(NOCB):
                    for piece in range(4):
                        n = it_(tt, ocb) * 4 + piece
                        if n >= 2:
                            g.wait_ge(smm, mm_piece((n - 2) // 4, (n - 2) % 4))
                        r0 = piece * 16 * 128
                        g.dma_start(
                            out=wst[n % 2].ap().rearrange("p (k c) -> p k c", k=16),
                            in_=g_["w_out_t"][r0:r0 + 16 * 128,
                                              ocb * 512:(ocb + 1) * 512]
                            .rearrange("(k p) c -> p k c", p=128)
                        ).then_inc(sw2[n % 2], 16)

        @blk.tensor
        def _(t):
            t = Trk(t)
            for tt in range(2):
                for ocb in range(NOCB):
                    n = it_(tt, ocb)
                    t.wait_ge(syl, 16 * (tt + 1))
                    if n >= 1:
                        t.wait_ge(sev, 4 * n)
                    pp = ps4[n % 2]
                    for piece in range(4):
                        ns = 4 * n + piece
                        t.wait_ge(sw2[ns % 2], 16 * (ns // 2 + 1))
                        for kk in range(16):
                            kc = piece * 16 + kk
                            for oc in range(4):
                                t.matmul(pp[:, oc * 512:(oc + 1) * 512],
                                         wst[ns % 2]
                                         [:, kk * 512 + oc * 128:kk * 512 + (oc + 1) * 128],
                                         ynh[:, kc * 512:(kc + 1) * 512],
                                         start=(kc == 0), stop=(kc == 63))
                        t.mi(smm, 1)
                    for oc in range(4):
                        t.wait_ge(sev, ev_evac(tt, ocb, oc))
                        t.wait_ge(sac, ac_sq(tt, ocb, oc))
                        gcol = slice(tt * 512, (tt + 1) * 512)
                        first = (ocb == 0 and oc == 0)
                        last = (ocb == NOCB - 1 and oc == 3)
                        occ = ocb * 4 + oc
                        t.matmul(psM[0:1, gcol], g_["p_onescol"][:, :],
                                 p_x1[:, occ * TOK + tt * 512:occ * TOK + (tt + 1) * 512],
                                 start=first, stop=last)
                        t.matmul(psQ[0:1, gcol], g_["p_onescol"][:, :],
                                 sqb[:, :], start=first, stop=last).then_inc(smm, 1)

        @blk.vector
        def _(v):
            v = Trk(v, g_["CHV"] if CHAIN else None)
            for tt in range(2):
                for ocb in range(NOCB):
                    n = it_(tt, ocb)
                    v.wait_ge(smm, mm_mm(tt, ocb))
                    v.wait_ge(sx, 16 * (n + 1))
                    pp = ps4[n % 2]
                    for oc in range(4):
                        occ = ocb * 4 + oc
                        pn = 4 * n + oc
                        if pn > 0:
                            v.wait_ge(sac, pn)
                        v.scalar_tensor_tensor(rtmp[:, :],
                                               xbl[:, oc * 512:(oc + 1) * 512],
                                               2.0, pp[:, oc * 512:(oc + 1) * 512],
                                               ALU.mult, ALU.add)
                        v.tensor_copy(
                            p_x1[:, occ * TOK + tt * 512:occ * TOK + (tt + 1) * 512],
                            rtmp[:, :]).then_inc(sev, 1)
            v.wait_ge(smm, mm_stats(1, NOCB - 1, 3))
            v.tensor_scalar(mrow[:, :], psM[0:1, :], 1.0 / D, None, ALU.mult)
            v.tensor_scalar(vrow[:, :], psQ[0:1, :], 1.0 / D, None, ALU.mult
                            ).then_inc(sev, 1)

        @blk.scalar
        def _(sc):
            sc = Trk(sc, g_["CHS"] if CHAIN else None)
            for tt in range(2):
                for ocb in range(NOCB):
                    for oc in range(4):
                        sc.wait_ge(sev, ev_evac(tt, ocb, oc))
                        pn = 4 * it_(tt, ocb) + oc
                        if pn > 0:
                            ptt, pocb, poc = pn - 1, None, None
                            pi, poc = divmod(pn - 1, 4)
                            sc.wait_ge(smm, mm_stats(pi // NOCB, pi % NOCB, poc))
                        sc.activation(sqb[:, :], rtmp[:, :], AF.Square
                                      ).then_inc(sac, 1)
            sc.wait_ge(sev, EV_MV)
            sc.activation(m2r[:, :], mrow[:, :], AF.Square).then_inc(sac, 1)


    with nc.Block("p7b") as blk, ExitStack() as ps:
        E = ps.enter_context
        psAB = E(nc.psum_tensor("p7_psAB", [128, 2 * TOK], F32))
        ttmp = E(nc.sbuf_tensor("p7b_ttmp", [128, TOK], F32))

        @blk.vector
        def _(v):
            v = Trk(v, g_["CHV"] if CHAIN else None)
            v.wait_ge(sac, AC_M2)
            v.tensor_tensor(vrow[:, :], vrow[:, :], m2r[:, :], ALU.subtract
                            ).then_inc(sev, 1)
            v.wait_ge(sac, AC_RSTD)
            v.tensor_tensor(browf[:, :], mrow[:, :], rstdr[:, :], ALU.mult)
            v.tensor_scalar(brow[:, :], browf[:, :], -1.0, None, ALU.mult
                            ).then_inc(sev, 1)
            v.wait_ge(smm, MM_BCAST)
            for kc in range(KD):
                v.tensor_tensor(ttmp[:, :], p_x1[:, kc * TOK:(kc + 1) * TOK],
                                psAB[:, 0:TOK], ALU.mult)
                v.tensor_tensor(p_x1[:, kc * TOK:(kc + 1) * TOK], ttmp[:, :],
                                psAB[:, TOK:2 * TOK], ALU.add).then_inc(sev, 1)

        @blk.scalar
        def _(sc):
            sc = Trk(sc, g_["CHS"] if CHAIN else None)
            sc.wait_ge(sev, EV_VAR)
            sc.activation(rstdr[:, :], vrow[:, :], AF.Ln,
                          bias=g_["p_eps"][0:1, :])
            sc.activation(rstdr[:, :], rstdr[:, :], AF.Exp, scale=-0.5)
            sc.activation(arow[:, :], rstdr[:, :], AF.Copy).then_inc(sac, 1)

        @blk.tensor
        def _(t):
            t = Trk(t)
            t.wait_ge(sac, AC_RSTD)
            t.wait_ge(sev, EV_B)
            for r in range(2):
                t.matmul(psAB[:, r * 512:(r + 1) * 512], g_["p_ones1"][:, :],
                         arow[:, r * 512:(r + 1) * 512])
                t.matmul(psAB[:, TOK + r * 512:TOK + (r + 1) * 512],
                         g_["p_ones1"][:, :], brow[:, r * 512:(r + 1) * 512])
            t.mi(smm, 1)
    ps7.close()


def _p8_ffn(nc, env):
    g_ = env
    p_x1 = g_["p_x1"]

    def mm_is(kc):
        return 4 + kc

    def mm_fp(ocb, piece):
        return 36 + 6 * ocb + piece

    def mm_os(ocb):
        return 36 + 6 * ocb + 4

    def mm_stats(ocb):
        return 36 + 6 * ocb + 5

    MM_BCAST = 228

    def ev_xin(kc):
        return 6 + kc

    def ev_ose(ocb):
        return 38 + 2 * ocb

    def ev_o2(ocb):
        return 39 + 2 * ocb

    EV_MV, EV_VAR, EV_B = 102, 103, 104

    def ev_fin(kc):
        return 105 + kc

    def ac_sq(ocb):
        return 2 + ocb

    AC_M2, AC_RSTD = 34, 35

    with nc.Block("p8") as blk, ExitStack() as ps:
        E = ps.enter_context
        xin = E(nc.sbuf_tensor("p8_xin", [128, KD * TOK], BF16))
        gwbf = E(nc.sbuf_tensor("p8_gwbf", [128, KD * 8], BF16))
        icsbf = E(nc.sbuf_tensor("p8_icsbf", [8, D], BF16))
        ocsbf = E(nc.sbuf_tensor("p8_ocsbf", [8, D], BF16))
        lrow = E(nc.sbuf_tensor("p8_lrow", [8, TOK], F32))
        stok = E(nc.sbuf_tensor("p8_stok", [128, 64], F32))
        estok = E(nc.sbuf_tensor("p8_estok", [128, 64], F32))
        nmx = E(nc.sbuf_tensor("p8_nmx", [128, 8], F32))
        ssum = E(nc.sbuf_tensor("p8_ssum", [128, 8], F32))
        srec = E(nc.sbuf_tensor("p8_srec", [128, 8], F32))
        rtok = E(nc.sbuf_tensor("p8_rtok", [128, 64], F32))
        routg = E(nc.sbuf_tensor("p8_routg", [8, TOK], BF16))
        wstg = [E(nc.sbuf_tensor(f"p8_wf{i}", [128, 8 * 128], BF16)) for i in range(2)]
        wsgn = [E(nc.sbuf_tensor(f"p8_ws{i}", [128, 8 * 128], BF16)) for i in range(2)]
        osb = E(nc.sbuf_tensor("p8_osb", [128, TOK], F32))
        ftmp = E(nc.sbuf_tensor("p8_ftmp", [128, TOK], F32))
        o2b = [E(nc.sbuf_tensor(f"p8_o2{i}", [128, TOK], BF16)) for i in range(2)]
        sqf = E(nc.sbuf_tensor("p8_sqf", [128, TOK], BF16))
        fob = [E(nc.sbuf_tensor(f"p8_fo{i}", [128, TOK], BF16)) for i in range(2)]
        outf2 = E(nc.sbuf_tensor("p8_of1", [128, TOK], F32))
        outf = [osb, outf2]
        mrow = E(nc.sbuf_tensor("p8_mrow", [1, TOK], F32))
        vrow = E(nc.sbuf_tensor("p8_vrow", [1, TOK], F32))
        m2r = E(nc.sbuf_tensor("p8_m2r", [1, TOK], F32))
        rstdr = vrow
        arow = E(nc.sbuf_tensor("p8_arow", [1, TOK], BF16))
        browf = m2r
        brow = E(nc.sbuf_tensor("p8_brow", [1, TOK], BF16))
        PS1 = E(nc.psum_tensor("p8_PS1", [128, TOK], F32))
        PS2 = E(nc.psum_tensor("p8_PS2", [128, TOK], F32))
        PS3 = E(nc.psum_tensor("p8_PS3", [128, TOK], F32))
        PS4 = E(nc.psum_tensor("p8_PS4", [128, TOK], F32))
        sgld = g_["SEM"]("p8_sgld")
        ssw2 = [g_["SEM"]("p8_ssw0"), g_["SEM"]("p8_ssw1")]
        ssg = g_["SEM"]("p8_ssg")
        smm = g_["SEM"]("p8_smm")
        sev = g_["SEM"]("p8_sev")
        sac = g_["SEM"]("p8_sac")
        sld2 = [g_["SEM"]("p8_sld0"), g_["SEM"]("p8_sld1")]
        s2s2 = [g_["SEM"]("p8_s2sa"), g_["SEM"]("p8_s2sb")]
        sst2 = [g_["SEM"]("p8_ssta"), g_["SEM"]("p8_sstb")]

        @blk.gpsimd
        def _(g):
            g.dma_start(out=gwbf.ap().rearrange("p (k c) -> p k c", k=KD),
                        in_=g_["gw_t"].rearrange("(k p) c -> p k c", p=128)
                        ).then_inc(sgld, 16)
            g.dma_start(out=icsbf[:, :], in_=g_["ics_i"]).then_inc(sgld, 16)
            g.dma_start(out=ocsbf[:, :], in_=g_["ocs_i"]).then_inc(sgld, 16)
            for ocb in range(32):
                for piece in range(4):
                    n = 4 * ocb + piece
                    if n >= 2:
                        pocb, ppc = divmod(n - 2, 4)
                        g.wait_ge(smm, mm_fp(pocb, ppc))
                    r0 = piece * 8 * 128
                    g.dma_start(
                        out=wstg[n % 2].ap().rearrange("p (k c) -> p k c", k=8),
                        in_=g_["w_ffn_t"][r0:r0 + 8 * 128, ocb * 128:(ocb + 1) * 128]
                        .rearrange("(k p) c -> p k c", p=128)
                    ).then_inc(ssw2[n % 2], 16)

        @blk.scalar
        def _(sc):
            sc = Trk(sc, g_["CHS"] if CHAIN else None)
            sc.wait_ge(sev, 3)
            for tc in range(8):
                sc.activation(estok[:, tc * 8:(tc + 1) * 8],
                              stok[:, tc * 8:(tc + 1) * 8], AF.Exp,
                              bias=nmx[:, tc:tc + 1],
                              accum_out=ssum[:, tc:tc + 1])
            sc.mi(sac, 1)
            for ocb in range(32):
                for piece in range(4):
                    n = 4 * ocb + piece
                    sc.wait_ge(ssw2[n % 2], 16 * (n // 2 + 1))
                    if n >= 2:
                        pocb, ppc = divmod(n - 2, 4)
                        sc.wait_ge(smm, mm_fp(pocb, ppc))
                    sc.activation(wsgn[n % 2][:, :], wstg[n % 2][:, :],
                                  AF.Sign).then_inc(ssg, 1)
                sc.wait_ge(sev, ev_o2(ocb))
                if ocb > 0:
                    sc.wait_ge(smm, mm_stats(ocb - 1))
                sc.activation(sqf[:, :], o2b[ocb % 2][:, :], AF.Square
                              ).then_inc(sac, 1)
            sc.wait_ge(sev, EV_MV)
            sc.activation(m2r[:, :], mrow[:, :], AF.Square).then_inc(sac, 1)
            sc.wait_ge(sev, EV_VAR)
            sc.activation(rstdr[:, :], vrow[:, :], AF.Ln,
                          bias=g_["p_eps"][0:1, :])
            sc.activation(rstdr[:, :], rstdr[:, :], AF.Exp, scale=-0.5)
            sc.activation(arow[:, :], rstdr[:, :], AF.Copy).then_inc(sac, 1)

        @blk.tensor
        def _(t):
            t = Trk(t)
            t.wait_ge(sgld, 48)
            for kc in range(KD):
                t.matmul(PS1[0:8, 0:512], gwbf[:, kc * 8:(kc + 1) * 8],
                         p_x1[:, kc * TOK:kc * TOK + 512],
                         start=(kc == 0), stop=(kc == KD - 1))
                t.matmul(PS1[0:8, 512:1024], gwbf[:, kc * 8:(kc + 1) * 8],
                         p_x1[:, kc * TOK + 512:(kc + 1) * TOK],
                         start=(kc == 0), stop=(kc == KD - 1))
            t.mi(smm, 1)
            t.wait_ge(sev, 1)
            for tc in range(8):
                t.transpose(PS2[:, tc * 8:(tc + 1) * 8],
                            lrow[:, tc * 128:(tc + 1) * 128], g_["p_iden8"][:, :])
            t.mi(smm, 1)
            t.wait_ge(sev, 4)
            for tc in range(8):
                t.transpose(PS3[0:8, tc * 128:(tc + 1) * 128],
                            rtok[:, tc * 8:(tc + 1) * 8], g_["p_iden128"][:, :])
            t.mi(smm, 1)
            t.wait_ge(sev, 5)
            for kc in range(KD):
                if kc >= 1:
                    t.wait_ge(sev, ev_xin(kc - 1))
                t.matmul(PS4[:, 0:512], icsbf[:, kc * 128:(kc + 1) * 128],
                         routg[:, 0:512])
                t.matmul(PS4[:, 512:1024], icsbf[:, kc * 128:(kc + 1) * 128],
                         routg[:, 512:1024]).then_inc(smm, 1)
            for ocb in range(32):
                if ocb == 0:
                    t.wait_ge(sev, ev_xin(KD - 1))
                else:
                    t.wait_ge(sev, ev_o2(ocb - 1))
                for piece in range(4):
                    n = 4 * ocb + piece
                    t.wait_ge(ssg, n + 1)
                    for kk in range(8):
                        kc = piece * 8 + kk
                        for half in range(2):
                            t.matmul(
                                PS1[:, half * 512:(half + 1) * 512],
                                wsgn[n % 2][:, kk * 128:(kk + 1) * 128],
                                xin[:, kc * TOK + half * 512:kc * TOK + (half + 1) * 512],
                                start=(kc == 0), stop=(kc == KD - 1))
                    t.mi(smm, 1)
                occ = ocb
                for half in range(2):
                    t.matmul(PS2[:, half * 512:(half + 1) * 512],
                             ocsbf[:, occ * 128:(occ + 1) * 128],
                             routg[:, half * 512:(half + 1) * 512])
                t.mi(smm, 1)
                t.wait_ge(sev, ev_o2(ocb))
                t.wait_ge(sac, ac_sq(ocb))
                first, last = (ocb == 0), (ocb == 31)
                for half in range(2):
                    hc = slice(half * 512, (half + 1) * 512)
                    t.matmul(PS3[0:1, hc], g_["p_onescol"][:, :],
                             o2b[ocb % 2][:, hc], start=first, stop=last)
                    t.matmul(PS4[0:1, hc], g_["p_onescol"][:, :],
                             sqf[:, hc], start=first, stop=last)
                t.mi(smm, 1)
            t.wait_ge(sac, AC_RSTD)
            t.wait_ge(sev, EV_B)
            for r in range(2):
                t.matmul(PS3[:, r * 512:(r + 1) * 512], g_["p_ones1"][:, :],
                         arow[:, r * 512:(r + 1) * 512])
                t.matmul(PS4[:, r * 512:(r + 1) * 512], g_["p_ones1"][:, :],
                         brow[:, r * 512:(r + 1) * 512])
            t.mi(smm, 1)

        @blk.vector
        def _(v):
            v = Trk(v, g_["CHV"] if CHAIN else None)
            v.wait_ge(smm, 1)
            v.tensor_copy(lrow[:, :], PS1[0:8, :]).then_inc(sev, 1)
            v.wait_ge(smm, 2)
            v.tensor_copy(stok[:, :], PS2[:, 0:64]).then_inc(sev, 1)
            v.tensor_reduce(nmx.ap().rearrange("p (a b) -> p a b", b=1),
                            stok.ap().rearrange("p (a b) -> p a b", a=8),
                            mybir.AxisListType.X, ALU.max, negate=True
                            ).then_inc(sev, 1)
            v.wait_ge(sac, 1)
            v.reciprocal(srec[:, :], ssum[:, :])
            v.tensor_tensor(rtok.ap().rearrange("p (a b) -> p a b", a=8),
                            estok.ap().rearrange("p (a b) -> p a b", a=8),
                            srec[:, :, None].broadcast_to([128, 8, 8]),
                            ALU.mult).then_inc(sev, 1)
            v.wait_ge(smm, 3)
            v.tensor_copy(routg[:, :], PS3[0:8, :]).then_inc(sev, 1)
            for kc in range(KD):
                v.wait_ge(smm, mm_is(kc))
                v.tensor_tensor(xin[:, kc * TOK:(kc + 1) * TOK],
                                p_x1[:, kc * TOK:(kc + 1) * TOK],
                                PS4[:, :], ALU.mult).then_inc(sev, 1)
            for ocb in range(32):
                v.wait_ge(smm, mm_os(ocb))
                v.tensor_copy(osb[:, :], PS2[:, :]).then_inc(sev, 1)
                occ = ocb
                v.tensor_tensor(ftmp[:, :], PS1[:, :], osb[:, :], ALU.mult)
                if occ >= 2:
                    v.wait_ge(s2s2[occ % 2], 16 * (occ // 2))
                    v.wait_ge(smm, mm_stats(occ - 2))
                v.scalar_tensor_tensor(o2b[ocb % 2][:, :], ftmp[:, :],
                                       g_["p_ffnb"][:, occ:occ + 1],
                                       p_x1[:, occ * TOK:(occ + 1) * TOK],
                                       ALU.add, ALU.add).then_inc(sev, 1)
            v.wait_ge(smm, mm_stats(31))
            v.tensor_scalar(mrow[:, :], PS3[0:1, :], 1.0 / D, None, ALU.mult)
            v.tensor_scalar(vrow[:, :], PS4[0:1, :], 1.0 / D, None, ALU.mult
                            ).then_inc(sev, 1)
            v.wait_ge(sac, AC_M2)
            v.tensor_tensor(vrow[:, :], vrow[:, :], m2r[:, :], ALU.subtract
                            ).then_inc(sev, 1)
            v.wait_ge(sac, AC_RSTD)
            v.tensor_tensor(browf[:, :], mrow[:, :], rstdr[:, :], ALU.mult)
            v.tensor_scalar(brow[:, :], browf[:, :], -1.0, None, ALU.mult
                            ).then_inc(sev, 1)
            v.wait_ge(smm, MM_BCAST)
            for kc in range(KD):
                v.wait_ge(sld2[kc % 2], 16 * (kc // 2 + 1))
                if kc >= 2:
                    v.wait_ge(sst2[kc % 2], 16 * (kc // 2))
                v.tensor_tensor(ftmp[:, :], fob[kc % 2][:, :], PS3[:, :],
                                ALU.mult)
                v.tensor_tensor(outf[kc % 2][:, :], ftmp[:, :], PS4[:, :],
                                ALU.add).then_inc(sev, 1)

        @blk.sync
        def _(sy):
            for ocb in range(32):
                sy.wait_ge(sev, ev_o2(ocb))
                sy.dma_start(out=g_["out2_d"][ocb * 128:(ocb + 1) * 128, :],
                             in_=o2b[ocb % 2][:, :]).then_inc(s2s2[ocb % 2], 16)
            for kc in range(KD):
                sy.wait_ge(s2s2[kc % 2], 16 * (kc // 2 + 1))
                if kc >= 2:
                    sy.wait_ge(sev, ev_fin(kc - 2))
                sy.dma_start(out=fob[kc % 2][:, :],
                             in_=g_["out2_d"][kc * 128:(kc + 1) * 128, :]
                             ).then_inc(sld2[kc % 2], 16)
                if kc >= 1:
                    sy.wait_ge(sev, ev_fin(kc - 1))
                    sy.dma_start(out=g_["out_d"][(kc - 1) * 128:kc * 128, :],
                                 in_=outf[(kc - 1) % 2][:, :]
                                 ).then_inc(sst2[(kc - 1) % 2], 16)
            sy.wait_ge(sev, ev_fin(KD - 1))
            sy.dma_start(out=g_["out_d"][(KD - 1) * 128:KD * 128, :],
                         in_=outf[(KD - 1) % 2][:, :]
                         ).then_inc(sst2[(KD - 1) % 2], 16)
            sy.wait_ge(sst2[0], 16 * (KD // 2))
            sy.wait_ge(sst2[1], 16 * (KD // 2))
            sy.wait_ge(s2s2[0], 16 * 16)
            sy.wait_ge(s2s2[1], 16 * 16)


def _finish_dummy(nc, out_d, s_fin):
    with nc.Block("fin") as blk, ExitStack() as ps:
        z = ps.enter_context(nc.sbuf_tensor("zfin", [128, TOK], F32))

        sz = ps.enter_context(nc.semaphore("fin_sz"))

        @blk.vector
        def _(v):
            v.memset(z[:, :], 0.0).then_inc(sz, 1)

        @blk.sync
        def _(sy):
            sy.wait_ge(sz, 1)
            for k in range(KD):
                sy.dma_start(out=out_d[k * 128:(k + 1) * 128, :], in_=z[:, :]
                             ).then_inc(s_fin, 16)
            sy.wait_ge(s_fin, 16 * KD)


def make_consts():
    c = {}
    c["e64_i"] = (np.arange(DIN)[None, :] // PD == np.arange(128)[:, None]).astype(BF)
    c["idbf_i"] = np.eye(128, dtype=np.float32).astype(BF)
    jj, ii = np.meshgrid(np.arange(128), np.arange(128), indexing="ij")
    c["mask01_i"] = (jj <= ii).astype(BF)
    c["ones1_i"] = np.ones((1, 128), dtype=BF)
    c["onescol_i"] = np.ones((128, 1), dtype=BF)
    c["onessq_i"] = np.ones((128, 128), dtype=BF)
    c["iden8_i"] = np.eye(8, dtype=np.float32)
    oh = np.zeros((16, 16, 128), dtype=np.float32)
    for t in range(16):
        oh[t, t, :] = 1.0
    c["onehot16_i"] = oh.transpose(1, 0, 2).reshape(16, 2048).astype(BF)
    c["iden128_i"] = np.eye(128, dtype=np.float32)
    return c


def prep_inputs(inp):
    g = {}
    g["w_in_t"] = np.ascontiguousarray(inp["L0_attn_in_proj_w"].T)
    g["w_out_t"] = np.ascontiguousarray(inp["L0_attn_out_proj_w"].T)
    g["w_ffn_t"] = np.ascontiguousarray(inp["L0_ffn_weight"].T)
    g["gw_t"] = np.ascontiguousarray(inp["L0_ffn_gate_w"].T)
    g["ics_i"] = np.ascontiguousarray(inp["L0_ffn_in_channel_scale"])
    g["ocs_i"] = np.ascontiguousarray(inp["L0_ffn_out_channel_scale"])
    g["norm_w_i"] = np.ascontiguousarray(inp["L0_attn_norm_w"].reshape(KD, 128).T)
    g["conv_w_i"] = np.ascontiguousarray(
        inp["L0_attn_conv_w"].reshape(NHBC, 128, KCONV).transpose(1, 0, 2)
        .reshape(128, NHBC * KCONV))
    g["conv_b_i"] = np.ascontiguousarray(inp["L0_attn_conv_b"].reshape(NHBC, 128).T)
    g["dtb_i"] = np.ascontiguousarray(inp["L0_attn_dt_bias"].reshape(128, 1))
    g["alog_i"] = np.ascontiguousarray(inp["L0_attn_A_log"].reshape(128, 1))
    g["ffnb_i"] = np.ascontiguousarray(inp["L0_ffn_bias"].reshape(KD, 128).T)
    g["gnw_i"] = np.ascontiguousarray(inp["L0_attn_gnorm_w"].reshape(NPAIR, 128).T)
    dp = inp["L0_attn_Dp"]
    dpc = np.empty((128, NPAIR), dtype=np.float32)
    dpc[:64, :] = dp[0::2][None, :]
    dpc[64:, :] = dp[1::2][None, :]
    g["dp_i"] = np.ascontiguousarray(dpc)
    g.update(make_consts())
    return g


def core_inputs(inp, g, b, j):
    m = dict(g)
    x = inp["L0_attn_x"][b, j * TOK:(j + 1) * TOK, :]
    m["x_fm"] = np.ascontiguousarray(x.T)
    m["masks_i"] = np.full((128, 1), 1.0 if j == 0 else 0.0, dtype=np.float32)
    m["maskb_i"] = np.full((128, 1), 1.0 if j == 1 else 0.0, dtype=np.float32)
    return m


def run(inp, stage=99, debug=()):
    key = (stage, tuple(n for n, _, _ in debug))
    if key not in _CACHE:
        _CACHE[key] = build_program(stage=stage, debug=debug)
    nc = _CACHE[key]
    g = prep_inputs(inp)
    in_maps = [core_inputs(inp, g, c // 2, c % 2) for c in range(8)]
    return run_bass_kernel_spmd(nc, in_maps, core_ids=list(range(8)))


def kernel(**inputs):
    res = run(inputs, stage=99)
    out = np.empty((B, S, D), dtype=np.float32)
    for c in range(8):
        b, j = c // 2, c % 2
        out[b, j * TOK:(j + 1) * TOK, :] = res.results[c]["out_d"].T
    return out
